# version 06
# baseline (speedup 1.0000x reference)
import contextlib

import numpy as np
import ml_dtypes
import concourse.bass as bass
import concourse.mybir as mybir
from concourse.bass_utils import run_bass_kernel_spmd

F32 = mybir.dt.float32
BF16 = mybir.dt.bfloat16
AF = mybir.ActivationFunctionType
ALU = mybir.AluOpType
BF = ml_dtypes.bfloat16

D = 4096
SEQ = 8192
CH = 512
NDT = 32
NET = 32
NH = 32
NCORES = 8
SCALE = np.float32(1.0 / np.sqrt(128.0))
EXPB = np.float32(-20.0)
EPS = np.float32(1e-5)
ZERO = np.float32(0.0)
ONE = np.float32(1.0)
HALF = np.float32(0.5)
NEGHALF = np.float32(-0.5)
INV_D = np.float32(1.0 / D)

ENGS = ("sync", "gpsimd", "tensor", "vector", "scalar")


class Prog:

    def __init__(self, nc, es):
        self.nc = nc
        self.es = es
        self.q = {e: [] for e in ENGS}
        self._sems = {}

    def sem(self, name):
        if name not in self._sems:
            h = self.es.enter_context(self.nc.semaphore(name))
            self._sems[name] = [h, 0]
        return self._sems[name]

    def wait(self, eng, tok):
        if tok is None:
            return
        rec, v = tok
        self.q[eng].append(lambda E, h=rec[0], vv=v: E.wait_ge(h, vv))

    def op(self, eng, meth, *args, waits=(), inc=None, dma=False, **kw):
        for t in waits:
            self.wait(eng, t)
        if inc is not None:
            rec = self.sem(inc)
            amt = 16 if dma else 1
            rec[1] += amt
            tok = (rec, rec[1])
            self.q[eng].append(
                lambda E, m=meth, a=args, k=kw, h=rec[0], am=amt:
                    getattr(E, m)(*a, **k).then_inc(h, am))
            return tok
        self.q[eng].append(lambda E, m=meth, a=args, k=kw: getattr(E, m)(*a, **k))
        return None

    def cur(self, name):
        rec = self.sem(name)
        return (rec, rec[1])

    def snapshot(self):
        return ({k: v[1] for k, v in self._sems.items()},
                {e: len(q) for e, q in self.q.items()})

    def restore_counts(self, snap):
        for k, v in snap[0].items():
            self._sems[k][1] = v

    def branch_begin(self, engines, regname, val):
        for e in engines:
            self.q[e].append(lambda E, r=regname, v=val:
                             E.__dict__.setdefault("_ifg", []).append(
                                 _enter_if(E, E.__dict__[r], v)))

    def branch_end(self, engines):
        for e in engines:
            self.q[e].append(lambda E: E.__dict__["_ifg"].pop().__exit__(None, None, None))

    def sem_pad(self, eng, name, amount):
        if amount > 0:
            rec = self.sem(name)
            self.q[eng].append(lambda E, h=rec[0], a=amount: E.sem_inc(h, a))

    def load_reg(self, eng, regname, src_ap):
        def f(E, r=regname, ap=src_ap):
            reg = E.alloc_register(r)
            E.__dict__[r] = reg
            E.reg_load(reg, ap)
        self.q[eng].append(f)


    def run(self):
        with self.nc.Block() as blk:
            for eng in ENGS:
                ops = self.q[eng]
                if ops:
                    getattr(blk, eng)(lambda E, ops=ops: [f(E) for f in ops])


def _enter_if(E, reg, val):
    g = E.If_eq(reg, val)
    g.__enter__()
    return g



def alloc_common(nc, es, which):
    sb = {}

    def t(name, shape, dt):
        sb[name] = es.enter_context(nc.sbuf_tensor("sb_" + name, shape, dt))
        return sb[name]

    t("xs", [128, NDT, CH], F32)
    t("lnxt", [128, NDT, CH], BF16)
    t("slab", [128, 2, NDT, 128], BF16)
    t("big", [128, 16512], BF16)
    t("A_combT", [128, 2, NDT * 64], BF16)
    t("B_combT", [128, D], BF16)
    t("sq2", [128, 2, CH], BF16)
    t("tmp", [128, CH], F32)
    t("rowbf", [1, 2, CH], BF16)
    t("sel", [128, 96], BF16)
    t("ident4", [128, 32], BF16)
    t("rstdB", [128, CH], BF16)
    t("mrB", [128, CH], BF16)
    t("axs", [128, CH], BF16)
    t("identbf", [128, 128], BF16)
    t("onescol", [128, 1], BF16)
    t("gate2f", [1, 32], F32)
    t("gate2b", [1, 32], BF16)
    t("growb", [1, 128], BF16)
    t("gbc", [128, 32], F32)
    t("gcol", [128, 8], F32)
    t("btf", [128, NET], F32)
    t("btfh", [128, NET], F32)
    if which == "A":
        t("kbc", [128, 1], F32)
        t("vbB", [128, 128], BF16)
        t("vbr", [1, 128], F32)
        t("ksh", [128, 2, CH], BF16)
        t("vsh", [128, 2, 4, 128], BF16)
    else:
        t("QT", [128, NH, CH], BF16)
        t("expS", [128, 2, 2 * CH], BF16)
        t("tri", [128, 128], BF16)
        t("ttok", [128, 2, 128], BF16)
        t("rec", [128, 1], F32)
        t("btq", [128, NET], F32)
        t("bto", [128, NET], F32)

    sb["psA"] = es.enter_context(nc.psum_tensor("psA", [128, 3584], F32))
    sb["psT"] = es.enter_context(nc.psum_tensor("psT", [128, 1024], BF16))
    sb["ps_ax"] = sb["psA"][0:64, 2048:2560]
    sb["ps_misc"] = sb["psA"][:, 2048:2560]
    sb["ps_sum"] = sb["psA"][0:1, 2560:3072]
    sb["ps_sq"] = sb["psA"][0:1, 3072:3584]
    sb["lsa"] = sb["big"][:, 0:8192].rearrange("p (b n x) -> p b n x", b=2, n=4)
    sb["lsb"] = sb["big"][:, 8192:12288].rearrange("p (b x) -> p b x", b=2)
    sb["accA"] = sb["big"][:, 12288:16384].bitcast(F32).rearrange("p (g x) -> p g x", g=2)
    sb["accB"] = sb["big"][:, 12288:16384].bitcast(F32)
    return sb



def _slab6(w, net):
    w = np.asarray(w, dtype=np.float32)
    return np.ascontiguousarray(w.reshape(NDT, 128, net, 128).transpose(2, 1, 0, 3))


def _bias_tile(b):
    return np.ascontiguousarray(np.asarray(b, dtype=np.float32).reshape(32, 128).T)



def emit_consts(p, sb, which):
    gp = "gpsimd"
    p.op(gp, "memset", sb["identbf"][:], 0.0)
    p.op(gp, "affine_select", out=sb["identbf"][:], in_=sb["identbf"][:],
         compare_op=ALU.not_equal, fill=1.0, base=0, pattern=[[-1, 128]],
         channel_multiplier=1)
    if which == "B":
        p.op(gp, "memset", sb["tri"][:], 1.0)
        p.op(gp, "affine_select", out=sb["tri"][:], in_=sb["tri"][:],
             compare_op=ALU.is_ge, fill=0.0, base=0, pattern=[[1, 128]],
             channel_multiplier=-1)
    return p.op(gp, "memset", sb["onescol"][:], 1.0, inc="gp")


def emit_lora_build(p, sb, dr, t_gate, t_const, t_miscbank, t_ident4):
    lsa, lsb = sb["lsa"], sb["lsb"]
    accA = sb["accA"]
    load_gate = {}
    t_g2b = p.op("vector", "tensor_copy", sb["gate2b"][:], sb["gate2f"][:],
                 waits=[t_gate], inc="ve")
    mm = p.op("tensor", "matmul", sb["ps_misc"][:, 0:32],
              sb["onescol"][0:1, 0:1].broadcast_to([1, 128]),
              sb["gate2b"][0:1, :], start=True, stop=True,
              waits=[t_g2b, t_const, t_miscbank], inc="pe")
    t_gbc = p.op("vector", "tensor_copy", sb["gbc"][:], sb["ps_misc"][:, 0:32],
                 waits=[mm], inc="ve")

    fma_done = {}
    load_toks = {}
    for j in range(4):
        w = []
        if j >= 2:
            w.append(fma_done[j - 2])
        load_toks[j] = p.op("gpsimd", "dma_start", out=lsa[:, j % 2],
                            in_=dr["loraA7"][4 * j:4 * j + 4].rearrange("n p x -> p n x"),
                            waits=w, inc=f"dla{j % 2}", dma=True)
        t_last = None
        for g in range(2):
            for a in range(4):
                n = 4 * j + a
                w = [load_toks[j], t_gbc] if (g == 0 and a == 0) else []
                src = lsa[:, j % 2, a]
                gc = sb["gbc"][:, g * 16 + n:g * 16 + n + 1]
                if j == 0 and a == 0:
                    t_last = p.op("vector", "tensor_scalar", accA[:, g], src, gc,
                                  None, op0=ALU.mult, waits=w, inc="ve")
                else:
                    t_last = p.op("vector", "scalar_tensor_tensor", accA[:, g], src,
                                  gc, accA[:, g], op0=ALU.mult, op1=ALU.add,
                                  waits=w, inc="ve")
        fma_done[j] = t_last
    t_own = None
    for g in range(2):
        t_own = p.op("vector", "tensor_copy",
                     sb["A_combT"][:, g].rearrange("p (dt r) -> p dt r", r=64)[:, :, 0:32],
                     accA[:, g].rearrange("p (dt r) -> p dt r", r=32), inc="ve")
    tsh = p.op("gpsimd", "dma_start", out=lsa[:, 0, 0],
               in_=dr["loraAs7"], waits=[t_own], inc="dla0", dma=True)
    tA = None
    for g in range(2):
        tA = p.op("vector", "tensor_copy",
                  sb["A_combT"][:, g].rearrange("p (dt r) -> p dt r", r=64)[:, :, 32:64],
                  lsa[:, 0, 0].rearrange("p (dt r) -> p dt r", r=32),
                  waits=[tsh] if g == 0 else [], inc="ve")

    t_z = p.op("vector", "memset", sb["sel"][:, 32:64], 0.0)
    t_i4 = t_ident4[0]
    tB = None
    prev_col = t_gbc
    li = 0
    load_tok = {}
    sel_used = None
    for g in range(2):
        for eh in range(2):
            for j in range(4):
                w = []
                if li >= 1:
                    w.append(load_gate[li - 1])
                load_tok[li] = p.op("gpsimd", "dma_start", out=lsb[:, li % 2],
                            in_=dr["loraB7"][4 * j:4 * j + 4, :, 2048 * eh:2048 * eh + 2048]
                            .rearrange("n r e -> (n r) e"),
                            waits=w, inc=f"dlb{li % 2}", dma=True)
                if eh == 0:
                    ci = 4 * g + j
                    t_row = p.op("vector", "tensor_copy",
                                 sb["growb"][:].rearrange("o (a r) -> o a r", a=4),
                                 sb["gate2b"][0:1, 16 * g + 4 * j:16 * g + 4 * j + 4]
                                 .unsqueeze(2).broadcast_to([1, 4, 32]),
                                 waits=[prev_col], inc="ve")
                    mmp = p.op("tensor", "matmul", sb["ps_misc"][:, 40 + ci:41 + ci],
                               sb["growb"][:], sb["onescol"][0:1, 0:1],
                               start=True, stop=True,
                               waits=[t_row], inc="pe")
                    prev_col = p.op("vector", "tensor_copy",
                                    sb["gcol"][:, ci:ci + 1],
                                    sb["ps_misc"][:, 40 + ci:41 + ci],
                                    waits=[mmp], inc="ve")
                wsel = [prev_col, t_i4]
                if g == 1 and eh == 0 and j == 0:
                    t_zz = p.op("vector", "memset", sb["sel"][:, 0:32], 0.0,
                                waits=[sel_used])
                if sel_used is not None:
                    wsel.append(sel_used)
                selv = sb["sel"][:, 64 * g:64 * g + 32]
                t_sel = p.op("vector", "tensor_scalar", selv, sb["ident4"][:],
                             sb["gcol"][:, 4 * g + j:4 * g + j + 1], None,
                             op0=ALU.mult, waits=wsel, inc="ve")
                lhs = sb["sel"][:, 0:32] if g == 0 else sb["sel"][:, 0:96]
                t_mmf = None
                for ec in range(4):
                    w2 = [load_tok[li], t_sel] if ec == 0 else []
                    if j == 0 and ec == 0 and tB is not None:
                        w2.append(tB)
                    t_mmf = p.op("tensor", "matmul",
                                 sb["psA"][0:96 if g else 32, 512 * ec:512 * ec + 512],
                                 lhs, lsb[:, li % 2, 512 * ec:512 * ec + 512],
                                 start=(j == 0), stop=(j == 3),
                                 waits=w2,
                                 inc="pe")
                sel_used = t_mmf
                load_gate[li] = t_mmf
                li += 1
            for ec in range(4):
                pb = 64 * g
                tB = p.op("scalar", "activation",
                          out=sb["B_combT"][pb:pb + 32, 2048 * eh + 512 * ec:2048 * eh + 512 * ec + 512],
                          in_=sb["psA"][pb:pb + 32, 512 * ec:512 * ec + 512],
                          func=AF.Identity, bias=0.0, scale=ONE,
                          waits=[load_gate[li - 1]], inc="sc")
    tBs0 = p.op("gpsimd", "dma_start", out=sb["B_combT"][32:64, :], in_=dr["loraBs"],
                inc="dlb0", dma=True)
    tBs1 = p.op("gpsimd", "dma_start", out=sb["B_combT"][96:128, :], in_=dr["loraBs"],
                inc="dlb1", dma=True)
    return [tA, tB, tBs0, tBs1]


def emit_ln(p, sb, xs_ch, lnxt, first_waits):
    t0 = p.op("vector", "tensor_copy", lnxt[:], xs_ch, waits=first_waits, inc="ve")
    sq = sb["sq2"]
    for dt in range(NDT):
        p.op("tensor", "matmul", sb["ps_sum"], sb["onescol"][:, 0:1], lnxt[:, dt],
             start=(dt == 0), stop=(dt == NDT - 1), waits=[t0] if dt == 0 else [])
    t_pe = [None] * NDT
    for dt in range(NDT):
        w = [t0] if dt == 0 else []
        if dt >= 2:
            assert t_pe[dt - 2] is not None
            w.append(t_pe[dt - 2])
        t_sq = p.op("vector", "tensor_mul", sq[:, dt % 2], lnxt[:, dt], lnxt[:, dt],
                    waits=w, inc="ve")
        t_pe[dt] = p.op("tensor", "matmul", sb["ps_sq"], sb["onescol"][:, 0:1],
                        sq[:, dt % 2], start=(dt == 0), stop=(dt == NDT - 1),
                        waits=[t_sq], inc="pe")
    t_mm = t_pe[NDT - 1]
    mean = sb["psT"][0:1, :].bitcast(F32)
    rv = sb["ps_sq"]
    p.op("vector", "tensor_scalar", mean, sb["ps_sum"], float(INV_D), None,
         op0=ALU.mult, waits=[t_mm])
    t_ex2 = p.op("vector", "tensor_scalar", rv, sb["ps_sq"], float(INV_D), None,
                 op0=ALU.mult, inc="ve")
    t_msq = p.op("scalar", "activation", out=sb["rowbf"][:, 0], in_=mean,
                 func=AF.Square, bias=ZERO, scale=ONE, waits=[t_ex2], inc="sc")
    t_var = p.op("vector", "tensor_sub", rv, rv, sb["rowbf"][:, 0],
                 waits=[t_msq], inc="ve")
    p.op("scalar", "activation", out=rv, in_=rv, func=AF.Ln, bias=EPS, scale=ONE,
         waits=[t_var])
    t_rstd = p.op("scalar", "activation", out=rv, in_=rv, func=AF.Exp,
                  bias=ZERO, scale=NEGHALF, inc="sc")
    p.op("vector", "tensor_copy", sb["rowbf"][:, 0], rv, waits=[t_rstd])
    t_rows = p.op("vector", "tensor_mul", sb["rowbf"][:, 1], mean, sb["rowbf"][:, 0],
                  inc="ve")
    m1 = p.op("tensor", "matmul", sb["ps_misc"][:, 0:CH],
              sb["onescol"][0:1, 0:1].broadcast_to([1, 128]), sb["rowbf"][0:1, 0],
              start=True, stop=True, waits=[t_rows], inc="pe")
    t_r = p.op("vector", "tensor_copy", sb["rstdB"][:], sb["ps_misc"][:, 0:CH],
               waits=[m1], inc="ve")
    m2 = p.op("tensor", "matmul", sb["ps_misc"][:, 0:CH],
              sb["onescol"][0:1, 0:1].broadcast_to([1, 128]), sb["rowbf"][0:1, 1],
              start=True, stop=True, waits=[t_r], inc="pe")
    t_m = p.op("vector", "tensor_copy", sb["mrB"][:], sb["ps_misc"][:, 0:CH],
               waits=[m2], inc="ve")
    t_out = None
    for dt in range(NDT):
        p.op("vector", "tensor_mul", lnxt[:, dt], lnxt[:, dt], sb["rstdB"][:],
             waits=[t_r] if dt == 0 else [])
        t_out = p.op("vector", "tensor_sub", lnxt[:, dt], lnxt[:, dt], sb["mrB"][:],
                     waits=[t_m] if dt == 0 else [], inc="ve")
    return t_out


class SlabStream:
    def __init__(self, p, sb):
        self.p, self.slab = p, sb["slab"]
        self.i = 0
        self.done = []

    def load(self, dram_et_view, extra_waits=()):
        buf = self.i % 2
        w = list(extra_waits)
        if self.i >= 2:
            w.append(self.done[self.i - 2])
        tok = self.p.op("gpsimd", "dma_start",
                        out=self.slab[:, buf].rearrange("p dt c -> p (dt c)"),
                        in_=dram_et_view.rearrange("p dt c -> p (dt c)"),
                        waits=w, inc=f"dk{buf}", dma=True)
        self.i += 1
        return buf, tok

    def note_done(self, tok):
        self.done.append(tok)

    def gate_next(self, tok, n=2):
        for k in range(1, n + 1):
            if len(self.done) >= k:
                self.done[-k] = tok


def emit_proj(p, sb, stream, wdram, rhs_src, out_writer, lora=None, first_waits=()):
    bank_free = [None] * 4
    t_ax = None
    if lora is not None:
        A_view, B_rows, g = lora
        pb = 64 * g
        ps_ax = sb["psA"][pb:pb + 64, 2048:2560]
        axs = sb["axs"][pb:pb + 64, :]
        mm = None
        for dt in range(NDT):
            w = list(first_waits) if dt == 0 else []
            mm = p.op("tensor", "matmul", ps_ax,
                      A_view[:, 64 * dt:64 * dt + 64], rhs_src[:, dt],
                      start=(dt == 0), stop=(dt == NDT - 1), waits=w,
                      inc="pe" if dt == NDT - 1 else None)
        t_ax = p.op("scalar", "activation", out=axs, in_=ps_ax,
                    func=AF.Identity, bias=0.0, scale=ONE, waits=[mm], inc="sc")
    for et in range(NET):
        buf, ltok = stream.load(wdram[et])
        bank = et % 4
        ps = sb["psA"][:, 512 * bank:512 * bank + 512]
        w = [ltok]
        if bank_free[bank] is not None:
            w.append(bank_free[bank])
        if et == 0 and lora is None:
            w += list(first_waits)
        tok = None
        for dt in range(NDT):
            stop = (dt == NDT - 1 and lora is None)
            tok = p.op("tensor", "matmul", ps, stream.slab[:, buf, dt], rhs_src[:, dt],
                       start=(dt == 0), stop=stop, waits=w if dt == 0 else [],
                       inc="pe" if stop else None)
        if lora is not None:
            tok = p.op("tensor", "matmul", ps, B_rows[:, 128 * et:128 * et + 128],
                       axs, start=False, stop=True,
                       waits=[t_ax] if et == 0 else [], inc="pe")
        stream.note_done(tok)
        bank_free[bank] = out_writer(et, ps, tok)
    if lora is not None and sb.get("_dbg_ax_out") is not None:
        p.op("sync", "dma_start", out=sb["_dbg_ax_out"], in_=sb["axs"][:],
             waits=[t_ax], inc="dst", dma=True)
        sb["_dbg_ax_out"] = None
    return bank_free


def make_residual_writer(p, sb, xs_t, btile_scaled, scale):
    state = {"prev": None}

    def writer(et, ps, tok_pe):
        w = [tok_pe]
        if state["prev"] is not None:
            w.append(state["prev"])
        t_sc = p.op("scalar", "activation", out=sb["tmp"][:], in_=ps,
                    func=AF.Identity, bias=btile_scaled[:, et:et + 1], scale=scale,
                    waits=w, inc="sc")
        state["prev"] = p.op("vector", "tensor_add", xs_t[:, et], xs_t[:, et],
                             sb["tmp"][:], waits=[t_sc], inc="ve")
        return t_sc
    return writer


def make_bf16_writer(p, sb, out_et, btile):
    def writer(et, ps, tok_pe):
        return p.op("scalar", "activation", out=out_et(et), in_=ps,
                    func=AF.Identity, bias=btile[:, et:et + 1], scale=ONE,
                    waits=[tok_pe], inc="sc")
    return writer


def emit_kv(p, sb, lnxt, ch, first_waits):
    ps_k = sb["psA"][:, 0:512]
    tok = None
    for dt in range(NDT):
        tok = p.op("tensor", "matmul", ps_k, sb["slab"][:, 0, dt], lnxt[:, dt],
                   start=(dt == 0), stop=(dt == NDT - 1),
                   waits=first_waits if dt == 0 else [],
                   inc="pe" if dt == NDT - 1 else None)
    t_k = p.op("scalar", "activation", out=sb["ksh"][:, ch], in_=ps_k,
               func=AF.Identity, bias=sb["kbc"][:, 0:1], scale=ONE,
               waits=[tok], inc="sc")
    t_v = []
    for qt in range(4):
        ps_v = sb["psA"][:, 512 + 512 * (qt % 2):512 + 512 * (qt % 2) + 128]
        w = []
        if qt >= 2:
            w.append(t_v[qt - 2])
        tok = None
        for dt in range(NDT):
            tok = p.op("tensor", "matmul", ps_v, lnxt[:, dt, 128 * qt:128 * qt + 128],
                       sb["slab"][:, 1, dt], start=(dt == 0), stop=(dt == NDT - 1),
                       waits=w if dt == 0 else [], inc="pe" if dt == NDT - 1 else None)
        t_v.append(p.op("vector", "scalar_tensor_tensor", sb["vsh"][:, ch, qt], ps_v,
                        1.0, sb["vbB"][:], op0=ALU.mult, op1=ALU.add,
                        waits=[tok], inc="ve"))
    return t_k, t_v


def emit_attention(p, sb, npos, KT, V1, QT, attnT, t_ready):
    t_div_prev = [None] * 4
    t_tr_prev = [None, None]
    t_copy = None
    t_div_last = None
    t_exp_prev = None
    t_avl = [None, None]
    npair = npos // 2

    def emit_S(h, pos, t_extra):
        jrel = 3 - pos if pos < 4 else -1
        qlo = 128 * jrel if jrel > 0 else 0
        N = 512 - qlo
        bank = pos % 2
        ps_S = sb["psA"][:, 512 * bank:512 * bank + N]
        w = list(t_extra)
        return p.op("tensor", "matmul", ps_S, KT[:, 128 * pos:128 * pos + 128],
                    QT[:, h, qlo:512], start=True, stop=True, waits=w, inc="pe")

    def emit_exp(h, i, t_s):
        buf = i % 2
        we = [t_s]
        if t_avl[buf] is not None:
            we.append(t_avl[buf])
        masks = {}
        if i < 2:
            for k in range(2):
                pos = 2 * i + k
                jrel = 3 - pos
                qlo = 128 * jrel if jrel > 0 else 0
                N = 512 - qlo
                t_e = p.op("scalar", "activation",
                           out=sb["expS"][:, buf, 512 * k:512 * k + N],
                           in_=sb["psA"][:, 512 * (pos % 2):512 * (pos % 2) + N],
                           func=AF.Exp, bias=EXPB, scale=SCALE,
                           waits=we if k == 0 else [], inc="sc")
                we = []
                masks[pos] = p.op("vector", "tensor_mul",
                                  sb["expS"][:, buf, 512 * k:512 * k + 128],
                                  sb["expS"][:, buf, 512 * k:512 * k + 128],
                                  sb["tri"][:], waits=[t_e], inc="ve")
            return t_e, masks
        t_e = p.op("scalar", "activation", out=sb["expS"][:, buf, :],
                   in_=sb["psA"][:, 0:1024], func=AF.Exp, bias=EXPB, scale=SCALE,
                   waits=we, inc="sc")
        return t_e, masks

    def emit_AV(h, i, t_e, masks, t_stop_out):
        buf = i % 2
        t_av = None
        for k in range(2):
            pos = 2 * i + k
            jrel = 3 - pos if pos < 4 else -1
            qlo = 128 * jrel if jrel > 0 else 0
            qt0 = jrel if jrel > 0 else 0
            for qt in range(qt0, 4):
                start = (pos == 3 - qt)
                stop = (pos == npos - 1)
                ps_O = sb["psA"][:, 1024 + 512 * qt:1024 + 512 * qt + 129]
                w = []
                if k == 0 and qt == qt0:
                    w.append(masks.get(pos, t_e))
                elif pos in masks and qt == qt0:
                    w.append(masks[pos])
                if start and t_div_prev[qt] is not None:
                    w.append(t_div_prev[qt])
                t_av = p.op("tensor", "matmul", ps_O,
                            sb["expS"][:, buf, 512 * k + 128 * qt - qlo:512 * k + 128 * qt - qlo + 128],
                            V1[:, pos, :], start=start, stop=stop, waits=w, inc="pe")
                if stop:
                    t_stop_out[qt] = t_av
        t_avl[buf] = t_av

    for h in range(NH):
        t_av_stop = [None] * 4
        pend = None
        for i in range(npair):
            t_s = emit_S(h, 2 * i, [t_exp_prev] if t_exp_prev is not None else
                         (list(t_ready) if h == 0 and i == 0 else []))
            t_s = emit_S(h, 2 * i + 1, [])
            t_e, masks = emit_exp(h, i, t_s)
            t_exp_prev = t_e
            if pend is not None:
                emit_AV(h, pend[0], pend[1], pend[2], t_av_stop)
            pend = (i, t_e, masks)
        emit_AV(h, pend[0], pend[1], pend[2], t_av_stop)
        for qt in range(4):
            ps_O = sb["psA"][:, 1024 + 512 * qt:1024 + 512 * qt + 129]
            w = [t_av_stop[qt]]
            if t_div_last is not None:
                w.append(t_div_last)
            t_rc = p.op("vector", "reciprocal", sb["rec"][:, 0:1], ps_O[:, 128:129],
                        waits=w, inc="ve")
            w = [t_rc]
            if t_tr_prev[qt % 2] is not None:
                w.append(t_tr_prev[qt % 2])
            t_div = p.op("scalar", "activation", out=sb["ttok"][:, qt % 2],
                         in_=ps_O[:, 0:128], func=AF.Identity, bias=0.0,
                         scale=sb["rec"][:, 0:1], waits=w, inc="sc")
            t_div_prev[qt] = t_div
            t_div_last = t_div
            w = [t_div]
            if t_copy is not None:
                w.append(t_copy)
            t_tr = p.op("tensor", "transpose", sb["psT"][:, 0:128],
                        sb["ttok"][:, qt % 2], sb["identbf"][:], waits=w, inc="pe")
            t_tr_prev[qt % 2] = t_tr
            t_copy = p.op("vector", "tensor_copy", attnT[:, h, 128 * qt:128 * qt + 128],
                          sb["psT"][:, 0:128], waits=[t_tr], inc="ve")
    return t_copy



def build_A(debug=False):
    nc = bass.Bass("TRN2", target_bir_lowering=False)
    dr = {}
    if debug:
        dr["dbg_ln1"] = nc.dram_tensor("dbg_ln1", [128, NDT * CH], BF16, kind="ExternalOutput").ap()
        dr["dbg_A"] = nc.dram_tensor("dbg_A", [128, 2 * NDT * 64], BF16, kind="ExternalOutput").ap()
        dr["dbg_B"] = nc.dram_tensor("dbg_B", [128, D], BF16, kind="ExternalOutput").ap()
        dr["dbg_ax"] = nc.dram_tensor("dbg_ax", [128, CH], BF16, kind="ExternalOutput").ap()
        dr["dbg_rstdB"] = nc.dram_tensor("dbg_rstdB", [128, CH], BF16, kind="ExternalOutput").ap()
        dr["dbg_mrB"] = nc.dram_tensor("dbg_mrB", [128, CH], BF16, kind="ExternalOutput").ap()
    dr["xT"] = nc.dram_tensor("xT", [D, 2 * CH], F32, kind="ExternalInput").ap()
    dr["kernel6"] = nc.dram_tensor("kernel6", [NET, 128, NDT, 128], F32, kind="ExternalInput").ap()
    dr["wk6"] = nc.dram_tensor("wk6", [128, NDT, 128], F32, kind="ExternalInput").ap()
    dr["wv6"] = nc.dram_tensor("wv6", [128, NDT, 128], F32, kind="ExternalInput").ap()
    dr["btf"] = nc.dram_tensor("btf", [128, NET], F32, kind="ExternalInput").ap()
    dr["kbc"] = nc.dram_tensor("kbc", [128, 1], F32, kind="ExternalInput").ap()
    dr["vbr"] = nc.dram_tensor("vbr", [1, 128], F32, kind="ExternalInput").ap()
    dr["gate2"] = nc.dram_tensor("gate2", [1, 32], F32, kind="ExternalInput").ap()
    dr["ident4"] = nc.dram_tensor("ident4", [128, 32], F32, kind="ExternalInput").ap()
    dr["loraA7"] = nc.dram_tensor("loraA7", [16, 128, 1024], F32, kind="ExternalInput").ap()
    dr["loraAs7"] = nc.dram_tensor("loraAs7", [128, 1024], F32, kind="ExternalInput").ap()
    dr["loraB7"] = nc.dram_tensor("loraB7", [16, 32, D], F32, kind="ExternalInput").ap()
    dr["loraBs"] = nc.dram_tensor("loraBs", [32, D], F32, kind="ExternalInput").ap()
    dr["x1T"] = nc.dram_tensor("x1T", [D, 2 * CH], F32, kind="ExternalOutput").ap()
    dr["kshard"] = nc.dram_tensor("kshard", [128, 2 * CH], BF16, kind="ExternalOutput").ap()
    dr["vshard"] = nc.dram_tensor("vshard", [2 * CH, 128], BF16, kind="ExternalOutput").ap()

    with contextlib.ExitStack() as es:
        sb = alloc_common(nc, es, "A")
        p = Prog(nc, es)
        t_const = emit_consts(p, sb, "A")
        t_gate = p.op("sync", "dma_start", out=sb["gate2f"][:], in_=dr["gate2"],
                      inc="dms", dma=True)
        t_i4 = p.op("gpsimd", "dma_start", out=sb["ident4"][:], in_=dr["ident4"],
                    inc="dmg", dma=True)
        p.op("sync", "dma_start", out=sb["btf"][:], in_=dr["btf"], inc="dms", dma=True)
        p.op("sync", "dma_start", out=sb["kbc"][:], in_=dr["kbc"], inc="dms", dma=True)
        p.op("sync", "dma_start", out=sb["vbr"][:], in_=dr["vbr"], inc="dms", dma=True)
        t_misc = p.cur("dms")
        t_btfh = p.op("vector", "tensor_scalar", sb["btfh"][:], sb["btf"][:], 0.5,
                      None, op0=ALU.mult, waits=[t_misc], inc="ve")
        t_vrow = p.op("vector", "tensor_copy", sb["rowbf"][0:1, 0, 0:128],
                      sb["vbr"][0:1, :], waits=[t_misc], inc="ve")
        mmv = p.op("tensor", "matmul", sb["ps_misc"][:, 100:228],
                   sb["onescol"][0:1, 0:1].broadcast_to([1, 128]),
                   sb["rowbf"][0:1, 0, 0:128],
                   start=True, stop=True, waits=[t_vrow, t_const], inc="pe")
        t_vbB = p.op("vector", "tensor_copy", sb["vbB"][:], sb["ps_misc"][:, 100:228],
                     waits=[mmv], inc="ve")
        t_lora = emit_lora_build(p, sb, dr, t_gate, t_const, t_vbB, [t_i4])

        stream = SlabStream(p, sb)
        xr = dr["xT"].rearrange("(dt p) t -> p dt t", p=128)
        x1r = dr["x1T"].rearrange("(dt p) t -> p dt t", p=128)
        t_store_prev = None
        t_prev_kv_pe = None
        for ci in range(2):
            ch = ci
            w = [] if ci == 0 else [t_store_prev]
            t_x = p.op("sync", "dma_start", out=sb["xs"][:],
                       in_=xr[:, :, CH * ch:CH * ch + CH], waits=w,
                       inc=f"dx{ch}", dma=True)
            lnw = [t_x]
            if t_prev_kv_pe is not None:
                lnw += [t_prev_kv_pe, p.cur("sc"), p.cur("ve")]
            t_ln1 = emit_ln(p, sb, sb["xs"][:], sb["lnxt"], lnw)
            if debug and ci == 0:
                p.op("sync", "dma_start", out=dr["dbg_rstdB"], in_=sb["rstdB"][:],
                     waits=[t_ln1], inc="dst", dma=True)
                p.op("sync", "dma_start", out=dr["dbg_mrB"], in_=sb["mrB"][:],
                     waits=[t_ln1], inc="dst", dma=True)
                p.op("sync", "dma_start", out=dr["dbg_ln1"],
                     in_=sb["lnxt"][:].rearrange("p a b -> p (a b)"),
                     waits=[t_ln1], inc="dst", dma=True)
                p.op("sync", "dma_start", out=dr["dbg_A"],
                     in_=sb["A_combT"][:].rearrange("p a b -> p (a b)"),
                     waits=t_lora, inc="dst", dma=True)
                p.op("sync", "dma_start", out=dr["dbg_B"], in_=sb["B_combT"][:],
                     waits=t_lora, inc="dst", dma=True)
            wr = make_residual_writer(p, sb, sb["xs"], sb["btfh"], HALF)
            lora = (sb["A_combT"][:, ci], sb["B_combT"][64 * ci:64 * ci + 64, :], ci)
            if debug and ci == 0:
                sb["_dbg_ax_out"] = dr["dbg_ax"]
            fw = [t_ln1, t_btfh] + (t_lora if ci == 0 else [])
            emit_proj(p, sb, stream, dr["kernel6"], sb["lnxt"], wr, lora=lora,
                      first_waits=fw)
            t_ffn_pe = p.cur("pe")
            t_ffn_ve = p.cur("ve")
            t_store_prev = p.op("sync", "dma_start",
                                out=x1r[:, :, CH * ch:CH * ch + CH],
                                in_=sb["xs"][:], waits=[t_ffn_ve], inc="dst", dma=True)
            t_ln2 = emit_ln(p, sb, sb["xs"][:], sb["lnxt"], [t_ffn_pe])
            p.op("gpsimd", "dma_start",
                 out=sb["slab"][:, 0].rearrange("p dt c -> p (dt c)"),
                 in_=dr["wk6"].rearrange("p dt c -> p (dt c)"),
                 waits=[stream.done[-1]], inc="dkv", dma=True)
            p.op("gpsimd", "dma_start",
                 out=sb["slab"][:, 1].rearrange("p dt c -> p (dt c)"),
                 in_=dr["wv6"].rearrange("p dt c -> p (dt c)"),
                 waits=[stream.done[-1]], inc="dkv", dma=True)
            t_wkv = p.cur("dkv")
            emit_kv(p, sb, sb["lnxt"], ch, [t_ln2, t_wkv, t_vbB, t_misc])
            t_prev_kv_pe = p.cur("pe")
            stream.gate_next(t_prev_kv_pe, 2)
        t_sk = p.op("sync", "dma_start", out=dr["kshard"],
                    in_=sb["ksh"][:].rearrange("p ch t -> p (ch t)"),
                    waits=[p.cur("sc")], inc="dst", dma=True)
        t_sv = p.op("sync", "dma_start",
                    out=dr["vshard"].rearrange("(ch qt p) e -> p ch qt e", p=128, qt=4),
                    in_=sb["vsh"][:], waits=[p.cur("ve")], inc="dst", dma=True)
        p.op("sync", "nop", waits=[p.cur("dst")])
        p.run()
    return nc


def build_B():
    nc = bass.Bass("TRN2", target_bir_lowering=False)
    dr = {}
    dr["x1T"] = nc.dram_tensor("x1T", [D, 2 * CH], F32, kind="ExternalInput").ap()
    dr["wq6"] = nc.dram_tensor("wq6", [NET, 128, NDT, 128], F32, kind="ExternalInput").ap()
    dr["wo6"] = nc.dram_tensor("wo6", [NET, 128, NDT, 128], F32, kind="ExternalInput").ap()
    dr["kernel6"] = nc.dram_tensor("kernel6", [NET, 128, NDT, 128], F32, kind="ExternalInput").ap()
    dr["btf"] = nc.dram_tensor("btf", [128, NET], F32, kind="ExternalInput").ap()
    dr["btq"] = nc.dram_tensor("btq", [128, NET], F32, kind="ExternalInput").ap()
    dr["bto"] = nc.dram_tensor("bto", [128, NET], F32, kind="ExternalInput").ap()
    dr["gate2"] = nc.dram_tensor("gate2", [1, 32], F32, kind="ExternalInput").ap()
    dr["ident4"] = nc.dram_tensor("ident4", [128, 32], F32, kind="ExternalInput").ap()
    dr["loraA7"] = nc.dram_tensor("loraA7", [16, 128, 1024], F32, kind="ExternalInput").ap()
    dr["loraAs7"] = nc.dram_tensor("loraAs7", [128, 1024], F32, kind="ExternalInput").ap()
    dr["loraB7"] = nc.dram_tensor("loraB7", [16, 32, D], F32, kind="ExternalInput").ap()
    dr["loraBs"] = nc.dram_tensor("loraBs", [32, D], F32, kind="ExternalInput").ap()
    dr["meta"] = nc.dram_tensor("meta", [1, 8], mybir.dt.int32, kind="ExternalInput").ap()
    dr["kA"] = nc.dram_tensor("kA", [128, 32 * 128], BF16, kind="ExternalInput").ap()
    dr["vA"] = nc.dram_tensor("vA", [32 * 128, 129], BF16, kind="ExternalInput").ap()
    dr["kB"] = nc.dram_tensor("kB", [128, 64 * 128], BF16, kind="ExternalInput").ap()
    dr["vB"] = nc.dram_tensor("vB", [64 * 128, 129], BF16, kind="ExternalInput").ap()
    dr["outT"] = nc.dram_tensor("outT", [D, 2 * CH], F32, kind="ExternalOutput").ap()

    with contextlib.ExitStack() as es:
        sb = alloc_common(nc, es, "B")
        sb["meta"] = es.enter_context(nc.sbuf_tensor("sb_meta", [1, 8], mybir.dt.int32))
        p = Prog(nc, es)
        t_const = emit_consts(p, sb, "B")
        t_gate = p.op("sync", "dma_start", out=sb["gate2f"][:], in_=dr["gate2"],
                      inc="dms", dma=True)
        t_meta = p.op("sync", "dma_start", out=sb["meta"][:], in_=dr["meta"],
                      inc="dms", dma=True)
        t_i4 = p.op("gpsimd", "dma_start", out=sb["ident4"][:], in_=dr["ident4"],
                    inc="dmg", dma=True)
        for e in ("tensor", "scalar", "vector"):
            p.wait(e, t_meta)
            p.load_reg(e, "_attn_reg", sb["meta"][0:1, 0:1])
        p.op("sync", "dma_start", out=sb["btf"][:], in_=dr["btf"], inc="dms", dma=True)
        p.op("sync", "dma_start", out=sb["btq"][:], in_=dr["btq"], inc="dms", dma=True)
        p.op("sync", "dma_start", out=sb["bto"][:], in_=dr["bto"], inc="dms", dma=True)
        t_misc = p.cur("dms")
        t_btfh = p.op("vector", "tensor_scalar", sb["btfh"][:], sb["btf"][:], 0.5,
                      None, op0=ALU.mult, waits=[t_misc], inc="ve")
        t_lora = emit_lora_build(p, sb, dr, t_gate, t_const, None, [t_i4])

        stream = SlabStream(p, sb)
        x1r = dr["x1T"].rearrange("(dt p) t -> p dt t", p=128)
        outr = dr["outT"].rearrange("(dt p) t -> p dt t", p=128)
        kv_dram = [(dr["kA"], dr["vA"], 32), (dr["kB"], dr["vB"], 64)]
        t_store_prev = None
        t_attn_prev = None
        for ci in range(2):
            ch = ci
            w = [t_misc] if ci == 0 else [t_store_prev]
            t_x = p.op("sync", "dma_start", out=sb["xs"][:],
                       in_=x1r[:, :, CH * ch:CH * ch + CH], waits=w,
                       inc=f"dx{ch}", dma=True)
            t_ln2 = emit_ln(p, sb, sb["xs"][:], sb["lnxt"],
                            [t_x] if ci == 0 else [t_x, p.cur("sc")])
            qw = make_bf16_writer(p, sb, lambda et: sb["QT"][:, et], sb["btq"])
            emit_proj(p, sb, stream, dr["wq6"], sb["lnxt"], qw,
                      first_waits=[t_ln2] + (t_lora if ci == 0 else []))
            t_q_sc = p.cur("sc")
            kdr, vdr, npos = kv_dram[ci]
            KT = sb["big"][:, 0:npos * 128]
            V1 = sb["big"][:, 8192:8192 + npos * 129].rearrange("p (k c) -> p k c", c=129)
            wkv = list(t_lora) + [p.cur("ve")] if ci == 0 else [t_attn_prev]
            p.op("scalar", "dma_start", out=KT, in_=kdr, waits=wkv,
                 inc=f"dkt{ci}", dma=True)
            p.op("scalar", "dma_start", out=V1,
                 in_=vdr.rearrange("(k p) c -> p k c", p=128),
                 inc=f"dkt{ci}", dma=True)
            t_kv = p.cur(f"dkt{ci}")
            engines = ("tensor", "scalar", "vector")
            snap = p.snapshot()
            deltas = []
            attn_sems = ("pe", "sc", "ve")
            for core in range(NCORES):
                p.restore_counts(snap)
                K = 4 * ((core if ci == 0 else 15 - core) + 1)
                p.branch_begin(engines, "_attn_reg", core)
                emit_attention(p, sb, K, KT, V1, sb["QT"], sb["lnxt"],
                               [t_q_sc, t_kv])
                deltas.append({n: p._sems[n][1] - snap[0].get(n, 0)
                               for n in attn_sems})
                p.branch_end(engines)
            mx = {n: max(d[n] for d in deltas) for n in attn_sems}
            for core in range(NCORES):
                need = {n: mx[n] - deltas[core][n] for n in attn_sems}
                if any(need.values()):
                    p.branch_begin(engines, "_attn_reg", core)
                    p.sem_pad("tensor", "pe", need["pe"])
                    p.sem_pad("scalar", "sc", need["sc"])
                    p.sem_pad("vector", "ve", need["ve"])
                    p.branch_end(engines)
            p.restore_counts(snap)
            for n in attn_sems:
                p._sems[n][1] = snap[0].get(n, 0) + mx[n]
            t_attn_prev = (p._sems["ve"], p._sems["ve"][1])
            ow = make_residual_writer(p, sb, sb["xs"], sb["bto"], ONE)
            emit_proj(p, sb, stream, dr["wo6"], sb["lnxt"], ow,
                      first_waits=[t_attn_prev])
            t_o_pe = p.cur("pe")
            t_ln3 = emit_ln(p, sb, sb["xs"][:], sb["lnxt"], [t_o_pe, p.cur("sc")])
            fw2 = make_residual_writer(p, sb, sb["xs"], sb["btfh"], HALF)
            lora = (sb["A_combT"][:, ci], sb["B_combT"][64 * ci:64 * ci + 64, :], ci)
            emit_proj(p, sb, stream, dr["kernel6"], sb["lnxt"], fw2, lora=lora,
                      first_waits=[t_ln3, t_btfh])
            t_f_ve = p.cur("ve")
            t_store_prev = p.op("sync", "dma_start",
                                out=outr[:, :, CH * ch:CH * ch + CH],
                                in_=sb["xs"][:], waits=[t_f_ve], inc="dst", dma=True)
        p.op("sync", "nop", waits=[p.cur("dst")])
        p.run()
    return nc



def kernel(L0_attn_x, L0_attn_wq_w, L0_attn_wq_b, L0_attn_wk_w, L0_attn_wk_b,
           L0_attn_wv_w, L0_attn_wv_b, L0_attn_wo_w, L0_attn_wo_b, L0_ffn_gate,
           L0_ffn_kernel, L0_ffn_bias, L0_ffn_lora_A, L0_ffn_lora_B,
           L0_ffn_lora_A_shared, L0_ffn_lora_B_shared):
    f32 = np.float32
    x = np.asarray(L0_attn_x, f32)[0]
    xT = np.ascontiguousarray(x.T)
    kernel6 = _slab6(np.asarray(L0_ffn_kernel, f32), NET)
    wq6 = _slab6(np.asarray(L0_attn_wq_w, f32).T, NET)
    wo6 = _slab6(np.asarray(L0_attn_wo_w, f32).T, NET)
    wk6 = _slab6(np.asarray(L0_attn_wk_w, f32).T, 1)[0]
    wv6 = _slab6(np.asarray(L0_attn_wv_w, f32).T, 1)[0]
    btf = _bias_tile(L0_ffn_bias)
    btq = _bias_tile(L0_attn_wq_b)
    bto = _bias_tile(L0_attn_wo_b)
    kbc = np.ascontiguousarray(np.asarray(L0_attn_wk_b, f32).reshape(128, 1))
    vbr = np.ascontiguousarray(np.asarray(L0_attn_wv_b, f32).reshape(1, 128))
    gate = np.asarray(L0_ffn_gate, f32)
    lora_A = np.asarray(L0_ffn_lora_A, f32)
    lora_B = np.asarray(L0_ffn_lora_B, f32)
    lA_s = np.asarray(L0_ffn_lora_A_shared, f32)
    lB_s = np.asarray(L0_ffn_lora_B_shared, f32)
    loraA7 = np.ascontiguousarray(
        lora_A.transpose(0, 2, 1).reshape(16, 32, 128, 32).transpose(0, 2, 1, 3)
    ).reshape(16, 128, 1024)
    loraAs7 = np.ascontiguousarray(
        lA_s.T.reshape(32, 128, 32).transpose(1, 0, 2)).reshape(128, 1024)
    loraB7 = np.ascontiguousarray(lora_B.transpose(0, 2, 1))
    loraBs = np.ascontiguousarray(lB_s.T)

    chunks = [(i, 15 - i) for i in range(NCORES)]
    ident4 = np.ascontiguousarray(np.tile(np.eye(32, dtype=f32), (4, 1)))

    ncA = build_A()
    in_maps = []
    for cA, cB in chunks:
        xTc = np.ascontiguousarray(
            np.concatenate([xT[:, CH * cA:CH * cA + CH],
                            xT[:, CH * cB:CH * cB + CH]], axis=1))
        gate2 = np.ascontiguousarray(
            np.stack([gate[cA // 2], gate[cB // 2]]).reshape(1, 32))
        in_maps.append(dict(xT=xTc, kernel6=kernel6, wk6=wk6, wv6=wv6, btf=btf,
                            kbc=kbc, vbr=vbr, gate2=gate2, ident4=ident4,
                            loraA7=loraA7, loraAs7=loraAs7, loraB7=loraB7,
                            loraBs=loraBs))
    resA = run_bass_kernel_spmd(ncA, in_maps, core_ids=list(range(NCORES)))

    kfull = np.zeros((128, 16, CH), dtype=BF)
    vfull = np.zeros((SEQ, 128), dtype=BF)
    x1s = []
    for core, (cA, cB) in enumerate(chunks):
        r = resA.results[core]
        ks = np.asarray(r["kshard"]).reshape(128, 2, CH)
        vs = np.asarray(r["vshard"])
        kfull[:, cA, :] = ks[:, 0]
        kfull[:, cB, :] = ks[:, 1]
        vfull[CH * cA:CH * cA + CH] = vs[0:CH]
        vfull[CH * cB:CH * cB + CH] = vs[CH:2 * CH]
        x1s.append(np.asarray(r["x1T"]))
    kflat = np.ascontiguousarray(kfull.reshape(128, SEQ))

    def pack_slot(c, npos):
        K = 4 * (c + 1)
        kt = np.zeros((128, npos * 128), dtype=BF)
        v1 = np.zeros((npos * 128, 129), dtype=BF)
        for pp in range(K):
            g = K - 1 - pp
            kt[:, 128 * pp:128 * pp + 128] = kflat[:, 128 * g:128 * g + 128]
            v1[128 * pp:128 * pp + 128, 0:128] = vfull[128 * g:128 * g + 128]
            v1[128 * pp:128 * pp + 128, 128] = np.asarray(1.0, dtype=BF)
        return np.ascontiguousarray(kt), np.ascontiguousarray(v1)

    ncB = build_B()
    in_mapsB = []
    for core, (cA, cB) in enumerate(chunks):
        kAp, vAp = pack_slot(cA, 32)
        kBp, vBp = pack_slot(cB, 64)
        gate2 = np.ascontiguousarray(
            np.stack([gate[cA // 2], gate[cB // 2]]).reshape(1, 32))
        in_mapsB.append(dict(x1T=x1s[core], wq6=wq6, wo6=wo6, kernel6=kernel6,
                             meta=np.full((1, 8), core, dtype=np.int32),
                             btf=btf, btq=btq, bto=bto, gate2=gate2, ident4=ident4,
                             loraA7=loraA7, loraAs7=loraAs7, loraB7=loraB7,
                             loraBs=loraBs, kA=kAp, vA=vAp, kB=kBp, vB=vBp))
    resB = run_bass_kernel_spmd(ncB, in_mapsB, core_ids=list(range(NCORES)))

    out = np.empty((SEQ, D), dtype=np.float32)
    for core, (cA, cB) in enumerate(chunks):
        oT = np.asarray(resB.results[core]["outT"])
        out[CH * cA:CH * cA + CH] = oT[:, 0:CH].T
        out[CH * cB:CH * cB + CH] = oT[:, CH:2 * CH].T
    return out.reshape(1, SEQ, D)


# version 08
# speedup vs baseline: 1.0212x; 1.0212x over previous
import contextlib

import numpy as np
import ml_dtypes
import concourse.bass as bass
import concourse.mybir as mybir
from concourse.bass_utils import run_bass_kernel_spmd

F32 = mybir.dt.float32
BF16 = mybir.dt.bfloat16
AF = mybir.ActivationFunctionType
ALU = mybir.AluOpType
BF = ml_dtypes.bfloat16

D = 4096
SEQ = 8192
CH = 512
NDT = 32
NET = 32
NH = 32
NCORES = 8
SCALE = np.float32(1.0 / np.sqrt(128.0))
EXPB = np.float32(-20.0)
EPS = np.float32(1e-5)
ZERO = np.float32(0.0)
ONE = np.float32(1.0)
HALF = np.float32(0.5)
NEGHALF = np.float32(-0.5)
INV_D = np.float32(1.0 / D)

ENGS = ("sync", "gpsimd", "tensor", "vector", "scalar")


class Prog:

    def __init__(self, nc, es):
        self.nc = nc
        self.es = es
        self.q = {e: [] for e in ENGS}
        self._sems = {}

    def sem(self, name):
        if name not in self._sems:
            h = self.es.enter_context(self.nc.semaphore(name))
            self._sems[name] = [h, 0]
        return self._sems[name]

    def wait(self, eng, tok):
        if tok is None:
            return
        rec, v = tok
        self.q[eng].append(lambda E, h=rec[0], vv=v: E.wait_ge(h, vv))

    def op(self, eng, meth, *args, waits=(), inc=None, dma=False, **kw):
        for t in waits:
            self.wait(eng, t)
        if inc is not None:
            rec = self.sem(inc)
            amt = 16 if dma else 1
            rec[1] += amt
            tok = (rec, rec[1])
            self.q[eng].append(
                lambda E, m=meth, a=args, k=kw, h=rec[0], am=amt:
                    getattr(E, m)(*a, **k).then_inc(h, am))
            return tok
        self.q[eng].append(lambda E, m=meth, a=args, k=kw: getattr(E, m)(*a, **k))
        return None

    def cur(self, name):
        rec = self.sem(name)
        return (rec, rec[1])

    def snapshot(self):
        return ({k: v[1] for k, v in self._sems.items()},
                {e: len(q) for e, q in self.q.items()})

    def restore_counts(self, snap):
        for k, v in snap[0].items():
            self._sems[k][1] = v

    def branch_begin(self, engines, regname, val):
        for e in engines:
            self.q[e].append(lambda E, r=regname, v=val:
                             E.__dict__.setdefault("_ifg", []).append(
                                 _enter_if(E, E.__dict__[r], v)))

    def branch_end(self, engines):
        for e in engines:
            self.q[e].append(lambda E: E.__dict__["_ifg"].pop().__exit__(None, None, None))

    def sem_pad(self, eng, name, amount):
        if amount > 0:
            rec = self.sem(name)
            self.q[eng].append(lambda E, h=rec[0], a=amount: E.sem_inc(h, a))

    def load_reg(self, eng, regname, src_ap):
        def f(E, r=regname, ap=src_ap):
            reg = E.alloc_register(r)
            E.__dict__[r] = reg
            E.reg_load(reg, ap)
        self.q[eng].append(f)


    def run(self):
        with self.nc.Block() as blk:
            for eng in ENGS:
                ops = self.q[eng]
                if ops:
                    getattr(blk, eng)(lambda E, ops=ops: [f(E) for f in ops])


def _enter_if(E, reg, val):
    g = E.If_eq(reg, val)
    g.__enter__()
    return g



def alloc_common(nc, es, which):
    sb = {}

    def t(name, shape, dt):
        sb[name] = es.enter_context(nc.sbuf_tensor("sb_" + name, shape, dt))
        return sb[name]

    t("xs", [128, NDT, CH], F32)
    t("lnxt", [128, NDT, CH], BF16)
    t("slab", [128, 2, NDT, 128], BF16)
    t("big", [128, 16512], BF16)
    t("A_combT", [128, 2, NDT * 64], BF16)
    t("B_combT", [128, D], BF16)
    t("sq2", [128, 2, CH], BF16)
    t("tmp", [128, CH], F32)
    t("rowbf", [1, 2, CH], BF16)
    t("sel", [128, 96], BF16)
    t("ident4", [128, 32], BF16)
    t("rstdB", [128, CH], BF16)
    t("mrB", [128, CH], BF16)
    t("axs", [128, CH], BF16)
    t("identbf", [128, 128], BF16)
    t("onescol", [128, 1], BF16)
    t("gate2f", [1, 32], F32)
    t("gate2b", [1, 32], BF16)
    t("growb", [1, 128], BF16)
    t("gbc", [128, 32], F32)
    t("gcol", [128, 8], F32)
    t("btf", [128, NET], F32)
    t("btfh", [128, NET], F32)
    if which == "A":
        t("btq", [128, NET], F32)
        t("kbc", [128, 1], F32)
        t("vbB", [128, 128], BF16)
        t("vbr", [1, 128], F32)
        t("ksh", [128, 2, CH], BF16)
        t("vsh", [128, 2, 4, 128], BF16)
    else:
        t("QT", [128, NH, CH], BF16)
        t("expS", [128, 2, 2 * CH], BF16)
        t("tri", [128, 128], BF16)
        t("ttok", [128, 2, 128], BF16)
        t("rec", [128, 1], F32)
        t("bto", [128, NET], F32)

    sb["psA"] = es.enter_context(nc.psum_tensor("psA", [128, 3584], F32))
    sb["psT"] = es.enter_context(nc.psum_tensor("psT", [128, 1024], BF16))
    sb["ps_ax"] = sb["psA"][0:64, 2048:2560]
    sb["ps_misc"] = sb["psA"][:, 2048:2560]
    sb["ps_sum"] = sb["psA"][0:1, 2560:3072]
    sb["ps_sq"] = sb["psA"][0:1, 3072:3584]
    sb["lsa"] = sb["big"][:, 0:8192].rearrange("p (b n x) -> p b n x", b=2, n=4)
    sb["lsb"] = sb["big"][:, 8192:12288].rearrange("p (b x) -> p b x", b=2)
    sb["accA"] = sb["big"][:, 12288:16384].bitcast(F32).rearrange("p (g x) -> p g x", g=2)
    sb["accB"] = sb["big"][:, 12288:16384].bitcast(F32)
    return sb



def _slab6(w, net):
    w = np.asarray(w, dtype=np.float32)
    return np.ascontiguousarray(w.reshape(NDT, 128, net, 128).transpose(2, 1, 0, 3))


def _bias_tile(b):
    return np.ascontiguousarray(np.asarray(b, dtype=np.float32).reshape(32, 128).T)



def emit_consts(p, sb, which):
    gp = "gpsimd"
    p.op(gp, "memset", sb["identbf"][:], 0.0)
    p.op(gp, "affine_select", out=sb["identbf"][:], in_=sb["identbf"][:],
         compare_op=ALU.not_equal, fill=1.0, base=0, pattern=[[-1, 128]],
         channel_multiplier=1)
    if which == "B":
        p.op(gp, "memset", sb["tri"][:], 1.0)
        p.op(gp, "affine_select", out=sb["tri"][:], in_=sb["tri"][:],
             compare_op=ALU.is_ge, fill=0.0, base=0, pattern=[[1, 128]],
             channel_multiplier=-1)
    return p.op(gp, "memset", sb["onescol"][:], 1.0, inc="gp")


def emit_lora_build(p, sb, dr, t_gate, t_const, t_miscbank, t_ident4):
    lsa, lsb = sb["lsa"], sb["lsb"]
    accA = sb["accA"]
    load_gate = {}
    t_g2b = p.op("vector", "tensor_copy", sb["gate2b"][:], sb["gate2f"][:],
                 waits=[t_gate], inc="ve")
    mm = p.op("tensor", "matmul", sb["ps_misc"][:, 0:32],
              sb["onescol"][0:1, 0:1].broadcast_to([1, 128]),
              sb["gate2b"][0:1, :], start=True, stop=True,
              waits=[t_g2b, t_const, t_miscbank], inc="pe")
    t_gbc = p.op("vector", "tensor_copy", sb["gbc"][:], sb["ps_misc"][:, 0:32],
                 waits=[mm], inc="ve")

    fma_done = {}
    load_toks = {}
    for j in range(4):
        w = []
        if j >= 2:
            w.append(fma_done[j - 2])
        load_toks[j] = p.op("gpsimd", "dma_start", out=lsa[:, j % 2],
                            in_=dr["loraA7"][4 * j:4 * j + 4].rearrange("n p x -> p n x"),
                            waits=w, inc=f"dla{j % 2}", dma=True)
        t_last = None
        for g in range(2):
            for a in range(4):
                n = 4 * j + a
                w = [load_toks[j], t_gbc] if (g == 0 and a == 0) else []
                src = lsa[:, j % 2, a]
                gc = sb["gbc"][:, g * 16 + n:g * 16 + n + 1]
                if j == 0 and a == 0:
                    t_last = p.op("vector", "tensor_scalar", accA[:, g], src, gc,
                                  None, op0=ALU.mult, waits=w, inc="ve")
                else:
                    t_last = p.op("vector", "scalar_tensor_tensor", accA[:, g], src,
                                  gc, accA[:, g], op0=ALU.mult, op1=ALU.add,
                                  waits=w, inc="ve")
        fma_done[j] = t_last
    t_own = None
    for g in range(2):
        t_own = p.op("vector", "tensor_copy",
                     sb["A_combT"][:, g].rearrange("p (dt r) -> p dt r", r=64)[:, :, 0:32],
                     accA[:, g].rearrange("p (dt r) -> p dt r", r=32), inc="ve")
    tsh = p.op("gpsimd", "dma_start", out=lsa[:, 0, 0],
               in_=dr["loraAs7"], waits=[t_own], inc="dla0", dma=True)
    tA = None
    for g in range(2):
        tA = p.op("vector", "tensor_copy",
                  sb["A_combT"][:, g].rearrange("p (dt r) -> p dt r", r=64)[:, :, 32:64],
                  lsa[:, 0, 0].rearrange("p (dt r) -> p dt r", r=32),
                  waits=[tsh] if g == 0 else [], inc="ve")

    t_z = p.op("vector", "memset", sb["sel"][:, 32:64], 0.0)
    t_i4 = t_ident4[0]
    tB = None
    prev_col = t_gbc
    li = 0
    load_tok = {}
    sel_used = None
    for g in range(2):
        for eh in range(2):
            for j in range(4):
                w = []
                if li >= 1:
                    w.append(load_gate[li - 1])
                load_tok[li] = p.op("gpsimd", "dma_start", out=lsb[:, li % 2],
                            in_=dr["loraB7"][4 * j:4 * j + 4, :, 2048 * eh:2048 * eh + 2048]
                            .rearrange("n r e -> (n r) e"),
                            waits=w, inc=f"dlb{li % 2}", dma=True)
                if eh == 0:
                    ci = 4 * g + j
                    t_row = p.op("vector", "tensor_copy",
                                 sb["growb"][:].rearrange("o (a r) -> o a r", a=4),
                                 sb["gate2b"][0:1, 16 * g + 4 * j:16 * g + 4 * j + 4]
                                 .unsqueeze(2).broadcast_to([1, 4, 32]),
                                 waits=[prev_col], inc="ve")
                    mmp = p.op("tensor", "matmul", sb["ps_misc"][:, 40 + ci:41 + ci],
                               sb["growb"][:], sb["onescol"][0:1, 0:1],
                               start=True, stop=True,
                               waits=[t_row], inc="pe")
                    prev_col = p.op("vector", "tensor_copy",
                                    sb["gcol"][:, ci:ci + 1],
                                    sb["ps_misc"][:, 40 + ci:41 + ci],
                                    waits=[mmp], inc="ve")
                wsel = [prev_col, t_i4]
                if g == 1 and eh == 0 and j == 0:
                    t_zz = p.op("vector", "memset", sb["sel"][:, 0:32], 0.0,
                                waits=[sel_used])
                if sel_used is not None:
                    wsel.append(sel_used)
                selv = sb["sel"][:, 64 * g:64 * g + 32]
                t_sel = p.op("vector", "tensor_scalar", selv, sb["ident4"][:],
                             sb["gcol"][:, 4 * g + j:4 * g + j + 1], None,
                             op0=ALU.mult, waits=wsel, inc="ve")
                lhs = sb["sel"][:, 0:32] if g == 0 else sb["sel"][:, 0:96]
                t_mmf = None
                for ec in range(4):
                    w2 = [load_tok[li], t_sel] if ec == 0 else []
                    if j == 0 and ec == 0 and tB is not None:
                        w2.append(tB)
                    t_mmf = p.op("tensor", "matmul",
                                 sb["psA"][0:96 if g else 32, 512 * ec:512 * ec + 512],
                                 lhs, lsb[:, li % 2, 512 * ec:512 * ec + 512],
                                 start=(j == 0), stop=(j == 3),
                                 waits=w2,
                                 inc="pe")
                sel_used = t_mmf
                load_gate[li] = t_mmf
                li += 1
            for ec in range(4):
                pb = 64 * g
                tB = p.op("scalar", "activation",
                          out=sb["B_combT"][pb:pb + 32, 2048 * eh + 512 * ec:2048 * eh + 512 * ec + 512],
                          in_=sb["psA"][pb:pb + 32, 512 * ec:512 * ec + 512],
                          func=AF.Identity, bias=0.0, scale=ONE,
                          waits=[load_gate[li - 1]], inc="sc")
    tBs0 = p.op("gpsimd", "dma_start", out=sb["B_combT"][32:64, :], in_=dr["loraBs"],
                inc="dlb0", dma=True)
    tBs1 = p.op("gpsimd", "dma_start", out=sb["B_combT"][96:128, :], in_=dr["loraBs"],
                inc="dlb1", dma=True)
    return [tA, tB, tBs0, tBs1]


def emit_ln(p, sb, xs_ch, lnxt, first_waits):
    t0 = p.op("vector", "tensor_copy", lnxt[:], xs_ch, waits=first_waits, inc="ve")
    sq = sb["sq2"]
    for dt in range(NDT):
        p.op("tensor", "matmul", sb["ps_sum"], sb["onescol"][:, 0:1], lnxt[:, dt],
             start=(dt == 0), stop=(dt == NDT - 1), waits=[t0] if dt == 0 else [])
    t_pe = [None] * NDT
    for dt in range(NDT):
        w = [t0] if dt == 0 else []
        if dt >= 2:
            assert t_pe[dt - 2] is not None
            w.append(t_pe[dt - 2])
        t_sq = p.op("vector", "tensor_mul", sq[:, dt % 2], lnxt[:, dt], lnxt[:, dt],
                    waits=w, inc="ve")
        t_pe[dt] = p.op("tensor", "matmul", sb["ps_sq"], sb["onescol"][:, 0:1],
                        sq[:, dt % 2], start=(dt == 0), stop=(dt == NDT - 1),
                        waits=[t_sq], inc="pe")
    t_mm = t_pe[NDT - 1]
    mean = sb["psT"][0:1, :].bitcast(F32)
    rv = sb["ps_sq"]
    p.op("vector", "tensor_scalar", mean, sb["ps_sum"], float(INV_D), None,
         op0=ALU.mult, waits=[t_mm])
    t_ex2 = p.op("vector", "tensor_scalar", rv, sb["ps_sq"], float(INV_D), None,
                 op0=ALU.mult, inc="ve")
    t_msq = p.op("scalar", "activation", out=sb["rowbf"][:, 0], in_=mean,
                 func=AF.Square, bias=ZERO, scale=ONE, waits=[t_ex2], inc="sc")
    t_var = p.op("vector", "tensor_sub", rv, rv, sb["rowbf"][:, 0],
                 waits=[t_msq], inc="ve")
    p.op("scalar", "activation", out=rv, in_=rv, func=AF.Ln, bias=EPS, scale=ONE,
         waits=[t_var])
    t_rstd = p.op("scalar", "activation", out=rv, in_=rv, func=AF.Exp,
                  bias=ZERO, scale=NEGHALF, inc="sc")
    p.op("vector", "tensor_copy", sb["rowbf"][:, 0], rv, waits=[t_rstd])
    t_rows = p.op("vector", "tensor_mul", sb["rowbf"][:, 1], mean, sb["rowbf"][:, 0],
                  inc="ve")
    m1 = p.op("tensor", "matmul", sb["ps_misc"][:, 0:CH],
              sb["onescol"][0:1, 0:1].broadcast_to([1, 128]), sb["rowbf"][0:1, 0],
              start=True, stop=True, waits=[t_rows], inc="pe")
    t_r = p.op("vector", "tensor_copy", sb["rstdB"][:], sb["ps_misc"][:, 0:CH],
               waits=[m1], inc="ve")
    m2 = p.op("tensor", "matmul", sb["ps_misc"][:, 0:CH],
              sb["onescol"][0:1, 0:1].broadcast_to([1, 128]), sb["rowbf"][0:1, 1],
              start=True, stop=True, waits=[t_r], inc="pe")
    t_m = p.op("vector", "tensor_copy", sb["mrB"][:], sb["ps_misc"][:, 0:CH],
               waits=[m2], inc="ve")
    t_out = None
    for dt in range(NDT):
        p.op("vector", "tensor_mul", lnxt[:, dt], lnxt[:, dt], sb["rstdB"][:],
             waits=[t_r] if dt == 0 else [])
        t_out = p.op("vector", "tensor_sub", lnxt[:, dt], lnxt[:, dt], sb["mrB"][:],
                     waits=[t_m] if dt == 0 else [], inc="ve")
    return t_out


class SlabStream:
    def __init__(self, p, sb):
        self.p, self.slab = p, sb["slab"]
        self.i = 0
        self.done = []

    def load(self, dram_et_view, extra_waits=()):
        buf = self.i % 2
        w = list(extra_waits)
        if self.i >= 2:
            w.append(self.done[self.i - 2])
        tok = self.p.op("gpsimd", "dma_start",
                        out=self.slab[:, buf].rearrange("p dt c -> p (dt c)"),
                        in_=dram_et_view.rearrange("p dt c -> p (dt c)"),
                        waits=w, inc=f"dk{buf}", dma=True)
        self.i += 1
        return buf, tok

    def note_done(self, tok):
        self.done.append(tok)

    def gate_next(self, tok, n=2):
        for k in range(1, n + 1):
            if len(self.done) >= k:
                self.done[-k] = tok


def emit_proj(p, sb, stream, wdram, rhs_src, out_writer, lora=None, first_waits=()):
    bank_free = [None] * 4
    t_ax = None
    if lora is not None:
        A_view, B_rows, g = lora
        pb = 64 * g
        ps_ax = sb["psA"][pb:pb + 64, 2048:2560]
        axs = sb["axs"][pb:pb + 64, :]
        mm = None
        for dt in range(NDT):
            w = list(first_waits) if dt == 0 else []
            mm = p.op("tensor", "matmul", ps_ax,
                      A_view[:, 64 * dt:64 * dt + 64], rhs_src[:, dt],
                      start=(dt == 0), stop=(dt == NDT - 1), waits=w,
                      inc="pe" if dt == NDT - 1 else None)
        t_ax = p.op("scalar", "activation", out=axs, in_=ps_ax,
                    func=AF.Identity, bias=0.0, scale=ONE, waits=[mm], inc="sc")
    for et in range(NET):
        buf, ltok = stream.load(wdram[et])
        bank = et % 4
        ps = sb["psA"][:, 512 * bank:512 * bank + 512]
        w = [ltok]
        if bank_free[bank] is not None:
            w.append(bank_free[bank])
        if et == 0 and lora is None:
            w += list(first_waits)
        tok = None
        for dt in range(NDT):
            stop = (dt == NDT - 1 and lora is None)
            tok = p.op("tensor", "matmul", ps, stream.slab[:, buf, dt], rhs_src[:, dt],
                       start=(dt == 0), stop=stop, waits=w if dt == 0 else [],
                       inc="pe" if stop else None)
        if lora is not None:
            tok = p.op("tensor", "matmul", ps, B_rows[:, 128 * et:128 * et + 128],
                       axs, start=False, stop=True,
                       waits=[t_ax] if et == 0 else [], inc="pe")
        stream.note_done(tok)
        bank_free[bank] = out_writer(et, ps, tok)
    if lora is not None and sb.get("_dbg_ax_out") is not None:
        p.op("sync", "dma_start", out=sb["_dbg_ax_out"], in_=sb["axs"][:],
             waits=[t_ax], inc="dst", dma=True)
        sb["_dbg_ax_out"] = None
    return bank_free


def make_residual_writer(p, sb, xs_t, btile_scaled, scale):
    state = {"prev": None}

    def writer(et, ps, tok_pe):
        w = [tok_pe]
        if state["prev"] is not None:
            w.append(state["prev"])
        t_sc = p.op("scalar", "activation", out=sb["tmp"][:], in_=ps,
                    func=AF.Identity, bias=btile_scaled[:, et:et + 1], scale=scale,
                    waits=w, inc="sc")
        state["prev"] = p.op("vector", "tensor_add", xs_t[:, et], xs_t[:, et],
                             sb["tmp"][:], waits=[t_sc], inc="ve")
        return t_sc
    return writer


def make_bf16_writer(p, sb, out_et, btile):
    def writer(et, ps, tok_pe):
        return p.op("scalar", "activation", out=out_et(et), in_=ps,
                    func=AF.Identity, bias=btile[:, et:et + 1], scale=ONE,
                    waits=[tok_pe], inc="sc")
    return writer


def emit_kv(p, sb, lnxt, ch, first_waits):
    ps_k = sb["psA"][:, 0:512]
    tok = None
    for dt in range(NDT):
        tok = p.op("tensor", "matmul", ps_k, sb["slab"][:, 0, dt], lnxt[:, dt],
                   start=(dt == 0), stop=(dt == NDT - 1),
                   waits=first_waits if dt == 0 else [],
                   inc="pe" if dt == NDT - 1 else None)
    t_k = p.op("scalar", "activation", out=sb["ksh"][:, ch], in_=ps_k,
               func=AF.Identity, bias=sb["kbc"][:, 0:1], scale=ONE,
               waits=[tok], inc="sc")
    t_v = []
    for qt in range(4):
        ps_v = sb["psA"][:, 512 + 512 * (qt % 2):512 + 512 * (qt % 2) + 128]
        w = []
        if qt >= 2:
            w.append(t_v[qt - 2])
        tok = None
        for dt in range(NDT):
            tok = p.op("tensor", "matmul", ps_v, lnxt[:, dt, 128 * qt:128 * qt + 128],
                       sb["slab"][:, 1, dt], start=(dt == 0), stop=(dt == NDT - 1),
                       waits=w if dt == 0 else [], inc="pe" if dt == NDT - 1 else None)
        t_v.append(p.op("vector", "scalar_tensor_tensor", sb["vsh"][:, ch, qt], ps_v,
                        1.0, sb["vbB"][:], op0=ALU.mult, op1=ALU.add,
                        waits=[tok], inc="ve"))
    return t_k, t_v


def emit_attention(p, sb, npos, KT, V1, QT, attnT, t_ready):
    t_div_prev = [None] * 4
    t_tr_prev = [None, None]
    t_copy = None
    t_div_last = None
    t_exp_prev = None
    t_avl = [None, None]
    npair = npos // 2

    def emit_S(h, pos, t_extra):
        jrel = 3 - pos if pos < 4 else -1
        qlo = 128 * jrel if jrel > 0 else 0
        N = 512 - qlo
        bank = pos % 2
        ps_S = sb["psA"][:, 512 * bank:512 * bank + N]
        w = list(t_extra)
        return p.op("tensor", "matmul", ps_S, KT[:, 128 * pos:128 * pos + 128],
                    QT[:, h, qlo:512], start=True, stop=True, waits=w, inc="pe")

    def emit_exp(h, i, t_s):
        buf = i % 2
        we = [t_s]
        if t_avl[buf] is not None:
            we.append(t_avl[buf])
        masks = {}
        if i < 2:
            for k in range(2):
                pos = 2 * i + k
                jrel = 3 - pos
                qlo = 128 * jrel if jrel > 0 else 0
                N = 512 - qlo
                t_e = p.op("scalar", "activation",
                           out=sb["expS"][:, buf, 512 * k:512 * k + N],
                           in_=sb["psA"][:, 512 * (pos % 2):512 * (pos % 2) + N],
                           func=AF.Exp, bias=EXPB, scale=SCALE,
                           waits=we if k == 0 else [], inc="sc")
                we = []
                masks[pos] = p.op("vector", "tensor_mul",
                                  sb["expS"][:, buf, 512 * k:512 * k + 128],
                                  sb["expS"][:, buf, 512 * k:512 * k + 128],
                                  sb["tri"][:], waits=[t_e], inc="ve")
            return t_e, masks
        t_e = p.op("scalar", "activation", out=sb["expS"][:, buf, :],
                   in_=sb["psA"][:, 0:1024], func=AF.Exp, bias=EXPB, scale=SCALE,
                   waits=we, inc="sc")
        return t_e, masks

    def emit_AV(h, i, t_e, masks, t_stop_out):
        buf = i % 2
        t_av = None
        for k in range(2):
            pos = 2 * i + k
            jrel = 3 - pos if pos < 4 else -1
            qlo = 128 * jrel if jrel > 0 else 0
            qt0 = jrel if jrel > 0 else 0
            for qt in range(qt0, 4):
                start = (pos == 3 - qt)
                stop = (pos == npos - 1)
                ps_O = sb["psA"][:, 1024 + 512 * qt:1024 + 512 * qt + 129]
                w = []
                if k == 0 and qt == qt0:
                    w.append(masks.get(pos, t_e))
                elif pos in masks and qt == qt0:
                    w.append(masks[pos])
                if start and t_div_prev[qt] is not None:
                    w.append(t_div_prev[qt])
                t_av = p.op("tensor", "matmul", ps_O,
                            sb["expS"][:, buf, 512 * k + 128 * qt - qlo:512 * k + 128 * qt - qlo + 128],
                            V1[:, pos, :], start=start, stop=stop, waits=w, inc="pe")
                if stop:
                    t_stop_out[qt] = t_av
        t_avl[buf] = t_av

    for h in range(NH):
        t_av_stop = [None] * 4
        pend = None
        for i in range(npair):
            t_s = emit_S(h, 2 * i, [t_exp_prev] if t_exp_prev is not None else
                         (list(t_ready) if h == 0 and i == 0 else []))
            t_s = emit_S(h, 2 * i + 1, [])
            t_e, masks = emit_exp(h, i, t_s)
            t_exp_prev = t_e
            if pend is not None:
                emit_AV(h, pend[0], pend[1], pend[2], t_av_stop)
            pend = (i, t_e, masks)
        emit_AV(h, pend[0], pend[1], pend[2], t_av_stop)
        for qt in range(4):
            ps_O = sb["psA"][:, 1024 + 512 * qt:1024 + 512 * qt + 129]
            w = [t_av_stop[qt]]
            if t_div_last is not None:
                w.append(t_div_last)
            t_rc = p.op("vector", "reciprocal", sb["rec"][:, 0:1], ps_O[:, 128:129],
                        waits=w, inc="ve")
            w = [t_rc]
            if t_tr_prev[qt % 2] is not None:
                w.append(t_tr_prev[qt % 2])
            t_div = p.op("scalar", "activation", out=sb["ttok"][:, qt % 2],
                         in_=ps_O[:, 0:128], func=AF.Identity, bias=0.0,
                         scale=sb["rec"][:, 0:1], waits=w, inc="sc")
            t_div_prev[qt] = t_div
            t_div_last = t_div
            w = [t_div]
            if t_copy is not None:
                w.append(t_copy)
            t_tr = p.op("tensor", "transpose", sb["psT"][:, 0:128],
                        sb["ttok"][:, qt % 2], sb["identbf"][:], waits=w, inc="pe")
            t_tr_prev[qt % 2] = t_tr
            t_copy = p.op("vector", "tensor_copy", attnT[:, h, 128 * qt:128 * qt + 128],
                          sb["psT"][:, 0:128], waits=[t_tr], inc="ve")
    return t_copy



def build_A(debug=False):
    nc = bass.Bass("TRN2", target_bir_lowering=False)
    dr = {}
    if debug:
        dr["dbg_ln1"] = nc.dram_tensor("dbg_ln1", [128, NDT * CH], BF16, kind="ExternalOutput").ap()
        dr["dbg_A"] = nc.dram_tensor("dbg_A", [128, 2 * NDT * 64], BF16, kind="ExternalOutput").ap()
        dr["dbg_B"] = nc.dram_tensor("dbg_B", [128, D], BF16, kind="ExternalOutput").ap()
        dr["dbg_ax"] = nc.dram_tensor("dbg_ax", [128, CH], BF16, kind="ExternalOutput").ap()
        dr["dbg_rstdB"] = nc.dram_tensor("dbg_rstdB", [128, CH], BF16, kind="ExternalOutput").ap()
        dr["dbg_mrB"] = nc.dram_tensor("dbg_mrB", [128, CH], BF16, kind="ExternalOutput").ap()
    dr["xT"] = nc.dram_tensor("xT", [D, 2 * CH], F32, kind="ExternalInput").ap()
    dr["kernel6"] = nc.dram_tensor("kernel6", [NET, 128, NDT, 128], F32, kind="ExternalInput").ap()
    dr["wk6"] = nc.dram_tensor("wk6", [128, NDT, 128], F32, kind="ExternalInput").ap()
    dr["wv6"] = nc.dram_tensor("wv6", [128, NDT, 128], F32, kind="ExternalInput").ap()
    dr["btf"] = nc.dram_tensor("btf", [128, NET], F32, kind="ExternalInput").ap()
    dr["btq"] = nc.dram_tensor("btq", [128, NET], F32, kind="ExternalInput").ap()
    dr["wq6"] = nc.dram_tensor("wq6", [NET, 128, NDT, 128], F32, kind="ExternalInput").ap()
    dr["kbc"] = nc.dram_tensor("kbc", [128, 1], F32, kind="ExternalInput").ap()
    dr["vbr"] = nc.dram_tensor("vbr", [1, 128], F32, kind="ExternalInput").ap()
    dr["gate2"] = nc.dram_tensor("gate2", [1, 32], F32, kind="ExternalInput").ap()
    dr["ident4"] = nc.dram_tensor("ident4", [128, 32], F32, kind="ExternalInput").ap()
    dr["loraA7"] = nc.dram_tensor("loraA7", [16, 128, 1024], F32, kind="ExternalInput").ap()
    dr["loraAs7"] = nc.dram_tensor("loraAs7", [128, 1024], F32, kind="ExternalInput").ap()
    dr["loraB7"] = nc.dram_tensor("loraB7", [16, 32, D], F32, kind="ExternalInput").ap()
    dr["loraBs"] = nc.dram_tensor("loraBs", [32, D], F32, kind="ExternalInput").ap()
    dr["x1T"] = nc.dram_tensor("x1T", [D, 2 * CH], F32, kind="ExternalOutput").ap()
    dr["qT"] = nc.dram_tensor("qT", [128, 2, NET, CH], BF16, kind="ExternalOutput").ap()
    dr["kshard"] = nc.dram_tensor("kshard", [128, 2 * CH], BF16, kind="ExternalOutput").ap()
    dr["vshard"] = nc.dram_tensor("vshard", [2 * CH, 128], BF16, kind="ExternalOutput").ap()

    with contextlib.ExitStack() as es:
        sb = alloc_common(nc, es, "A")
        p = Prog(nc, es)
        t_const = emit_consts(p, sb, "A")
        t_gate = p.op("sync", "dma_start", out=sb["gate2f"][:], in_=dr["gate2"],
                      inc="dms", dma=True)
        t_i4 = p.op("gpsimd", "dma_start", out=sb["ident4"][:], in_=dr["ident4"],
                    inc="dmg", dma=True)
        p.op("sync", "dma_start", out=sb["btf"][:], in_=dr["btf"], inc="dms", dma=True)
        p.op("sync", "dma_start", out=sb["btq"][:], in_=dr["btq"], inc="dms", dma=True)
        p.op("sync", "dma_start", out=sb["kbc"][:], in_=dr["kbc"], inc="dms", dma=True)
        p.op("sync", "dma_start", out=sb["vbr"][:], in_=dr["vbr"], inc="dms", dma=True)
        t_misc = p.cur("dms")
        t_btfh = p.op("vector", "tensor_scalar", sb["btfh"][:], sb["btf"][:], 0.5,
                      None, op0=ALU.mult, waits=[t_misc], inc="ve")
        t_vrow = p.op("vector", "tensor_copy", sb["rowbf"][0:1, 0, 0:128],
                      sb["vbr"][0:1, :], waits=[t_misc], inc="ve")
        mmv = p.op("tensor", "matmul", sb["ps_misc"][:, 100:228],
                   sb["onescol"][0:1, 0:1].broadcast_to([1, 128]),
                   sb["rowbf"][0:1, 0, 0:128],
                   start=True, stop=True, waits=[t_vrow, t_const], inc="pe")
        t_vbB = p.op("vector", "tensor_copy", sb["vbB"][:], sb["ps_misc"][:, 100:228],
                     waits=[mmv], inc="ve")
        t_lora = emit_lora_build(p, sb, dr, t_gate, t_const, t_vbB, [t_i4])

        stream = SlabStream(p, sb)
        xr = dr["xT"].rearrange("(dt p) t -> p dt t", p=128)
        x1r = dr["x1T"].rearrange("(dt p) t -> p dt t", p=128)
        t_store_prev = None
        t_prev_kv_pe = None
        for ci in range(2):
            ch = ci
            w = [] if ci == 0 else [t_store_prev]
            t_x = p.op("sync", "dma_start", out=sb["xs"][:],
                       in_=xr[:, :, CH * ch:CH * ch + CH], waits=w,
                       inc=f"dx{ch}", dma=True)
            lnw = [t_x]
            if t_prev_kv_pe is not None:
                lnw += [t_prev_kv_pe, p.cur("sc"), p.cur("ve")]
            t_ln1 = emit_ln(p, sb, sb["xs"][:], sb["lnxt"], lnw)
            if debug and ci == 0:
                p.op("sync", "dma_start", out=dr["dbg_rstdB"], in_=sb["rstdB"][:],
                     waits=[t_ln1], inc="dst", dma=True)
                p.op("sync", "dma_start", out=dr["dbg_mrB"], in_=sb["mrB"][:],
                     waits=[t_ln1], inc="dst", dma=True)
                p.op("sync", "dma_start", out=dr["dbg_ln1"],
                     in_=sb["lnxt"][:].rearrange("p a b -> p (a b)"),
                     waits=[t_ln1], inc="dst", dma=True)
                p.op("sync", "dma_start", out=dr["dbg_A"],
                     in_=sb["A_combT"][:].rearrange("p a b -> p (a b)"),
                     waits=t_lora, inc="dst", dma=True)
                p.op("sync", "dma_start", out=dr["dbg_B"], in_=sb["B_combT"][:],
                     waits=t_lora, inc="dst", dma=True)
            wr = make_residual_writer(p, sb, sb["xs"], sb["btfh"], HALF)
            lora = (sb["A_combT"][:, ci], sb["B_combT"][64 * ci:64 * ci + 64, :], ci)
            if debug and ci == 0:
                sb["_dbg_ax_out"] = dr["dbg_ax"]
            fw = [t_ln1, t_btfh] + (t_lora if ci == 0 else [])
            emit_proj(p, sb, stream, dr["kernel6"], sb["lnxt"], wr, lora=lora,
                      first_waits=fw)
            t_ffn_pe = p.cur("pe")
            t_ffn_ve = p.cur("ve")
            t_store_prev = p.op("sync", "dma_start",
                                out=x1r[:, :, CH * ch:CH * ch + CH],
                                in_=sb["xs"][:], waits=[t_ffn_ve], inc="dst", dma=True)
            t_ln2 = emit_ln(p, sb, sb["xs"][:], sb["lnxt"], [t_ffn_pe])
            qstage = sb["tmp"][:].bitcast(BF16)
            st_toks = [None, None]

            def qwriter(et, ps, tok_pe, ch=ch, qstage=qstage, st_toks=st_toks):
                w = [tok_pe]
                if st_toks[et % 2] is not None:
                    w.append(st_toks[et % 2])
                t_sc = p.op("scalar", "activation",
                            out=qstage[:, 512 * (et % 2):512 * (et % 2) + 512],
                            in_=ps, func=AF.Identity, bias=sb["btq"][:, et:et + 1],
                            scale=ONE, waits=w, inc="sc")
                st_toks[et % 2] = p.op("sync", "dma_start",
                                       out=dr["qT"][:, ch, et, :],
                                       in_=qstage[:, 512 * (et % 2):512 * (et % 2) + 512],
                                       waits=[t_sc], inc=f"dq{et % 2}", dma=True)
                return st_toks[et % 2]
            emit_proj(p, sb, stream, dr["wq6"], sb["lnxt"], qwriter,
                      first_waits=[t_ln2])
            p.op("gpsimd", "dma_start",
                 out=sb["slab"][:, 0].rearrange("p dt c -> p (dt c)"),
                 in_=dr["wk6"].rearrange("p dt c -> p (dt c)"),
                 waits=[stream.done[-1]], inc="dkv", dma=True)
            p.op("gpsimd", "dma_start",
                 out=sb["slab"][:, 1].rearrange("p dt c -> p (dt c)"),
                 in_=dr["wv6"].rearrange("p dt c -> p (dt c)"),
                 waits=[stream.done[-1]], inc="dkv", dma=True)
            t_wkv = p.cur("dkv")
            emit_kv(p, sb, sb["lnxt"], ch, [t_ln2, t_wkv, t_vbB, t_misc, p.cur("dq0"), p.cur("dq1")])
            t_prev_kv_pe = p.cur("pe")
            stream.gate_next(t_prev_kv_pe, 2)
        t_sk = p.op("sync", "dma_start", out=dr["kshard"],
                    in_=sb["ksh"][:].rearrange("p ch t -> p (ch t)"),
                    waits=[p.cur("sc")], inc="dst", dma=True)
        t_sv = p.op("sync", "dma_start",
                    out=dr["vshard"].rearrange("(ch qt p) e -> p ch qt e", p=128, qt=4),
                    in_=sb["vsh"][:], waits=[p.cur("ve")], inc="dst", dma=True)
        p.op("sync", "nop", waits=[p.cur("dst")])
        p.run()
    return nc


def build_B():
    nc = bass.Bass("TRN2", target_bir_lowering=False)
    dr = {}
    dr["x1T"] = nc.dram_tensor("x1T", [D, 2 * CH], F32, kind="ExternalInput").ap()
    dr["qT"] = nc.dram_tensor("qT", [128, 2, NET, CH], BF16, kind="ExternalInput").ap()
    dr["wo6"] = nc.dram_tensor("wo6", [NET, 128, NDT, 128], F32, kind="ExternalInput").ap()
    dr["kernel6"] = nc.dram_tensor("kernel6", [NET, 128, NDT, 128], F32, kind="ExternalInput").ap()
    dr["btf"] = nc.dram_tensor("btf", [128, NET], F32, kind="ExternalInput").ap()
    dr["bto"] = nc.dram_tensor("bto", [128, NET], F32, kind="ExternalInput").ap()
    dr["gate2"] = nc.dram_tensor("gate2", [1, 32], F32, kind="ExternalInput").ap()
    dr["ident4"] = nc.dram_tensor("ident4", [128, 32], F32, kind="ExternalInput").ap()
    dr["loraA7"] = nc.dram_tensor("loraA7", [16, 128, 1024], F32, kind="ExternalInput").ap()
    dr["loraAs7"] = nc.dram_tensor("loraAs7", [128, 1024], F32, kind="ExternalInput").ap()
    dr["loraB7"] = nc.dram_tensor("loraB7", [16, 32, D], F32, kind="ExternalInput").ap()
    dr["loraBs"] = nc.dram_tensor("loraBs", [32, D], F32, kind="ExternalInput").ap()
    dr["meta"] = nc.dram_tensor("meta", [1, 8], mybir.dt.int32, kind="ExternalInput").ap()
    dr["kA"] = nc.dram_tensor("kA", [128, 32 * 128], BF16, kind="ExternalInput").ap()
    dr["vA"] = nc.dram_tensor("vA", [32 * 128, 129], BF16, kind="ExternalInput").ap()
    dr["kB"] = nc.dram_tensor("kB", [128, 64 * 128], BF16, kind="ExternalInput").ap()
    dr["vB"] = nc.dram_tensor("vB", [64 * 128, 129], BF16, kind="ExternalInput").ap()
    dr["outT"] = nc.dram_tensor("outT", [D, 2 * CH], F32, kind="ExternalOutput").ap()

    with contextlib.ExitStack() as es:
        sb = alloc_common(nc, es, "B")
        sb["meta"] = es.enter_context(nc.sbuf_tensor("sb_meta", [1, 8], mybir.dt.int32))
        p = Prog(nc, es)
        t_const = emit_consts(p, sb, "B")
        t_gate = p.op("sync", "dma_start", out=sb["gate2f"][:], in_=dr["gate2"],
                      inc="dms", dma=True)
        t_meta = p.op("sync", "dma_start", out=sb["meta"][:], in_=dr["meta"],
                      inc="dms", dma=True)
        t_i4 = p.op("gpsimd", "dma_start", out=sb["ident4"][:], in_=dr["ident4"],
                    inc="dmg", dma=True)
        for e in ("tensor", "scalar", "vector"):
            p.wait(e, t_meta)
            p.load_reg(e, "_attn_reg", sb["meta"][0:1, 0:1])
        p.op("sync", "dma_start", out=sb["btf"][:], in_=dr["btf"], inc="dms", dma=True)
        p.op("sync", "dma_start", out=sb["bto"][:], in_=dr["bto"], inc="dms", dma=True)
        t_misc = p.cur("dms")
        t_btfh = p.op("vector", "tensor_scalar", sb["btfh"][:], sb["btf"][:], 0.5,
                      None, op0=ALU.mult, waits=[t_misc], inc="ve")
        t_lora = emit_lora_build(p, sb, dr, t_gate, t_const, None, [t_i4])

        stream = SlabStream(p, sb)
        x1r = dr["x1T"].rearrange("(dt p) t -> p dt t", p=128)
        outr = dr["outT"].rearrange("(dt p) t -> p dt t", p=128)
        kv_dram = [(dr["kA"], dr["vA"], 32), (dr["kB"], dr["vB"], 64)]
        t_store_prev = None
        t_attn_prev = None
        for ci in range(2):
            ch = ci
            w = [t_misc] if ci == 0 else [t_store_prev]
            t_x = p.op("sync", "dma_start", out=sb["xs"][:],
                       in_=x1r[:, :, CH * ch:CH * ch + CH], waits=w,
                       inc=f"dx{ch}", dma=True)
            w = [t_lora[2]] if ci == 0 else [t_attn_prev]
            t_q_sc = p.op("scalar", "dma_start",
                          out=sb["QT"][:].rearrange("p h t -> p (h t)"),
                          in_=dr["qT"][:, ci].rearrange("p e t -> p (e t)"),
                          waits=w + [t_x], inc=f"dq{ci}", dma=True)
            kdr, vdr, npos = kv_dram[ci]
            KT = sb["big"][:, 0:npos * 128]
            V1 = sb["big"][:, 8192:8192 + npos * 129].rearrange("p (k c) -> p k c", c=129)
            wkv = list(t_lora) + [p.cur("ve")] if ci == 0 else [t_attn_prev]
            p.op("scalar", "dma_start", out=KT, in_=kdr, waits=wkv,
                 inc=f"dkt{ci}", dma=True)
            p.op("scalar", "dma_start", out=V1,
                 in_=vdr.rearrange("(k p) c -> p k c", p=128),
                 inc=f"dkt{ci}", dma=True)
            t_kv = p.cur(f"dkt{ci}")
            engines = ("tensor", "scalar", "vector")
            snap = p.snapshot()
            deltas = []
            attn_sems = ("pe", "sc", "ve")
            for core in range(NCORES):
                p.restore_counts(snap)
                K = 4 * ((core if ci == 0 else 15 - core) + 1)
                p.branch_begin(engines, "_attn_reg", core)
                emit_attention(p, sb, K, KT, V1, sb["QT"], sb["lnxt"],
                               [t_q_sc, t_kv])
                deltas.append({n: p._sems[n][1] - snap[0].get(n, 0)
                               for n in attn_sems})
                p.branch_end(engines)
            mx = {n: max(d[n] for d in deltas) for n in attn_sems}
            for core in range(NCORES):
                need = {n: mx[n] - deltas[core][n] for n in attn_sems}
                if any(need.values()):
                    p.branch_begin(engines, "_attn_reg", core)
                    p.sem_pad("tensor", "pe", need["pe"])
                    p.sem_pad("scalar", "sc", need["sc"])
                    p.sem_pad("vector", "ve", need["ve"])
                    p.branch_end(engines)
            p.restore_counts(snap)
            for n in attn_sems:
                p._sems[n][1] = snap[0].get(n, 0) + mx[n]
            t_attn_prev = (p._sems["ve"], p._sems["ve"][1])
            ow = make_residual_writer(p, sb, sb["xs"], sb["bto"], ONE)
            emit_proj(p, sb, stream, dr["wo6"], sb["lnxt"], ow,
                      first_waits=[t_attn_prev])
            t_o_pe = p.cur("pe")
            t_ln3 = emit_ln(p, sb, sb["xs"][:], sb["lnxt"], [t_o_pe, p.cur("sc")])
            fw2 = make_residual_writer(p, sb, sb["xs"], sb["btfh"], HALF)
            lora = (sb["A_combT"][:, ci], sb["B_combT"][64 * ci:64 * ci + 64, :], ci)
            emit_proj(p, sb, stream, dr["kernel6"], sb["lnxt"], fw2, lora=lora,
                      first_waits=[t_ln3, t_btfh])
            t_f_ve = p.cur("ve")
            t_store_prev = p.op("sync", "dma_start",
                                out=outr[:, :, CH * ch:CH * ch + CH],
                                in_=sb["xs"][:], waits=[t_f_ve], inc="dst", dma=True)
        p.op("sync", "nop", waits=[p.cur("dst")])
        p.run()
    return nc



def kernel(L0_attn_x, L0_attn_wq_w, L0_attn_wq_b, L0_attn_wk_w, L0_attn_wk_b,
           L0_attn_wv_w, L0_attn_wv_b, L0_attn_wo_w, L0_attn_wo_b, L0_ffn_gate,
           L0_ffn_kernel, L0_ffn_bias, L0_ffn_lora_A, L0_ffn_lora_B,
           L0_ffn_lora_A_shared, L0_ffn_lora_B_shared):
    f32 = np.float32
    x = np.asarray(L0_attn_x, f32)[0]
    xT = np.ascontiguousarray(x.T)
    kernel6 = _slab6(np.asarray(L0_ffn_kernel, f32), NET)
    wq6 = _slab6(np.asarray(L0_attn_wq_w, f32).T, NET)
    wo6 = _slab6(np.asarray(L0_attn_wo_w, f32).T, NET)
    wk6 = _slab6(np.asarray(L0_attn_wk_w, f32).T, 1)[0]
    wv6 = _slab6(np.asarray(L0_attn_wv_w, f32).T, 1)[0]
    btf = _bias_tile(L0_ffn_bias)
    btq = _bias_tile(L0_attn_wq_b)
    bto = _bias_tile(L0_attn_wo_b)
    kbc = np.ascontiguousarray(np.asarray(L0_attn_wk_b, f32).reshape(128, 1))
    vbr = np.ascontiguousarray(np.asarray(L0_attn_wv_b, f32).reshape(1, 128))
    gate = np.asarray(L0_ffn_gate, f32)
    lora_A = np.asarray(L0_ffn_lora_A, f32)
    lora_B = np.asarray(L0_ffn_lora_B, f32)
    lA_s = np.asarray(L0_ffn_lora_A_shared, f32)
    lB_s = np.asarray(L0_ffn_lora_B_shared, f32)
    loraA7 = np.ascontiguousarray(
        lora_A.transpose(0, 2, 1).reshape(16, 32, 128, 32).transpose(0, 2, 1, 3)
    ).reshape(16, 128, 1024)
    loraAs7 = np.ascontiguousarray(
        lA_s.T.reshape(32, 128, 32).transpose(1, 0, 2)).reshape(128, 1024)
    loraB7 = np.ascontiguousarray(lora_B.transpose(0, 2, 1))
    loraBs = np.ascontiguousarray(lB_s.T)

    chunks = [(i, 15 - i) for i in range(NCORES)]
    ident4 = np.ascontiguousarray(np.tile(np.eye(32, dtype=f32), (4, 1)))

    ncA = build_A()
    in_maps = []
    for cA, cB in chunks:
        xTc = np.ascontiguousarray(
            np.concatenate([xT[:, CH * cA:CH * cA + CH],
                            xT[:, CH * cB:CH * cB + CH]], axis=1))
        gate2 = np.ascontiguousarray(
            np.stack([gate[cA // 2], gate[cB // 2]]).reshape(1, 32))
        in_maps.append(dict(xT=xTc, kernel6=kernel6, wk6=wk6, wv6=wv6, btf=btf,
                            btq=btq, wq6=wq6, kbc=kbc, vbr=vbr, gate2=gate2,
                            ident4=ident4, loraA7=loraA7, loraAs7=loraAs7,
                            loraB7=loraB7, loraBs=loraBs))
    resA = run_bass_kernel_spmd(ncA, in_maps, core_ids=list(range(NCORES)))

    kfull = np.zeros((128, 16, CH), dtype=BF)
    vfull = np.zeros((SEQ, 128), dtype=BF)
    x1s = []
    for core, (cA, cB) in enumerate(chunks):
        r = resA.results[core]
        ks = np.asarray(r["kshard"]).reshape(128, 2, CH)
        vs = np.asarray(r["vshard"])
        kfull[:, cA, :] = ks[:, 0]
        kfull[:, cB, :] = ks[:, 1]
        vfull[CH * cA:CH * cA + CH] = vs[0:CH]
        vfull[CH * cB:CH * cB + CH] = vs[CH:2 * CH]
        x1s.append(np.asarray(r["x1T"]))
    kflat = np.ascontiguousarray(kfull.reshape(128, SEQ))

    def pack_slot(c, npos):
        K = 4 * (c + 1)
        kt = np.zeros((128, npos * 128), dtype=BF)
        v1 = np.zeros((npos * 128, 129), dtype=BF)
        for pp in range(K):
            g = K - 1 - pp
            kt[:, 128 * pp:128 * pp + 128] = kflat[:, 128 * g:128 * g + 128]
            v1[128 * pp:128 * pp + 128, 0:128] = vfull[128 * g:128 * g + 128]
            v1[128 * pp:128 * pp + 128, 128] = np.asarray(1.0, dtype=BF)
        return np.ascontiguousarray(kt), np.ascontiguousarray(v1)

    ncB = build_B()
    in_mapsB = []
    for core, (cA, cB) in enumerate(chunks):
        kAp, vAp = pack_slot(cA, 32)
        kBp, vBp = pack_slot(cB, 64)
        gate2 = np.ascontiguousarray(
            np.stack([gate[cA // 2], gate[cB // 2]]).reshape(1, 32))
        in_mapsB.append(dict(x1T=x1s[core], qT=np.asarray(resA.results[core]["qT"]),
                             wo6=wo6, kernel6=kernel6,
                             meta=np.full((1, 8), core, dtype=np.int32),
                             btf=btf, bto=bto, gate2=gate2, ident4=ident4,
                             loraA7=loraA7, loraAs7=loraAs7, loraB7=loraB7,
                             loraBs=loraBs, kA=kAp, vA=vAp, kB=kBp, vB=vBp))
    resB = run_bass_kernel_spmd(ncB, in_mapsB, core_ids=list(range(NCORES)))

    out = np.empty((SEQ, D), dtype=np.float32)
    for core, (cA, cB) in enumerate(chunks):
        oT = np.asarray(resB.results[core]["outT"])
        out[CH * cA:CH * cA + CH] = oT[:, 0:CH].T
        out[CH * cB:CH * cB + CH] = oT[:, CH:2 * CH].T
    return out.reshape(1, SEQ, D)


# version 12
# speedup vs baseline: 1.2097x; 1.1846x over previous
import contextlib

import numpy as np
import ml_dtypes
import concourse.bass as bass
import concourse.mybir as mybir
from concourse.bass_utils import run_bass_kernel_spmd

F32 = mybir.dt.float32
BF16 = mybir.dt.bfloat16
AF = mybir.ActivationFunctionType
ALU = mybir.AluOpType
BF = ml_dtypes.bfloat16

D = 4096
SEQ = 8192
CH = 512
NDT = 32
NET = 32
NH = 32
NCORES = 8
SCALE = np.float32(1.0 / np.sqrt(128.0))
EXPB = np.float32(-20.0)
EPS = np.float32(1e-5)
ZERO = np.float32(0.0)
ONE = np.float32(1.0)
HALF = np.float32(0.5)
NEGHALF = np.float32(-0.5)
INV_D = np.float32(1.0 / D)

ENGS = ("sync", "gpsimd", "tensor", "vector", "scalar")


class Prog:

    def __init__(self, nc, es):
        self.nc = nc
        self.es = es
        self.q = {e: [] for e in ENGS}
        self._sems = {}

    def sem(self, name):
        if name not in self._sems:
            h = self.es.enter_context(self.nc.semaphore(name))
            self._sems[name] = [h, 0]
        return self._sems[name]

    def wait(self, eng, tok):
        if tok is None:
            return
        rec, v = tok
        self.q[eng].append(lambda E, h=rec[0], vv=v: E.wait_ge(h, vv))

    def op(self, eng, meth, *args, waits=(), inc=None, dma=False, **kw):
        for t in waits:
            self.wait(eng, t)
        if inc is not None:
            rec = self.sem(inc)
            amt = 16 if dma else 1
            rec[1] += amt
            tok = (rec, rec[1])
            self.q[eng].append(
                lambda E, m=meth, a=args, k=kw, h=rec[0], am=amt:
                    getattr(E, m)(*a, **k).then_inc(h, am))
            return tok
        self.q[eng].append(lambda E, m=meth, a=args, k=kw: getattr(E, m)(*a, **k))
        return None

    def cur(self, name):
        rec = self.sem(name)
        return (rec, rec[1])

    def snapshot(self):
        return ({k: v[1] for k, v in self._sems.items()},
                {e: len(q) for e, q in self.q.items()})

    def restore_counts(self, snap):
        for k, v in snap[0].items():
            self._sems[k][1] = v

    def branch_begin(self, engines, regname, val):
        for e in engines:
            self.q[e].append(lambda E, r=regname, v=val:
                             E.__dict__.setdefault("_ifg", []).append(
                                 _enter_if(E, E.__dict__[r], v)))

    def branch_end(self, engines):
        for e in engines:
            self.q[e].append(lambda E: E.__dict__["_ifg"].pop().__exit__(None, None, None))

    def sem_pad(self, eng, name, amount):
        if amount > 0:
            rec = self.sem(name)
            self.q[eng].append(lambda E, h=rec[0], a=amount: E.sem_inc(h, a))

    def load_reg(self, eng, regname, src_ap):
        def f(E, r=regname, ap=src_ap):
            reg = E.alloc_register(r)
            E.__dict__[r] = reg
            E.reg_load(reg, ap)
        self.q[eng].append(f)


    def run(self):
        with self.nc.Block() as blk:
            for eng in ENGS:
                ops = self.q[eng]
                if ops:
                    getattr(blk, eng)(lambda E, ops=ops: [f(E) for f in ops])


def _enter_if(E, reg, val):
    g = E.If_eq(reg, val)
    g.__enter__()
    return g



def alloc_common(nc, es, which):
    sb = {}

    def t(name, shape, dt):
        sb[name] = es.enter_context(nc.sbuf_tensor("sb_" + name, shape, dt))
        return sb[name]

    t("xs", [128, NDT, CH], F32)
    t("lnxt", [128, NDT, CH], BF16)
    t("slab", [128, 2, NDT, 128], BF16)
    t("big", [128, 16448], BF16)
    t("A_combT", [128, 2, NDT * 64], BF16)
    t("B_combT", [128, D], BF16)
    t("sq2", [128, 2, CH], BF16)
    t("tmp", [128, CH], F32)
    t("rowbf", [1, 2, CH], BF16)
    t("sel", [128, 96], BF16)
    t("ident4", [128, 32], BF16)
    t("rstdB", [128, CH], BF16)
    t("mrB", [128, CH], BF16)
    t("axs", [128, CH], BF16)
    t("identbf", [128, 128], BF16)
    t("onescol", [128, 1], BF16)
    t("gate2b", [1, 32], BF16)
    t("growb", [1, 128], BF16)
    t("gbc", [128, 32], F32)
    t("gcol", [128, 8], F32)
    t("btf", [128, NET], F32)
    t("btfh", [128, NET], F32)
    if which == "A":
        t("btq", [128, NET], F32)
        t("kbc", [128, 1], F32)
        t("vbB", [128, 128], BF16)
        t("vbr", [1, 128], F32)
        t("ksh", [128, 2, CH], BF16)
        t("vsh", [128, 2, 4, 128], BF16)
    else:
        t("QT", [128, NH, CH], BF16)
        t("expS", [128, 2, 2 * CH], BF16)
        t("tri", [128, 128], BF16)
        t("ttok", [128, 4, 128], BF16)
        t("rec", [128, 1], F32)
        t("bto", [128, NET], F32)

    sb["psA"] = es.enter_context(nc.psum_tensor("psA", [128, 3584], F32))
    sb["psT"] = es.enter_context(nc.psum_tensor("psT", [128, 1024], BF16))
    sb["ps_ax"] = sb["psA"][0:64, 2048:2560]
    sb["ps_misc"] = sb["psA"][:, 2048:2560]
    sb["ps_sum"] = sb["psA"][0:1, 2560:3072]
    sb["ps_sq"] = sb["psA"][0:1, 3072:3584]
    sb["lsa"] = sb["big"][:, 0:8192].rearrange("p (b n x) -> p b n x", b=2, n=4)
    sb["lsb"] = sb["big"][:, 8192:12288].rearrange("p (b x) -> p b x", b=2)
    sb["accA"] = sb["big"][:, 12288:16384].bitcast(F32).rearrange("p (g x) -> p g x", g=2)
    sb["accB"] = sb["big"][:, 12288:16384].bitcast(F32)
    return sb



def _slab6(w, net):
    w = np.asarray(w, dtype=np.float32)
    return np.ascontiguousarray(w.reshape(NDT, 128, net, 128).transpose(2, 1, 0, 3))


def _bias_tile(b):
    return np.ascontiguousarray(np.asarray(b, dtype=np.float32).reshape(32, 128).T)



def emit_consts(p, sb, which):
    gp = "gpsimd"
    p.op(gp, "memset", sb["identbf"][:], 0.0)
    p.op(gp, "affine_select", out=sb["identbf"][:], in_=sb["identbf"][:],
         compare_op=ALU.not_equal, fill=1.0, base=0, pattern=[[-1, 128]],
         channel_multiplier=1)
    if which == "B":
        p.op(gp, "memset", sb["tri"][:], 1.0)
        p.op(gp, "affine_select", out=sb["tri"][:], in_=sb["tri"][:],
             compare_op=ALU.is_ge, fill=0.0, base=0, pattern=[[1, 128]],
             channel_multiplier=-1)
    return p.op(gp, "memset", sb["onescol"][:], 1.0, inc="gp")


def emit_lora_build(p, sb, dr, t_gate, t_const, t_miscbank, t_ident4):
    lsa, lsb = sb["lsa"], sb["lsb"]
    accA = sb["accA"]
    load_gate = {}
    mm = p.op("tensor", "matmul", sb["ps_misc"][:, 0:32],
              sb["onescol"][0:1, 0:1].broadcast_to([1, 128]),
              sb["gate2b"][0:1, :], start=True, stop=True,
              waits=[t_gate, t_const, t_miscbank], inc="pe")
    t_gbc = p.op("vector", "tensor_copy", sb["gbc"][:], sb["ps_misc"][:, 0:32],
                 waits=[mm], inc="ve")

    fma_done = {}
    load_toks = {}
    for j in range(4):
        w = []
        if j >= 2:
            w.append(fma_done[j - 2])
        load_toks[j] = p.op("gpsimd", "dma_start", out=lsa[:, j % 2],
                            in_=dr["loraA7"][4 * j:4 * j + 4].rearrange("n p x -> p n x"),
                            waits=w, inc=f"dla{j % 2}", dma=True)
        t_last = None
        for g in range(2):
            for a in range(4):
                n = 4 * j + a
                w = [load_toks[j], t_gbc] if (g == 0 and a == 0) else []
                src = lsa[:, j % 2, a]
                gc = sb["gbc"][:, g * 16 + n:g * 16 + n + 1]
                if j == 0 and a == 0:
                    t_last = p.op("vector", "tensor_scalar", accA[:, g], src, gc,
                                  None, op0=ALU.mult, waits=w, inc="ve")
                else:
                    t_last = p.op("vector", "scalar_tensor_tensor", accA[:, g], src,
                                  gc, accA[:, g], op0=ALU.mult, op1=ALU.add,
                                  waits=w, inc="ve")
        fma_done[j] = t_last
    t_own = None
    for g in range(2):
        t_own = p.op("vector", "tensor_copy",
                     sb["A_combT"][:, g].rearrange("p (dt r) -> p dt r", r=64)[:, :, 0:32],
                     accA[:, g].rearrange("p (dt r) -> p dt r", r=32), inc="ve")
    tsh = p.op("gpsimd", "dma_start", out=lsa[:, 0, 0],
               in_=dr["loraAs7"], waits=[t_own], inc="dla0", dma=True)
    tA = None
    for g in range(2):
        tA = p.op("vector", "tensor_copy",
                  sb["A_combT"][:, g].rearrange("p (dt r) -> p dt r", r=64)[:, :, 32:64],
                  lsa[:, 0, 0].rearrange("p (dt r) -> p dt r", r=32),
                  waits=[tsh] if g == 0 else [], inc="ve")

    t_z = p.op("vector", "memset", sb["sel"][:, 32:64], 0.0)
    t_i4 = t_ident4[0]
    tB = None
    prev_col = t_gbc
    li = 0
    load_tok = {}
    sel_used = None
    for g in range(2):
        for eh in range(2):
            for j in range(4):
                w = []
                if li >= 1:
                    w.append(load_gate[li - 1])
                load_tok[li] = p.op("gpsimd", "dma_start", out=lsb[:, li % 2],
                            in_=dr["loraB7"][4 * j:4 * j + 4, :, 2048 * eh:2048 * eh + 2048]
                            .rearrange("n r e -> (n r) e"),
                            waits=w, inc=f"dlb{li % 2}", dma=True)
                if eh == 0:
                    ci = 4 * g + j
                    t_row = p.op("vector", "tensor_copy",
                                 sb["growb"][:].rearrange("o (a r) -> o a r", a=4),
                                 sb["gate2b"][0:1, 16 * g + 4 * j:16 * g + 4 * j + 4]
                                 .unsqueeze(2).broadcast_to([1, 4, 32]),
                                 waits=[prev_col], inc="ve")
                    mmp = p.op("tensor", "matmul", sb["ps_misc"][:, 40 + ci:41 + ci],
                               sb["growb"][:], sb["onescol"][0:1, 0:1],
                               start=True, stop=True,
                               waits=[t_row], inc="pe")
                    prev_col = p.op("vector", "tensor_copy",
                                    sb["gcol"][:, ci:ci + 1],
                                    sb["ps_misc"][:, 40 + ci:41 + ci],
                                    waits=[mmp], inc="ve")
                wsel = [prev_col, t_i4]
                if g == 1 and eh == 0 and j == 0:
                    t_zz = p.op("vector", "memset", sb["sel"][:, 0:32], 0.0,
                                waits=[sel_used])
                if sel_used is not None:
                    wsel.append(sel_used)
                selv = sb["sel"][:, 64 * g:64 * g + 32]
                t_sel = p.op("vector", "tensor_scalar", selv, sb["ident4"][:],
                             sb["gcol"][:, 4 * g + j:4 * g + j + 1], None,
                             op0=ALU.mult, waits=wsel, inc="ve")
                lhs = sb["sel"][:, 0:32] if g == 0 else sb["sel"][:, 0:96]
                t_mmf = None
                for ec in range(4):
                    w2 = [load_tok[li], t_sel] if ec == 0 else []
                    if j == 0 and ec == 0 and tB is not None:
                        w2.append(tB)
                    t_mmf = p.op("tensor", "matmul",
                                 sb["psA"][0:96 if g else 32, 512 * ec:512 * ec + 512],
                                 lhs, lsb[:, li % 2, 512 * ec:512 * ec + 512],
                                 start=(j == 0), stop=(j == 3),
                                 waits=w2,
                                 inc="pe")
                sel_used = t_mmf
                load_gate[li] = t_mmf
                li += 1
            for ec in range(4):
                pb = 64 * g
                tB = p.op("scalar", "activation",
                          out=sb["B_combT"][pb:pb + 32, 2048 * eh + 512 * ec:2048 * eh + 512 * ec + 512],
                          in_=sb["psA"][pb:pb + 32, 512 * ec:512 * ec + 512],
                          func=AF.Identity, bias=0.0, scale=ONE,
                          waits=[load_gate[li - 1]], inc="sc")
    tBs0 = p.op("gpsimd", "dma_start", out=sb["B_combT"][32:64, :], in_=dr["loraBs"],
                inc="dlb0", dma=True)
    tBs1 = p.op("gpsimd", "dma_start", out=sb["B_combT"][96:128, :], in_=dr["loraBs"],
                inc="dlb1", dma=True)
    return [tA, tB, tBs0, tBs1]


def emit_ln(p, sb, xs_ch, lnxt, first_waits):
    t0 = p.op("vector", "tensor_copy", lnxt[:], xs_ch, waits=first_waits, inc="ve")
    sq = sb["sq2"]
    for dt in range(NDT):
        p.op("tensor", "matmul", sb["ps_sum"], sb["onescol"][:, 0:1], lnxt[:, dt],
             start=(dt == 0), stop=(dt == NDT - 1), waits=[t0] if dt == 0 else [])
    t_pe = [None] * NDT
    for dt in range(NDT):
        w = [t0] if dt == 0 else []
        if dt >= 2:
            assert t_pe[dt - 2] is not None
            w.append(t_pe[dt - 2])
        t_sq = p.op("vector", "tensor_mul", sq[:, dt % 2], lnxt[:, dt], lnxt[:, dt],
                    waits=w, inc="ve")
        t_pe[dt] = p.op("tensor", "matmul", sb["ps_sq"], sb["onescol"][:, 0:1],
                        sq[:, dt % 2], start=(dt == 0), stop=(dt == NDT - 1),
                        waits=[t_sq], inc="pe")
    t_mm = t_pe[NDT - 1]
    mean = sb["psT"][0:1, :].bitcast(F32)
    rv = sb["ps_sq"]
    p.op("vector", "tensor_scalar", mean, sb["ps_sum"], float(INV_D), None,
         op0=ALU.mult, waits=[t_mm])
    t_ex2 = p.op("vector", "tensor_scalar", rv, sb["ps_sq"], float(INV_D), None,
                 op0=ALU.mult, inc="ve")
    t_msq = p.op("scalar", "activation", out=sb["rowbf"][:, 0], in_=mean,
                 func=AF.Square, bias=ZERO, scale=ONE, waits=[t_ex2], inc="sc")
    t_var = p.op("vector", "tensor_sub", rv, rv, sb["rowbf"][:, 0],
                 waits=[t_msq], inc="ve")
    p.op("scalar", "activation", out=rv, in_=rv, func=AF.Ln, bias=EPS, scale=ONE,
         waits=[t_var])
    t_rstd = p.op("scalar", "activation", out=rv, in_=rv, func=AF.Exp,
                  bias=ZERO, scale=NEGHALF, inc="sc")
    p.op("vector", "tensor_copy", sb["rowbf"][:, 0], rv, waits=[t_rstd])
    t_rows = p.op("vector", "tensor_mul", sb["rowbf"][:, 1], mean, sb["rowbf"][:, 0],
                  inc="ve")
    m1 = p.op("tensor", "matmul", sb["ps_misc"][:, 0:CH],
              sb["onescol"][0:1, 0:1].broadcast_to([1, 128]), sb["rowbf"][0:1, 0],
              start=True, stop=True, waits=[t_rows], inc="pe")
    t_r = p.op("vector", "tensor_copy", sb["rstdB"][:], sb["ps_misc"][:, 0:CH],
               waits=[m1], inc="ve")
    m2 = p.op("tensor", "matmul", sb["ps_misc"][:, 0:CH],
              sb["onescol"][0:1, 0:1].broadcast_to([1, 128]), sb["rowbf"][0:1, 1],
              start=True, stop=True, waits=[t_r], inc="pe")
    t_m = p.op("vector", "tensor_copy", sb["mrB"][:], sb["ps_misc"][:, 0:CH],
               waits=[m2], inc="ve")
    t_out = None
    for dt in range(NDT):
        p.op("vector", "tensor_mul", lnxt[:, dt], lnxt[:, dt], sb["rstdB"][:],
             waits=[t_r] if dt == 0 else [])
        t_out = p.op("vector", "tensor_sub", lnxt[:, dt], lnxt[:, dt], sb["mrB"][:],
                     waits=[t_m] if dt == 0 else [], inc="ve")
    return t_out


class SlabStream:
    def __init__(self, p, sb):
        self.p, self.slab = p, sb["slab"]
        self.i = 0
        self.done = []
        self.wb = {}

    def load(self, dram_et_view, extra_waits=(), eng="gpsimd", writeback=None):
        buf = self.i % 2
        w = list(extra_waits)
        if self.i >= 2:
            w.append(self.done[self.i - 2])
        if self.wb.get(buf) is not None:
            w.append(self.wb.pop(buf))
        tok = self.p.op(eng, "dma_start",
                        out=self.slab[:, buf].rearrange("p dt c -> p (dt c)"),
                        in_=dram_et_view.rearrange("p dt c -> p (dt c)"),
                        waits=w, inc=f"dk{buf}", dma=True)
        if writeback is not None:
            self.wb[buf] = self.p.op("sync", "dma_start",
                                     out=writeback.rearrange("p dt c -> p (dt c)"),
                                     in_=self.slab[:, buf]
                                     .rearrange("p dt c -> p (dt c)"),
                                     waits=[tok], inc=f"wb{buf}", dma=True)
        self.i += 1
        return buf, tok

    def take_wb(self):
        toks = [t for t in self.wb.values() if t is not None]
        self.wb = {}
        return toks

    def note_done(self, tok):
        self.done.append(tok)

    def gate_next(self, tok, n=2):
        for k in range(1, n + 1):
            if len(self.done) >= k:
                self.done[-k] = tok


def emit_proj(p, sb, stream, wdram, rhs_src, out_writer, lora=None, first_waits=(),
              eng="gpsimd", wb=None, lw=()):
    bank_free = [None] * 4
    t_ax = None
    if lora is not None:
        A_view, B_rows, g = lora
        pb = 64 * g
        ps_ax = sb["psA"][pb:pb + 64, 2048:2560]
        axs = sb["axs"][pb:pb + 64, :]
        mm = None
        for dt in range(NDT):
            w = list(first_waits) if dt == 0 else []
            mm = p.op("tensor", "matmul", ps_ax,
                      A_view[:, 64 * dt:64 * dt + 64], rhs_src[:, dt],
                      start=(dt == 0), stop=(dt == NDT - 1), waits=w,
                      inc="pe" if dt == NDT - 1 else None)
        t_ax = p.op("scalar", "activation", out=axs, in_=ps_ax,
                    func=AF.Identity, bias=0.0, scale=ONE, waits=[mm], inc="sc")
    for et in range(NET):
        buf, ltok = stream.load(wdram[et], extra_waits=(list(lw) if et < 2 else ()),
                                eng=eng, writeback=(wb[et] if wb is not None else None))
        bank = et % 4
        ps = sb["psA"][:, 512 * bank:512 * bank + 512]
        w = [ltok]
        if bank_free[bank] is not None:
            w.append(bank_free[bank])
        if et == 0 and lora is None:
            w += list(first_waits)
        tok = None
        for dt in range(NDT):
            stop = (dt == NDT - 1 and lora is None)
            tok = p.op("tensor", "matmul", ps, stream.slab[:, buf, dt], rhs_src[:, dt],
                       start=(dt == 0), stop=stop, waits=w if dt == 0 else [],
                       inc="pe" if stop else None)
        if lora is not None:
            tok = p.op("tensor", "matmul", ps, B_rows[:, 128 * et:128 * et + 128],
                       axs, start=False, stop=True,
                       waits=[t_ax] if et == 0 else [], inc="pe")
        stream.note_done(tok)
        bank_free[bank] = out_writer(et, ps, tok)
    if lora is not None and sb.get("_dbg_ax_out") is not None:
        p.op("sync", "dma_start", out=sb["_dbg_ax_out"], in_=sb["axs"][:],
             waits=[t_ax], inc="dst", dma=True)
        sb["_dbg_ax_out"] = None
    return bank_free


def make_residual_writer(p, sb, xs_t, btile_scaled, scale, store_to=None):
    state = {"prev": None}

    def writer(et, ps, tok_pe):
        w = [tok_pe]
        if state["prev"] is not None:
            w.append(state["prev"])
        t_sc = p.op("scalar", "activation", out=sb["tmp"][:], in_=ps,
                    func=AF.Identity, bias=btile_scaled[:, et:et + 1], scale=scale,
                    waits=w, inc="sc")
        t_add = p.op("vector", "tensor_add", xs_t[:, et], xs_t[:, et],
                     sb["tmp"][:], waits=[t_sc], inc="ve")
        state["prev"] = t_add
        if store_to is not None:
            p.op("sync", "dma_start", out=store_to[:, et, :], in_=xs_t[:, et],
                 waits=[t_add], inc="dst", dma=True)
        return t_sc
    return writer


def make_bf16_writer(p, sb, out_et, btile):
    def writer(et, ps, tok_pe):
        return p.op("scalar", "activation", out=out_et(et), in_=ps,
                    func=AF.Identity, bias=btile[:, et:et + 1], scale=ONE,
                    waits=[tok_pe], inc="sc")
    return writer


def emit_kv(p, sb, lnxt, ch, first_waits):
    ps_k = sb["psA"][:, 0:512]
    tok = None
    for dt in range(NDT):
        tok = p.op("tensor", "matmul", ps_k, sb["slab"][:, 0, dt], lnxt[:, dt],
                   start=(dt == 0), stop=(dt == NDT - 1),
                   waits=first_waits if dt == 0 else [],
                   inc="pe" if dt == NDT - 1 else None)
    t_k = p.op("scalar", "activation", out=sb["ksh"][:, ch], in_=ps_k,
               func=AF.Identity, bias=sb["kbc"][:, 0:1], scale=ONE,
               waits=[tok], inc="sc")
    t_v = []
    for qt in range(4):
        ps_v = sb["psA"][:, 512 + 512 * (qt % 2):512 + 512 * (qt % 2) + 128]
        w = []
        if qt >= 2:
            w.append(t_v[qt - 2])
        tok = None
        for dt in range(NDT):
            tok = p.op("tensor", "matmul", ps_v, lnxt[:, dt, 128 * qt:128 * qt + 128],
                       sb["slab"][:, 1, dt], start=(dt == 0), stop=(dt == NDT - 1),
                       waits=w if dt == 0 else [], inc="pe" if dt == NDT - 1 else None)
        t_v.append(p.op("vector", "scalar_tensor_tensor", sb["vsh"][:, ch, qt], ps_v,
                        1.0, sb["vbB"][:], op0=ALU.mult, op1=ALU.add,
                        waits=[tok], inc="ve"))
    return t_k, t_v


def emit_attention(p, sb, npos, KT, V1, QT, attnT, t_ready):
    t_div_prev = [None] * 4
    t_trcopy = [None] * 4
    t_copy = None
    t_div_last = None
    t_exp_pos = [None] * 3
    t_avl = [None, None]
    npair = npos // 2
    pend_tr = []

    def flush_tr(k):
        nonlocal t_copy
        for _ in range(k):
            qt, hh, t_div = pend_tr.pop(0)
            w = [t_div]
            if t_copy is not None:
                w.append(t_copy)
            t_tr = p.op("tensor", "transpose", sb["psT"][:, 0:128],
                        sb["ttok"][:, qt], sb["identbf"][:], waits=w, inc="pe")
            t_copy = p.op("vector", "tensor_copy",
                          attnT[:, hh, 128 * qt:128 * qt + 128],
                          sb["psT"][:, 0:128], waits=[t_tr], inc="ve")
            t_trcopy[qt] = t_copy
        return t_copy

    SB3 = (0, 512, 3072)

    def emit_S(h, pos, t_extra):
        jrel = 3 - pos if pos < 4 else -1
        qlo = 128 * jrel if jrel > 0 else 0
        N = 512 - qlo
        ps_S = sb["psA"][:, SB3[pos % 3]:SB3[pos % 3] + N]
        w = list(t_extra)
        if t_exp_pos[pos % 3] is not None:
            w.append(t_exp_pos[pos % 3])
        return p.op("tensor", "matmul", ps_S, KT[:, 128 * pos:128 * pos + 128],
                    QT[:, h, qlo:512], start=True, stop=True, waits=w, inc="pe")

    def emit_exp(h, i, t_s01):
        buf = i % 2
        masks = {}
        t_es = [None, None]
        for k in range(2):
            pos = 2 * i + k
            jrel = 3 - pos if pos < 4 else -1
            qlo = 128 * jrel if jrel > 0 else 0
            N = 512 - qlo
            we = [t_s01[k]]
            if k == 0 and t_avl[buf] is not None:
                we.append(t_avl[buf])
            t_e = p.op("scalar", "activation",
                       out=sb["expS"][:, buf, 512 * k:512 * k + N],
                       in_=sb["psA"][:, SB3[pos % 3]:SB3[pos % 3] + N],
                       func=AF.Exp, bias=EXPB, scale=SCALE, waits=we, inc="sc")
            t_exp_pos[pos % 3] = t_e
            t_es[k] = t_e
            if pos < 4:
                masks[pos] = p.op("vector", "tensor_mul",
                                  sb["expS"][:, buf, 512 * k:512 * k + 128],
                                  sb["expS"][:, buf, 512 * k:512 * k + 128],
                                  sb["tri"][:], waits=[t_e], inc="ve")
        return t_es, masks

    def emit_AV(h, i, t_es, masks, t_stop_out):
        buf = i % 2
        t_av = None
        for k in range(2):
            pos = 2 * i + k
            jrel = 3 - pos if pos < 4 else -1
            qlo = 128 * jrel if jrel > 0 else 0
            qt0 = jrel if jrel > 0 else 0
            for qt in range(qt0, 4):
                start = (pos == 3 - qt)
                stop = (pos == npos - 1)
                ps_O = sb["psA"][:, 1024 + 512 * qt:1024 + 512 * qt + 129]
                w = []
                if qt == qt0:
                    w.append(masks.get(pos, t_es[k]))
                if start and t_div_prev[qt] is not None:
                    w.append(t_div_prev[qt])
                t_av = p.op("tensor", "matmul", ps_O,
                            sb["expS"][:, buf, 512 * k + 128 * qt - qlo:512 * k + 128 * qt - qlo + 128],
                            V1[:, pos, :], start=start, stop=stop, waits=w, inc="pe")
                if stop:
                    t_stop_out[qt] = t_av
        t_avl[buf] = t_av

    for h in range(NH):
        t_av_stop = [None] * 4
        pend = None
        for i in range(npair):
            entry = list(t_ready) if (h == 0 and i <= 1) else []
            ts0 = emit_S(h, 2 * i, entry)
            ts1 = emit_S(h, 2 * i + 1, entry if i == 1 else [])
            t_es, masks = emit_exp(h, i, [ts0, ts1])
            if pend is not None:
                emit_AV(h, pend[0], pend[1], pend[2], t_av_stop)
            if pend_tr:
                flush_tr(1)
            pend = (i, t_es, masks)
        emit_AV(h, pend[0], pend[1], pend[2], t_av_stop)
        flush_tr(len(pend_tr))
        for qt in range(4):
            ps_O = sb["psA"][:, 1024 + 512 * qt:1024 + 512 * qt + 129]
            w = [t_av_stop[qt]]
            if t_div_last is not None:
                w.append(t_div_last)
            t_rc = p.op("vector", "reciprocal", sb["rec"][:, 0:1], ps_O[:, 128:129],
                        waits=w, inc="ve")
            w = [t_rc]
            if t_trcopy[qt] is not None:
                w.append(t_trcopy[qt])
            t_div = p.op("scalar", "activation", out=sb["ttok"][:, qt],
                         in_=ps_O[:, 0:128], func=AF.Identity, bias=0.0,
                         scale=sb["rec"][:, 0:1], waits=w, inc="sc")
            t_div_prev[qt] = t_div
            t_div_last = t_div
            pend_tr.append((qt, h, t_div))
        if h == NH - 1:
            t_copy = flush_tr(len(pend_tr))
    return t_copy



def build_A(debug=False):
    nc = bass.Bass("TRN2", target_bir_lowering=False)
    dr = {}
    if debug:
        dr["dbg_ln1"] = nc.dram_tensor("dbg_ln1", [128, NDT * CH], BF16, kind="ExternalOutput").ap()
        dr["dbg_A"] = nc.dram_tensor("dbg_A", [128, 2 * NDT * 64], BF16, kind="ExternalOutput").ap()
        dr["dbg_B"] = nc.dram_tensor("dbg_B", [128, D], BF16, kind="ExternalOutput").ap()
        dr["dbg_ax"] = nc.dram_tensor("dbg_ax", [128, CH], BF16, kind="ExternalOutput").ap()
        dr["dbg_rstdB"] = nc.dram_tensor("dbg_rstdB", [128, CH], BF16, kind="ExternalOutput").ap()
        dr["dbg_mrB"] = nc.dram_tensor("dbg_mrB", [128, CH], BF16, kind="ExternalOutput").ap()
    dr["xT"] = nc.dram_tensor("xT", [D, 2 * CH], F32, kind="ExternalInput").ap()
    dr["kernel6"] = nc.dram_tensor("kernel6", [NET, 128, NDT, 128], F32, kind="ExternalInput").ap()
    dr["wk6"] = nc.dram_tensor("wk6", [128, NDT, 128], F32, kind="ExternalInput").ap()
    dr["wv6"] = nc.dram_tensor("wv6", [128, NDT, 128], F32, kind="ExternalInput").ap()
    dr["btf"] = nc.dram_tensor("btf", [128, NET], F32, kind="ExternalInput").ap()
    dr["btq"] = nc.dram_tensor("btq", [128, NET], F32, kind="ExternalInput").ap()
    dr["wq6"] = nc.dram_tensor("wq6", [NET, 128, NDT, 128], F32, kind="ExternalInput").ap()
    dr["kernelb"] = nc.dram_tensor("kernelb", [NET, 128, NDT, 128], BF16, kind="ExternalOutput").ap()
    dr["wqb"] = nc.dram_tensor("wqb", [NET, 128, NDT, 128], BF16, kind="ExternalOutput").ap()
    dr["kbc"] = nc.dram_tensor("kbc", [128, 1], F32, kind="ExternalInput").ap()
    dr["vbr"] = nc.dram_tensor("vbr", [1, 128], F32, kind="ExternalInput").ap()
    dr["gate2"] = nc.dram_tensor("gate2", [1, 32], F32, kind="ExternalInput").ap()
    dr["ident4"] = nc.dram_tensor("ident4", [128, 32], F32, kind="ExternalInput").ap()
    dr["loraA7"] = nc.dram_tensor("loraA7", [16, 128, 1024], F32, kind="ExternalInput").ap()
    dr["loraAs7"] = nc.dram_tensor("loraAs7", [128, 1024], F32, kind="ExternalInput").ap()
    dr["loraB7"] = nc.dram_tensor("loraB7", [16, 32, D], F32, kind="ExternalInput").ap()
    dr["loraBs"] = nc.dram_tensor("loraBs", [32, D], F32, kind="ExternalInput").ap()
    dr["x1T"] = nc.dram_tensor("x1T", [D, 2 * CH], F32, kind="ExternalOutput").ap()
    dr["qT"] = nc.dram_tensor("qT", [128, 2, NET, CH], BF16, kind="ExternalOutput").ap()
    dr["kshard"] = nc.dram_tensor("kshard", [128, 2 * CH], BF16, kind="ExternalOutput").ap()
    dr["vshard"] = nc.dram_tensor("vshard", [2 * CH, 128], BF16, kind="ExternalOutput").ap()

    with contextlib.ExitStack() as es:
        sb = alloc_common(nc, es, "A")
        p = Prog(nc, es)
        t_const = emit_consts(p, sb, "A")
        t_gate = p.op("gpsimd", "dma_start", out=sb["gate2b"][:], in_=dr["gate2"],
                      inc="dmg", dma=True)
        t_i4 = p.op("gpsimd", "dma_start", out=sb["ident4"][:], in_=dr["ident4"],
                    inc="dmg", dma=True)
        t_gate = t_i4
        p.op("sync", "dma_start", out=sb["btf"][:], in_=dr["btf"], inc="dms", dma=True)
        p.op("sync", "dma_start", out=sb["btq"][:], in_=dr["btq"], inc="dms", dma=True)
        p.op("sync", "dma_start", out=sb["kbc"][:], in_=dr["kbc"], inc="dms", dma=True)
        p.op("sync", "dma_start", out=sb["vbr"][:], in_=dr["vbr"], inc="dms", dma=True)
        t_misc = p.cur("dms")
        t_btfh = p.op("vector", "tensor_scalar", sb["btfh"][:], sb["btf"][:], 0.5,
                      None, op0=ALU.mult, waits=[t_misc], inc="ve")
        t_vrow = p.op("vector", "tensor_copy", sb["rowbf"][0:1, 0, 0:128],
                      sb["vbr"][0:1, :], waits=[t_misc], inc="ve")
        mmv = p.op("tensor", "matmul", sb["ps_misc"][:, 100:228],
                   sb["onescol"][0:1, 0:1].broadcast_to([1, 128]),
                   sb["rowbf"][0:1, 0, 0:128],
                   start=True, stop=True, waits=[t_vrow, t_const], inc="pe")
        t_vbB = p.op("vector", "tensor_copy", sb["vbB"][:], sb["ps_misc"][:, 100:228],
                     waits=[mmv], inc="ve")
        t_lora = emit_lora_build(p, sb, dr, t_gate, t_const, t_vbB, [t_i4])

        stream = SlabStream(p, sb)
        xr = dr["xT"].rearrange("(dt p) t -> p dt t", p=128)
        x1r = dr["x1T"].rearrange("(dt p) t -> p dt t", p=128)
        t_store_prev = None
        t_prev_kv_pe = None
        for ci in range(2):
            ch = ci
            w = [] if ci == 0 else [t_store_prev]
            t_x = p.op("sync", "dma_start", out=sb["xs"][:],
                       in_=xr[:, :, CH * ch:CH * ch + CH], waits=w,
                       inc=f"dx{ch}", dma=True)
            lnw = [t_x]
            if t_prev_kv_pe is not None:
                lnw += [t_prev_kv_pe, p.cur("sc"), p.cur("ve")]
            t_ln1 = emit_ln(p, sb, sb["xs"][:], sb["lnxt"], lnw)
            if debug and ci == 0:
                p.op("sync", "dma_start", out=dr["dbg_rstdB"], in_=sb["rstdB"][:],
                     waits=[t_ln1], inc="dst", dma=True)
                p.op("sync", "dma_start", out=dr["dbg_mrB"], in_=sb["mrB"][:],
                     waits=[t_ln1], inc="dst", dma=True)
                p.op("sync", "dma_start", out=dr["dbg_ln1"],
                     in_=sb["lnxt"][:].rearrange("p a b -> p (a b)"),
                     waits=[t_ln1], inc="dst", dma=True)
                p.op("sync", "dma_start", out=dr["dbg_A"],
                     in_=sb["A_combT"][:].rearrange("p a b -> p (a b)"),
                     waits=t_lora, inc="dst", dma=True)
                p.op("sync", "dma_start", out=dr["dbg_B"], in_=sb["B_combT"][:],
                     waits=t_lora, inc="dst", dma=True)
            wr = make_residual_writer(p, sb, sb["xs"], sb["btfh"], HALF)
            lora = (sb["A_combT"][:, ci], sb["B_combT"][64 * ci:64 * ci + 64, :], ci)
            if debug and ci == 0:
                sb["_dbg_ax_out"] = dr["dbg_ax"]
            fw = [t_ln1, t_btfh] + (t_lora if ci == 0 else [])
            if ci == 0:
                emit_proj(p, sb, stream, dr["kernel6"], sb["lnxt"], wr, lora=lora,
                          first_waits=fw, wb=dr["kernelb"])
            else:
                emit_proj(p, sb, stream, dr["kernelb"], sb["lnxt"], wr, lora=lora,
                          first_waits=fw, eng="sync",
                          lw=[p.cur("wb0"), p.cur("wb1")])
            t_ffn_pe = p.cur("pe")
            t_ffn_ve = p.cur("ve")
            t_store_prev = p.op("sync", "dma_start",
                                out=x1r[:, :, CH * ch:CH * ch + CH],
                                in_=sb["xs"][:], waits=[t_ffn_ve], inc="dst", dma=True)
            t_ln2 = emit_ln(p, sb, sb["xs"][:], sb["lnxt"], [t_ffn_pe])
            qstage = sb["tmp"][:].bitcast(BF16)
            st_toks = [None, None]

            def qwriter(et, ps, tok_pe, ch=ch, qstage=qstage, st_toks=st_toks):
                w = [tok_pe]
                if st_toks[et % 2] is not None:
                    w.append(st_toks[et % 2])
                t_sc = p.op("scalar", "activation",
                            out=qstage[:, 512 * (et % 2):512 * (et % 2) + 512],
                            in_=ps, func=AF.Identity, bias=sb["btq"][:, et:et + 1],
                            scale=ONE, waits=w, inc="sc")
                st_toks[et % 2] = p.op("sync", "dma_start",
                                       out=dr["qT"][:, ch, et, :],
                                       in_=qstage[:, 512 * (et % 2):512 * (et % 2) + 512],
                                       waits=[t_sc], inc=f"dq{et % 2}", dma=True)
                return st_toks[et % 2]
            if ci == 0:
                emit_proj(p, sb, stream, dr["wq6"], sb["lnxt"], qwriter,
                          first_waits=[t_ln2], wb=dr["wqb"])
            else:
                emit_proj(p, sb, stream, dr["wqb"], sb["lnxt"], qwriter,
                          first_waits=[t_ln2], eng="sync",
                          lw=[p.cur("wb0"), p.cur("wb1")])
            wbt = stream.take_wb()
            p.op("gpsimd", "dma_start",
                 out=sb["slab"][:, 0].rearrange("p dt c -> p (dt c)"),
                 in_=dr["wk6"].rearrange("p dt c -> p (dt c)"),
                 waits=[stream.done[-1]] + wbt, inc="dkv", dma=True)
            p.op("gpsimd", "dma_start",
                 out=sb["slab"][:, 1].rearrange("p dt c -> p (dt c)"),
                 in_=dr["wv6"].rearrange("p dt c -> p (dt c)"),
                 waits=[stream.done[-1]] + wbt, inc="dkv", dma=True)
            t_wkv = p.cur("dkv")
            emit_kv(p, sb, sb["lnxt"], ch, [t_ln2, t_wkv, t_vbB, t_misc, p.cur("dq0"), p.cur("dq1")])
            t_prev_kv_pe = p.cur("pe")
            stream.gate_next(t_prev_kv_pe, 2)
        t_sk = p.op("sync", "dma_start", out=dr["kshard"],
                    in_=sb["ksh"][:].rearrange("p ch t -> p (ch t)"),
                    waits=[p.cur("sc")], inc="dst", dma=True)
        t_sv = p.op("sync", "dma_start",
                    out=dr["vshard"].rearrange("(ch qt p) e -> p ch qt e", p=128, qt=4),
                    in_=sb["vsh"][:], waits=[p.cur("ve")], inc="dst", dma=True)
        p.op("sync", "nop", waits=[p.cur("dst")])
        p.run()
    return nc


def build_B():
    nc = bass.Bass("TRN2", target_bir_lowering=False)
    dr = {}
    dr["x1T"] = nc.dram_tensor("x1T", [D, 2 * CH], F32, kind="ExternalInput").ap()
    dr["qT"] = nc.dram_tensor("qT", [128, 2, NET, CH], BF16, kind="ExternalInput").ap()
    dr["wo6"] = nc.dram_tensor("wo6", [NET, 128, NDT, 128], F32, kind="ExternalInput").ap()
    dr["kernelb"] = nc.dram_tensor("kernelb", [NET, 128, NDT, 128], BF16, kind="ExternalInput").ap()
    dr["btf"] = nc.dram_tensor("btf", [128, NET], F32, kind="ExternalInput").ap()
    dr["bto"] = nc.dram_tensor("bto", [128, NET], F32, kind="ExternalInput").ap()
    dr["gate2"] = nc.dram_tensor("gate2", [1, 32], F32, kind="ExternalInput").ap()
    dr["ident4"] = nc.dram_tensor("ident4", [128, 32], F32, kind="ExternalInput").ap()
    dr["loraA7"] = nc.dram_tensor("loraA7", [16, 128, 1024], F32, kind="ExternalInput").ap()
    dr["loraAs7"] = nc.dram_tensor("loraAs7", [128, 1024], F32, kind="ExternalInput").ap()
    dr["loraB7"] = nc.dram_tensor("loraB7", [16, 32, D], F32, kind="ExternalInput").ap()
    dr["loraBs"] = nc.dram_tensor("loraBs", [32, D], F32, kind="ExternalInput").ap()
    dr["meta"] = nc.dram_tensor("meta", [1, 8], mybir.dt.int32, kind="ExternalInput").ap()
    dr["kA"] = nc.dram_tensor("kA", [128, 32 * 128], BF16, kind="ExternalInput").ap()
    dr["vA"] = nc.dram_tensor("vA", [32 * 128, 129], BF16, kind="ExternalInput").ap()
    dr["kB"] = nc.dram_tensor("kB", [128, 64 * 128], BF16, kind="ExternalInput").ap()
    dr["vB"] = nc.dram_tensor("vB", [64 * 128, 129], BF16, kind="ExternalInput").ap()
    dr["outT"] = nc.dram_tensor("outT", [D, 2 * CH], F32, kind="ExternalOutput").ap()

    with contextlib.ExitStack() as es:
        sb = alloc_common(nc, es, "B")
        sb["meta"] = es.enter_context(nc.sbuf_tensor("sb_meta", [1, 8], mybir.dt.int32))
        p = Prog(nc, es)
        t_const = emit_consts(p, sb, "B")
        t_gate = p.op("gpsimd", "dma_start", out=sb["gate2b"][:], in_=dr["gate2"],
                      inc="dmg", dma=True)
        t_meta = p.op("sync", "dma_start", out=sb["meta"][:], in_=dr["meta"],
                      inc="dms", dma=True)
        t_i4 = p.op("gpsimd", "dma_start", out=sb["ident4"][:], in_=dr["ident4"],
                    inc="dmg", dma=True)
        t_gate = t_i4
        for e in ("tensor", "scalar", "vector"):
            p.wait(e, t_meta)
            p.load_reg(e, "_attn_reg", sb["meta"][0:1, 0:1])
        p.op("sync", "dma_start", out=sb["btf"][:], in_=dr["btf"], inc="dms", dma=True)
        p.op("sync", "dma_start", out=sb["bto"][:], in_=dr["bto"], inc="dms", dma=True)
        t_misc = p.cur("dms")
        t_btfh = p.op("vector", "tensor_scalar", sb["btfh"][:], sb["btf"][:], 0.5,
                      None, op0=ALU.mult, waits=[t_misc], inc="ve")
        t_lora = emit_lora_build(p, sb, dr, t_gate, t_const, None, [t_i4])

        stream = SlabStream(p, sb)
        x1r = dr["x1T"].rearrange("(dt p) t -> p dt t", p=128)
        outr = dr["outT"].rearrange("(dt p) t -> p dt t", p=128)
        kv_dram = [(dr["kA"], dr["vA"], 32), (dr["kB"], dr["vB"], 64)]
        t_store_prev = None
        t_attn_prev = None
        for ci in range(2):
            ch = ci
            w = [t_misc] if ci == 0 else [t_store_prev]
            t_x = p.op("sync", "dma_start", out=sb["xs"][:],
                       in_=x1r[:, :, CH * ch:CH * ch + CH], waits=w,
                       inc=f"dx{ch}", dma=True)
            w = [t_lora[2]] if ci == 0 else [t_attn_prev]
            t_q_sc = p.op("scalar", "dma_start",
                          out=sb["QT"][:].rearrange("p h t -> p (h t)"),
                          in_=dr["qT"][:, ci].rearrange("p e t -> p (e t)"),
                          waits=w + [t_x], inc=f"dq{ci}", dma=True)
            kdr, vdr, npos = kv_dram[ci]
            KT = sb["big"][:, 0:npos * 128]
            V1 = sb["big"][:, 8192:8192 + npos * 129].rearrange("p (k c) -> p k c", c=129)
            wkv = list(t_lora) + [p.cur("ve")] if ci == 0 else [t_attn_prev]
            p.op("scalar", "dma_start", out=KT, in_=kdr, waits=wkv,
                 inc=f"dkt{ci}", dma=True)
            p.op("scalar", "dma_start", out=V1,
                 in_=vdr.rearrange("(k p) c -> p k c", p=128),
                 inc=f"dkt{ci}", dma=True)
            t_kv = p.cur(f"dkt{ci}")
            engines = ("tensor", "scalar", "vector")
            t_entry = [t_q_sc, t_kv, p.cur("sc"), p.cur("ve")]
            snap = p.snapshot()
            deltas = []
            attn_sems = ("pe", "sc", "ve")
            for core in range(NCORES):
                p.restore_counts(snap)
                K = 4 * ((core if ci == 0 else 15 - core) + 1)
                p.branch_begin(engines, "_attn_reg", core)
                emit_attention(p, sb, K, KT, V1, sb["QT"], sb["lnxt"], t_entry)
                deltas.append({n: p._sems[n][1] - snap[0].get(n, 0)
                               for n in attn_sems})
                p.branch_end(engines)
            mx = {n: max(d[n] for d in deltas) for n in attn_sems}
            for core in range(NCORES):
                need = {n: mx[n] - deltas[core][n] for n in attn_sems}
                if any(need.values()):
                    p.branch_begin(engines, "_attn_reg", core)
                    p.sem_pad("tensor", "pe", need["pe"])
                    p.sem_pad("scalar", "sc", need["sc"])
                    p.sem_pad("vector", "ve", need["ve"])
                    p.branch_end(engines)
            p.restore_counts(snap)
            for n in attn_sems:
                p._sems[n][1] = snap[0].get(n, 0) + mx[n]
            t_attn_prev = (p._sems["ve"], p._sems["ve"][1])
            ow = make_residual_writer(p, sb, sb["xs"], sb["bto"], ONE)
            emit_proj(p, sb, stream, dr["wo6"], sb["lnxt"], ow,
                      first_waits=[t_attn_prev])
            t_o_pe = p.cur("pe")
            t_ln3 = emit_ln(p, sb, sb["xs"][:], sb["lnxt"], [t_o_pe, p.cur("sc")])
            fw2 = make_residual_writer(p, sb, sb["xs"], sb["btfh"], HALF,
                                       store_to=outr[:, :, CH * ch:CH * ch + CH])
            lora = (sb["A_combT"][:, ci], sb["B_combT"][64 * ci:64 * ci + 64, :], ci)
            emit_proj(p, sb, stream, dr["kernelb"], sb["lnxt"], fw2, lora=lora,
                      first_waits=[t_ln3, t_btfh], eng="sync")
            t_store_prev = p.cur("dst")
        p.op("sync", "nop", waits=[p.cur("dst")])
        p.run()
    return nc



def kernel(L0_attn_x, L0_attn_wq_w, L0_attn_wq_b, L0_attn_wk_w, L0_attn_wk_b,
           L0_attn_wv_w, L0_attn_wv_b, L0_attn_wo_w, L0_attn_wo_b, L0_ffn_gate,
           L0_ffn_kernel, L0_ffn_bias, L0_ffn_lora_A, L0_ffn_lora_B,
           L0_ffn_lora_A_shared, L0_ffn_lora_B_shared):
    f32 = np.float32
    x = np.asarray(L0_attn_x, f32)[0]
    xT = np.ascontiguousarray(x.T)
    kernel6 = _slab6(np.asarray(L0_ffn_kernel, f32), NET)
    wq6 = _slab6(np.asarray(L0_attn_wq_w, f32).T, NET)
    wo6 = _slab6(np.asarray(L0_attn_wo_w, f32).T, NET)
    wk6 = _slab6(np.asarray(L0_attn_wk_w, f32).T, 1)[0]
    wv6 = _slab6(np.asarray(L0_attn_wv_w, f32).T, 1)[0]
    btf = _bias_tile(L0_ffn_bias)
    btq = _bias_tile(L0_attn_wq_b)
    bto = _bias_tile(L0_attn_wo_b)
    kbc = np.ascontiguousarray(np.asarray(L0_attn_wk_b, f32).reshape(128, 1))
    vbr = np.ascontiguousarray(np.asarray(L0_attn_wv_b, f32).reshape(1, 128))
    gate = np.asarray(L0_ffn_gate, f32)
    lora_A = np.asarray(L0_ffn_lora_A, f32)
    lora_B = np.asarray(L0_ffn_lora_B, f32)
    lA_s = np.asarray(L0_ffn_lora_A_shared, f32)
    lB_s = np.asarray(L0_ffn_lora_B_shared, f32)
    loraA7 = np.ascontiguousarray(
        lora_A.transpose(0, 2, 1).reshape(16, 32, 128, 32).transpose(0, 2, 1, 3)
    ).reshape(16, 128, 1024)
    loraAs7 = np.ascontiguousarray(
        lA_s.T.reshape(32, 128, 32).transpose(1, 0, 2)).reshape(128, 1024)
    loraB7 = np.ascontiguousarray(lora_B.transpose(0, 2, 1))
    loraBs = np.ascontiguousarray(lB_s.T)

    chunks = [(i, 15 - i) for i in range(NCORES)]
    ident4 = np.ascontiguousarray(np.tile(np.eye(32, dtype=f32), (4, 1)))

    ncA = build_A()
    in_maps = []
    for cA, cB in chunks:
        xTc = np.ascontiguousarray(
            np.concatenate([xT[:, CH * cA:CH * cA + CH],
                            xT[:, CH * cB:CH * cB + CH]], axis=1))
        gate2 = np.ascontiguousarray(
            np.stack([gate[cA // 2], gate[cB // 2]]).reshape(1, 32))
        in_maps.append(dict(xT=xTc, kernel6=kernel6, wk6=wk6, wv6=wv6, btf=btf,
                            btq=btq, wq6=wq6, kbc=kbc, vbr=vbr, gate2=gate2,
                            ident4=ident4, loraA7=loraA7, loraAs7=loraAs7,
                            loraB7=loraB7, loraBs=loraBs))
    resA = run_bass_kernel_spmd(ncA, in_maps, core_ids=list(range(NCORES)))

    kfull = np.zeros((128, 16, CH), dtype=BF)
    vfull = np.zeros((SEQ, 128), dtype=BF)
    x1s = []
    for core, (cA, cB) in enumerate(chunks):
        r = resA.results[core]
        ks = np.asarray(r["kshard"]).reshape(128, 2, CH)
        vs = np.asarray(r["vshard"])
        kfull[:, cA, :] = ks[:, 0]
        kfull[:, cB, :] = ks[:, 1]
        vfull[CH * cA:CH * cA + CH] = vs[0:CH]
        vfull[CH * cB:CH * cB + CH] = vs[CH:2 * CH]
        x1s.append(np.asarray(r["x1T"]))
    kflat = np.ascontiguousarray(kfull.reshape(128, SEQ))

    def pack_slot(c, npos):
        K = 4 * (c + 1)
        kt = np.zeros((128, npos * 128), dtype=BF)
        v1 = np.zeros((npos * 128, 129), dtype=BF)
        for pp in range(K):
            g = K - 1 - pp
            kt[:, 128 * pp:128 * pp + 128] = kflat[:, 128 * g:128 * g + 128]
            v1[128 * pp:128 * pp + 128, 0:128] = vfull[128 * g:128 * g + 128]
            v1[128 * pp:128 * pp + 128, 128] = np.asarray(1.0, dtype=BF)
        return np.ascontiguousarray(kt), np.ascontiguousarray(v1)

    ncB = build_B()
    in_mapsB = []
    for core, (cA, cB) in enumerate(chunks):
        kAp, vAp = pack_slot(cA, 32)
        kBp, vBp = pack_slot(cB, 64)
        gate2 = np.ascontiguousarray(
            np.stack([gate[cA // 2], gate[cB // 2]]).reshape(1, 32))
        in_mapsB.append(dict(x1T=x1s[core], qT=np.asarray(resA.results[core]["qT"]),
                             wo6=wo6,
                             kernelb=np.asarray(resA.results[core]["kernelb"]),
                             meta=np.full((1, 8), core, dtype=np.int32),
                             btf=btf, bto=bto, gate2=gate2, ident4=ident4,
                             loraA7=loraA7, loraAs7=loraAs7, loraB7=loraB7,
                             loraBs=loraBs, kA=kAp, vA=vAp, kB=kBp, vB=vBp))
    resB = run_bass_kernel_spmd(ncB, in_mapsB, core_ids=list(range(NCORES)))

    out = np.empty((SEQ, D), dtype=np.float32)
    for core, (cA, cB) in enumerate(chunks):
        oT = np.asarray(resB.results[core]["outT"])
        out[CH * cA:CH * cA + CH] = oT[:, 0:CH].T
        out[CH * cB:CH * cB + CH] = oT[:, CH:2 * CH].T
    return out.reshape(1, SEQ, D)


# version 13 (final)
# speedup vs baseline: 1.3410x; 1.1086x over previous
import contextlib

import numpy as np
import ml_dtypes
import concourse.bass as bass
import concourse.mybir as mybir
from concourse.bass_utils import run_bass_kernel_spmd

F32 = mybir.dt.float32
BF16 = mybir.dt.bfloat16
AF = mybir.ActivationFunctionType
ALU = mybir.AluOpType
BF = ml_dtypes.bfloat16

D = 4096
SEQ = 8192
CH = 512
NDT = 32
NET = 32
NH = 32
NCORES = 8
SCALE = np.float32(1.0 / np.sqrt(128.0))
EXPB = np.float32(-20.0)
EPS = np.float32(1e-5)
ZERO = np.float32(0.0)
ONE = np.float32(1.0)
HALF = np.float32(0.5)
NEGHALF = np.float32(-0.5)
INV_D = np.float32(1.0 / D)

ENGS = ("sync", "gpsimd", "tensor", "vector", "scalar")


class Prog:

    def __init__(self, nc, es):
        self.nc = nc
        self.es = es
        self.q = {e: [] for e in ENGS}
        self._sems = {}

    def sem(self, name):
        if name not in self._sems:
            h = self.es.enter_context(self.nc.semaphore(name))
            self._sems[name] = [h, 0]
        return self._sems[name]

    def wait(self, eng, tok):
        if tok is None:
            return
        rec, v = tok
        self.q[eng].append(lambda E, h=rec[0], vv=v: E.wait_ge(h, vv))

    def op(self, eng, meth, *args, waits=(), inc=None, dma=False, **kw):
        for t in waits:
            self.wait(eng, t)
        if inc is not None:
            rec = self.sem(inc)
            amt = 16 if dma else 1
            rec[1] += amt
            tok = (rec, rec[1])
            self.q[eng].append(
                lambda E, m=meth, a=args, k=kw, h=rec[0], am=amt:
                    getattr(E, m)(*a, **k).then_inc(h, am))
            return tok
        self.q[eng].append(lambda E, m=meth, a=args, k=kw: getattr(E, m)(*a, **k))
        return None

    def cur(self, name):
        rec = self.sem(name)
        return (rec, rec[1])

    def snapshot(self):
        return ({k: v[1] for k, v in self._sems.items()},
                {e: len(q) for e, q in self.q.items()})

    def restore_counts(self, snap):
        for k, v in snap[0].items():
            self._sems[k][1] = v

    def branch_begin(self, engines, regname, val):
        for e in engines:
            self.q[e].append(lambda E, r=regname, v=val:
                             E.__dict__.setdefault("_ifg", []).append(
                                 _enter_if(E, E.__dict__[r], v)))

    def branch_end(self, engines):
        for e in engines:
            self.q[e].append(lambda E: E.__dict__["_ifg"].pop().__exit__(None, None, None))

    def sem_pad(self, eng, name, amount):
        if amount > 0:
            rec = self.sem(name)
            self.q[eng].append(lambda E, h=rec[0], a=amount: E.sem_inc(h, a))

    def load_reg(self, eng, regname, src_ap):
        def f(E, r=regname, ap=src_ap):
            reg = E.alloc_register(r)
            E.__dict__[r] = reg
            E.reg_load(reg, ap)
        self.q[eng].append(f)


    def run(self):
        with self.nc.Block() as blk:
            for eng in ENGS:
                ops = self.q[eng]
                if ops:
                    getattr(blk, eng)(lambda E, ops=ops: [f(E) for f in ops])


def _enter_if(E, reg, val):
    g = E.If_eq(reg, val)
    g.__enter__()
    return g



def alloc_common(nc, es, which):
    sb = {}

    def t(name, shape, dt):
        sb[name] = es.enter_context(nc.sbuf_tensor("sb_" + name, shape, dt))
        return sb[name]

    t("xs", [128, NDT, CH], F32)
    t("lnxt", [128, NDT, CH], BF16)
    t("slab", [128, 2, NDT, 128], BF16)
    t("big", [128, 16448], BF16)
    t("A_combT", [128, 2, NDT * 64], BF16)
    t("B_combT", [128, D], BF16)
    t("sq2", [128, 2, CH], BF16)
    t("tmp", [128, CH], F32)
    t("rowbf", [1, 2, CH], BF16)
    t("sel", [128, 96], BF16)
    t("ident4", [128, 32], BF16)
    t("rstdB", [128, CH], BF16)
    t("mrB", [128, CH], BF16)
    t("axs", [128, CH], BF16)
    t("identbf", [128, 128], BF16)
    t("onescol", [128, 1], BF16)
    t("gate2b", [1, 32], BF16)
    t("growb", [1, 128], BF16)
    t("gbc", [128, 32], F32)
    t("gcol", [128, 8], F32)
    t("btf", [128, NET], F32)
    t("btfh", [128, NET], F32)
    if which == "A":
        t("btq", [128, NET], F32)
        t("kbc", [128, 1], F32)
        t("vbB", [128, 128], BF16)
        t("vbr", [1, 128], F32)
        t("ksh", [128, 2, CH], BF16)
        t("vsh", [128, 2, 4, 128], BF16)
    else:
        t("QT", [128, NH, CH], BF16)
        t("expS", [128, 2, 2 * CH], BF16)
        t("tri", [128, 128], BF16)
        t("ttok", [128, 4, 128], BF16)
        t("rec", [128, 1], F32)
        t("bto", [128, NET], F32)

    sb["psA"] = es.enter_context(nc.psum_tensor("psA", [128, 3584], F32))
    sb["psT"] = es.enter_context(nc.psum_tensor("psT", [128, 1024], BF16))
    sb["ps_ax"] = sb["psA"][0:64, 2048:2560]
    sb["ps_misc"] = sb["psA"][:, 2048:2560]
    sb["ps_sum"] = sb["psA"][0:1, 2560:3072]
    sb["ps_sq"] = sb["psA"][0:1, 3072:3584]
    sb["lsa"] = sb["big"][:, 0:8192].rearrange("p (b n x) -> p b n x", b=2, n=4)
    sb["lsb"] = sb["big"][:, 8192:12288].rearrange("p (b x) -> p b x", b=2)
    sb["accA"] = sb["big"][:, 12288:16384].bitcast(F32).rearrange("p (g x) -> p g x", g=2)
    sb["accB"] = sb["big"][:, 12288:16384].bitcast(F32)
    return sb



def _slab6(w, net):
    w = np.asarray(w, dtype=np.float32)
    return np.ascontiguousarray(w.reshape(NDT, 128, net, 128).transpose(2, 1, 0, 3))


def _bias_tile(b):
    return np.ascontiguousarray(np.asarray(b, dtype=np.float32).reshape(32, 128).T)



def emit_consts(p, sb, which):
    gp = "gpsimd"
    p.op(gp, "memset", sb["identbf"][:], 0.0)
    p.op(gp, "affine_select", out=sb["identbf"][:], in_=sb["identbf"][:],
         compare_op=ALU.not_equal, fill=1.0, base=0, pattern=[[-1, 128]],
         channel_multiplier=1)
    if which == "B":
        p.op(gp, "memset", sb["tri"][:], 1.0)
        p.op(gp, "affine_select", out=sb["tri"][:], in_=sb["tri"][:],
             compare_op=ALU.is_ge, fill=0.0, base=0, pattern=[[1, 128]],
             channel_multiplier=-1)
    return p.op(gp, "memset", sb["onescol"][:], 1.0, inc="gp")


def emit_lora_build(p, sb, dr, t_gate, t_const, t_miscbank, t_ident4):
    lsa, lsb = sb["lsa"], sb["lsb"]
    accA = sb["accA"]
    load_gate = {}
    mm = p.op("tensor", "matmul", sb["ps_misc"][:, 0:32],
              sb["onescol"][0:1, 0:1].broadcast_to([1, 128]),
              sb["gate2b"][0:1, :], start=True, stop=True,
              waits=[t_gate, t_const, t_miscbank], inc="pe")
    t_gbc = p.op("vector", "tensor_copy", sb["gbc"][:], sb["ps_misc"][:, 0:32],
                 waits=[mm], inc="ve")

    fma_done = {}
    load_toks = {}
    for j in range(4):
        w = []
        if j >= 2:
            w.append(fma_done[j - 2])
        load_toks[j] = p.op("gpsimd", "dma_start", out=lsa[:, j % 2],
                            in_=dr["loraA7"][4 * j:4 * j + 4].rearrange("n p x -> p n x"),
                            waits=w, inc=f"dla{j % 2}", dma=True)
        t_last = None
        for g in range(2):
            for a in range(4):
                n = 4 * j + a
                w = [load_toks[j], t_gbc] if (g == 0 and a == 0) else []
                src = lsa[:, j % 2, a]
                gc = sb["gbc"][:, g * 16 + n:g * 16 + n + 1]
                if j == 0 and a == 0:
                    t_last = p.op("vector", "tensor_scalar", accA[:, g], src, gc,
                                  None, op0=ALU.mult, waits=w, inc="ve")
                else:
                    t_last = p.op("vector", "scalar_tensor_tensor", accA[:, g], src,
                                  gc, accA[:, g], op0=ALU.mult, op1=ALU.add,
                                  waits=w, inc="ve")
        fma_done[j] = t_last
    t_own = None
    for g in range(2):
        t_own = p.op("vector", "tensor_copy",
                     sb["A_combT"][:, g].rearrange("p (dt r) -> p dt r", r=64)[:, :, 0:32],
                     accA[:, g].rearrange("p (dt r) -> p dt r", r=32), inc="ve")
    tsh = p.op("gpsimd", "dma_start", out=lsa[:, 0, 0],
               in_=dr["loraAs7"], waits=[t_own], inc="dla0", dma=True)
    tA = None
    for g in range(2):
        tA = p.op("vector", "tensor_copy",
                  sb["A_combT"][:, g].rearrange("p (dt r) -> p dt r", r=64)[:, :, 32:64],
                  lsa[:, 0, 0].rearrange("p (dt r) -> p dt r", r=32),
                  waits=[tsh] if g == 0 else [], inc="ve")

    t_z = p.op("vector", "memset", sb["sel"][:, 32:64], 0.0)
    t_i4 = t_ident4[0]
    tB = None
    prev_col = t_gbc
    li = 0
    load_tok = {}
    sel_used = None
    for g in range(2):
        for eh in range(2):
            for j in range(4):
                w = []
                if li >= 1:
                    w.append(load_gate[li - 1])
                load_tok[li] = p.op("gpsimd", "dma_start", out=lsb[:, li % 2],
                            in_=dr["loraB7"][4 * j:4 * j + 4, :, 2048 * eh:2048 * eh + 2048]
                            .rearrange("n r e -> (n r) e"),
                            waits=w, inc=f"dlb{li % 2}", dma=True)
                if eh == 0:
                    ci = 4 * g + j
                    t_row = p.op("vector", "tensor_copy",
                                 sb["growb"][:].rearrange("o (a r) -> o a r", a=4),
                                 sb["gate2b"][0:1, 16 * g + 4 * j:16 * g + 4 * j + 4]
                                 .unsqueeze(2).broadcast_to([1, 4, 32]),
                                 waits=[prev_col], inc="ve")
                    mmp = p.op("tensor", "matmul", sb["ps_misc"][:, 40 + ci:41 + ci],
                               sb["growb"][:], sb["onescol"][0:1, 0:1],
                               start=True, stop=True,
                               waits=[t_row], inc="pe")
                    prev_col = p.op("vector", "tensor_copy",
                                    sb["gcol"][:, ci:ci + 1],
                                    sb["ps_misc"][:, 40 + ci:41 + ci],
                                    waits=[mmp], inc="ve")
                wsel = [prev_col, t_i4]
                if g == 1 and eh == 0 and j == 0:
                    t_zz = p.op("vector", "memset", sb["sel"][:, 0:32], 0.0,
                                waits=[sel_used])
                if sel_used is not None:
                    wsel.append(sel_used)
                selv = sb["sel"][:, 64 * g:64 * g + 32]
                t_sel = p.op("vector", "tensor_scalar", selv, sb["ident4"][:],
                             sb["gcol"][:, 4 * g + j:4 * g + j + 1], None,
                             op0=ALU.mult, waits=wsel, inc="ve")
                lhs = sb["sel"][:, 0:32] if g == 0 else sb["sel"][:, 0:96]
                t_mmf = None
                for ec in range(4):
                    w2 = [load_tok[li], t_sel] if ec == 0 else []
                    if j == 0 and ec == 0 and tB is not None:
                        w2.append(tB)
                    t_mmf = p.op("tensor", "matmul",
                                 sb["psA"][0:96 if g else 32, 512 * ec:512 * ec + 512],
                                 lhs, lsb[:, li % 2, 512 * ec:512 * ec + 512],
                                 start=(j == 0), stop=(j == 3),
                                 waits=w2,
                                 inc="pe")
                sel_used = t_mmf
                load_gate[li] = t_mmf
                li += 1
            for ec in range(4):
                pb = 64 * g
                tB = p.op("scalar", "activation",
                          out=sb["B_combT"][pb:pb + 32, 2048 * eh + 512 * ec:2048 * eh + 512 * ec + 512],
                          in_=sb["psA"][pb:pb + 32, 512 * ec:512 * ec + 512],
                          func=AF.Identity, bias=0.0, scale=ONE,
                          waits=[load_gate[li - 1]], inc="sc")
    tBs0 = p.op("gpsimd", "dma_start", out=sb["B_combT"][32:64, :], in_=dr["loraBs"],
                inc="dlb0", dma=True)
    tBs1 = p.op("gpsimd", "dma_start", out=sb["B_combT"][96:128, :], in_=dr["loraBs"],
                inc="dlb1", dma=True)
    return [tA, tB, tBs0, tBs1]


def emit_ln(p, sb, xs_ch, lnxt, first_waits):
    t0 = p.op("vector", "tensor_copy", lnxt[:], xs_ch, waits=first_waits, inc="ve")
    sq = sb["sq2"]
    for dt in range(NDT):
        p.op("tensor", "matmul", sb["ps_sum"], sb["onescol"][:, 0:1], lnxt[:, dt],
             start=(dt == 0), stop=(dt == NDT - 1), waits=[t0] if dt == 0 else [])
    t_pe = [None] * NDT
    for dt in range(NDT):
        w = [t0] if dt == 0 else []
        if dt >= 2:
            assert t_pe[dt - 2] is not None
            w.append(t_pe[dt - 2])
        t_sq = p.op("vector", "tensor_mul", sq[:, dt % 2], lnxt[:, dt], lnxt[:, dt],
                    waits=w, inc="ve")
        t_pe[dt] = p.op("tensor", "matmul", sb["ps_sq"], sb["onescol"][:, 0:1],
                        sq[:, dt % 2], start=(dt == 0), stop=(dt == NDT - 1),
                        waits=[t_sq], inc="pe")
    t_mm = t_pe[NDT - 1]
    mean = sb["psT"][0:1, :].bitcast(F32)
    rv = sb["ps_sq"]
    p.op("vector", "tensor_scalar", mean, sb["ps_sum"], float(INV_D), None,
         op0=ALU.mult, waits=[t_mm])
    t_ex2 = p.op("vector", "tensor_scalar", rv, sb["ps_sq"], float(INV_D), None,
                 op0=ALU.mult, inc="ve")
    t_msq = p.op("scalar", "activation", out=sb["rowbf"][:, 0], in_=mean,
                 func=AF.Square, bias=ZERO, scale=ONE, waits=[t_ex2], inc="sc")
    t_var = p.op("vector", "tensor_sub", rv, rv, sb["rowbf"][:, 0],
                 waits=[t_msq], inc="ve")
    p.op("scalar", "activation", out=rv, in_=rv, func=AF.Ln, bias=EPS, scale=ONE,
         waits=[t_var])
    t_rstd = p.op("scalar", "activation", out=rv, in_=rv, func=AF.Exp,
                  bias=ZERO, scale=NEGHALF, inc="sc")
    p.op("vector", "tensor_copy", sb["rowbf"][:, 0], rv, waits=[t_rstd])
    t_rows = p.op("vector", "tensor_mul", sb["rowbf"][:, 1], mean, sb["rowbf"][:, 0],
                  inc="ve")
    m1 = p.op("tensor", "matmul", sb["ps_misc"][:, 0:CH],
              sb["onescol"][0:1, 0:1].broadcast_to([1, 128]), sb["rowbf"][0:1, 0],
              start=True, stop=True, waits=[t_rows], inc="pe")
    t_r = p.op("vector", "tensor_copy", sb["rstdB"][:], sb["ps_misc"][:, 0:CH],
               waits=[m1], inc="ve")
    m2 = p.op("tensor", "matmul", sb["ps_misc"][:, 0:CH],
              sb["onescol"][0:1, 0:1].broadcast_to([1, 128]), sb["rowbf"][0:1, 1],
              start=True, stop=True, waits=[t_r], inc="pe")
    t_m = p.op("vector", "tensor_copy", sb["mrB"][:], sb["ps_misc"][:, 0:CH],
               waits=[m2], inc="ve")
    t_out = None
    for dt in range(NDT):
        p.op("vector", "tensor_mul", lnxt[:, dt], lnxt[:, dt], sb["rstdB"][:],
             waits=[t_r] if dt == 0 else [])
        t_out = p.op("vector", "tensor_sub", lnxt[:, dt], lnxt[:, dt], sb["mrB"][:],
                     waits=[t_m] if dt == 0 else [], inc="ve")
    return t_out


class SlabStream:
    def __init__(self, p, sb):
        self.p, self.slab = p, sb["slab"]
        self.i = 0
        self.done = []
        self.wb = {}

    def load(self, dram_et_view, extra_waits=(), eng="gpsimd", writeback=None):
        buf = self.i % 2
        w = list(extra_waits)
        if self.i >= 2:
            w.append(self.done[self.i - 2])
        if self.wb.get(buf) is not None:
            w.append(self.wb.pop(buf))
        tok = self.p.op(eng, "dma_start",
                        out=self.slab[:, buf].rearrange("p dt c -> p (dt c)"),
                        in_=dram_et_view.rearrange("p dt c -> p (dt c)"),
                        waits=w, inc=f"dk{buf}", dma=True)
        if writeback is not None:
            self.wb[buf] = self.p.op("sync", "dma_start",
                                     out=writeback.rearrange("p dt c -> p (dt c)"),
                                     in_=self.slab[:, buf]
                                     .rearrange("p dt c -> p (dt c)"),
                                     waits=[tok], inc=f"wb{buf}", dma=True)
        self.i += 1
        return buf, tok

    def take_wb(self):
        toks = [t for t in self.wb.values() if t is not None]
        self.wb = {}
        return toks

    def note_done(self, tok):
        self.done.append(tok)

    def gate_next(self, tok, n=2):
        for k in range(1, n + 1):
            if len(self.done) >= k:
                self.done[-k] = tok


def emit_proj(p, sb, stream, wdram, rhs_src, out_writer, lora=None, first_waits=(),
              eng="gpsimd", wb=None, lw=()):
    bank_free = [None] * 4
    t_ax = None
    if lora is not None:
        A_view, B_rows, g = lora
        pb = 64 * g
        ps_ax = sb["psA"][pb:pb + 64, 2048:2560]
        axs = sb["axs"][pb:pb + 64, :]
        mm = None
        for dt in range(NDT):
            w = list(first_waits) if dt == 0 else []
            mm = p.op("tensor", "matmul", ps_ax,
                      A_view[:, 64 * dt:64 * dt + 64], rhs_src[:, dt],
                      start=(dt == 0), stop=(dt == NDT - 1), waits=w,
                      inc="pe" if dt == NDT - 1 else None)
        t_ax = p.op("scalar", "activation", out=axs, in_=ps_ax,
                    func=AF.Identity, bias=0.0, scale=ONE, waits=[mm], inc="sc")
    for et in range(NET):
        buf, ltok = stream.load(wdram[et], extra_waits=(list(lw) if et < 2 else ()),
                                eng=eng, writeback=(wb[et] if wb is not None else None))
        bank = et % 4
        ps = sb["psA"][:, 512 * bank:512 * bank + 512]
        w = [ltok]
        if bank_free[bank] is not None:
            w.append(bank_free[bank])
        if et == 0 and lora is None:
            w += list(first_waits)
        tok = None
        for dt in range(NDT):
            stop = (dt == NDT - 1 and lora is None)
            tok = p.op("tensor", "matmul", ps, stream.slab[:, buf, dt], rhs_src[:, dt],
                       start=(dt == 0), stop=stop, waits=w if dt == 0 else [],
                       inc="pe" if stop else None)
        if lora is not None:
            tok = p.op("tensor", "matmul", ps, B_rows[:, 128 * et:128 * et + 128],
                       axs, start=False, stop=True,
                       waits=[t_ax] if et == 0 else [], inc="pe")
        stream.note_done(tok)
        bank_free[bank] = out_writer(et, ps, tok)
    if lora is not None and sb.get("_dbg_ax_out") is not None:
        p.op("sync", "dma_start", out=sb["_dbg_ax_out"], in_=sb["axs"][:],
             waits=[t_ax], inc="dst", dma=True)
        sb["_dbg_ax_out"] = None
    return bank_free


def make_residual_writer(p, sb, xs_t, btile_scaled, scale, store_to=None):
    state = {"prev": None}

    def writer(et, ps, tok_pe):
        w = [tok_pe]
        if state["prev"] is not None:
            w.append(state["prev"])
        t_sc = p.op("scalar", "activation", out=sb["tmp"][:], in_=ps,
                    func=AF.Identity, bias=btile_scaled[:, et:et + 1], scale=scale,
                    waits=w, inc="sc")
        t_add = p.op("vector", "tensor_add", xs_t[:, et], xs_t[:, et],
                     sb["tmp"][:], waits=[t_sc], inc="ve")
        state["prev"] = t_add
        if store_to is not None:
            p.op("sync", "dma_start", out=store_to[:, et, :], in_=xs_t[:, et],
                 waits=[t_add], inc="dst", dma=True)
        return t_sc
    return writer


def make_bf16_writer(p, sb, out_et, btile):
    def writer(et, ps, tok_pe):
        return p.op("scalar", "activation", out=out_et(et), in_=ps,
                    func=AF.Identity, bias=btile[:, et:et + 1], scale=ONE,
                    waits=[tok_pe], inc="sc")
    return writer


def emit_kv(p, sb, lnxt, ch, first_waits):
    ps_k = sb["psA"][:, 0:512]
    tok = None
    for dt in range(NDT):
        tok = p.op("tensor", "matmul", ps_k, sb["slab"][:, 0, dt], lnxt[:, dt],
                   start=(dt == 0), stop=(dt == NDT - 1),
                   waits=first_waits if dt == 0 else [],
                   inc="pe" if dt == NDT - 1 else None)
    t_k = p.op("scalar", "activation", out=sb["ksh"][:, ch], in_=ps_k,
               func=AF.Identity, bias=sb["kbc"][:, 0:1], scale=ONE,
               waits=[tok], inc="sc")
    t_v = []
    for qt in range(4):
        ps_v = sb["psA"][:, 512 + 512 * (qt % 2):512 + 512 * (qt % 2) + 128]
        w = []
        if qt >= 2:
            w.append(t_v[qt - 2])
        tok = None
        for dt in range(NDT):
            tok = p.op("tensor", "matmul", ps_v, lnxt[:, dt, 128 * qt:128 * qt + 128],
                       sb["slab"][:, 1, dt], start=(dt == 0), stop=(dt == NDT - 1),
                       waits=w if dt == 0 else [], inc="pe" if dt == NDT - 1 else None)
        t_v.append(p.op("vector", "scalar_tensor_tensor", sb["vsh"][:, ch, qt], ps_v,
                        1.0, sb["vbB"][:], op0=ALU.mult, op1=ALU.add,
                        waits=[tok], inc="ve"))
    return t_k, t_v


def emit_attention(p, sb, npos, KT, V1, QT, attnT, t_ready):
    t_div_prev = [None] * 4
    t_trcopy = [None] * 4
    t_copy = None
    t_div_last = None
    t_exp_pos = [None] * 3
    t_avl = [None, None]
    npair = npos // 2
    pend_tr = []

    def flush_tr(k):
        nonlocal t_copy
        for _ in range(k):
            qt, hh, t_div = pend_tr.pop(0)
            w = [t_div]
            if t_copy is not None:
                w.append(t_copy)
            t_tr = p.op("tensor", "transpose", sb["psT"][:, 0:128],
                        sb["ttok"][:, qt], sb["identbf"][:], waits=w, inc="pe")
            t_copy = p.op("vector", "tensor_copy",
                          attnT[:, hh, 128 * qt:128 * qt + 128],
                          sb["psT"][:, 0:128], waits=[t_tr], inc="ve")
            t_trcopy[qt] = t_copy
        return t_copy

    SB3 = (0, 512, 3072)

    def emit_S(h, pos, t_extra):
        jrel = 3 - pos if pos < 4 else -1
        qlo = 128 * jrel if jrel > 0 else 0
        N = 512 - qlo
        ps_S = sb["psA"][:, SB3[pos % 3]:SB3[pos % 3] + N]
        w = list(t_extra)
        if t_exp_pos[pos % 3] is not None:
            w.append(t_exp_pos[pos % 3])
        return p.op("tensor", "matmul", ps_S, KT[:, 128 * pos:128 * pos + 128],
                    QT[:, h, qlo:512], start=True, stop=True, waits=w, inc="pe")

    def emit_exp(h, i, t_s01):
        buf = i % 2
        masks = {}
        t_es = [None, None]
        for k in range(2):
            pos = 2 * i + k
            jrel = 3 - pos if pos < 4 else -1
            qlo = 128 * jrel if jrel > 0 else 0
            N = 512 - qlo
            we = [t_s01[k]]
            if k == 0 and t_avl[buf] is not None:
                we.append(t_avl[buf])
            t_e = p.op("scalar", "activation",
                       out=sb["expS"][:, buf, 512 * k:512 * k + N],
                       in_=sb["psA"][:, SB3[pos % 3]:SB3[pos % 3] + N],
                       func=AF.Exp, bias=EXPB, scale=SCALE, waits=we, inc="sc")
            t_exp_pos[pos % 3] = t_e
            t_es[k] = t_e
            if pos < 4:
                masks[pos] = p.op("vector", "tensor_mul",
                                  sb["expS"][:, buf, 512 * k:512 * k + 128],
                                  sb["expS"][:, buf, 512 * k:512 * k + 128],
                                  sb["tri"][:], waits=[t_e], inc="ve")
        return t_es, masks

    def emit_AV(h, i, t_es, masks, t_stop_out):
        buf = i % 2
        t_av = None
        for k in range(2):
            pos = 2 * i + k
            jrel = 3 - pos if pos < 4 else -1
            qlo = 128 * jrel if jrel > 0 else 0
            qt0 = jrel if jrel > 0 else 0
            for qt in range(qt0, 4):
                start = (pos == 3 - qt)
                stop = (pos == npos - 1)
                ps_O = sb["psA"][:, 1024 + 512 * qt:1024 + 512 * qt + 129]
                w = []
                if qt == qt0:
                    w.append(masks.get(pos, t_es[k]))
                if start and t_div_prev[qt] is not None:
                    w.append(t_div_prev[qt])
                t_av = p.op("tensor", "matmul", ps_O,
                            sb["expS"][:, buf, 512 * k + 128 * qt - qlo:512 * k + 128 * qt - qlo + 128],
                            V1[:, pos, :], start=start, stop=stop, waits=w, inc="pe")
                if stop:
                    t_stop_out[qt] = t_av
        t_avl[buf] = t_av

    for h in range(NH):
        t_av_stop = [None] * 4
        pend = None
        for i in range(npair):
            entry = list(t_ready) if (h == 0 and i <= 1) else []
            ts0 = emit_S(h, 2 * i, entry)
            ts1 = emit_S(h, 2 * i + 1, entry if i == 1 else [])
            t_es, masks = emit_exp(h, i, [ts0, ts1])
            if pend is not None:
                emit_AV(h, pend[0], pend[1], pend[2], t_av_stop)
            if pend_tr:
                flush_tr(1)
            pend = (i, t_es, masks)
        emit_AV(h, pend[0], pend[1], pend[2], t_av_stop)
        flush_tr(len(pend_tr))
        for qt in range(4):
            ps_O = sb["psA"][:, 1024 + 512 * qt:1024 + 512 * qt + 129]
            w = [t_av_stop[qt]]
            if t_div_last is not None:
                w.append(t_div_last)
            t_rc = p.op("vector", "reciprocal", sb["rec"][:, 0:1], ps_O[:, 128:129],
                        waits=w, inc="ve")
            w = [t_rc]
            if t_trcopy[qt] is not None:
                w.append(t_trcopy[qt])
            t_div = p.op("scalar", "activation", out=sb["ttok"][:, qt],
                         in_=ps_O[:, 0:128], func=AF.Identity, bias=0.0,
                         scale=sb["rec"][:, 0:1], waits=w, inc="sc")
            t_div_prev[qt] = t_div
            t_div_last = t_div
            pend_tr.append((qt, h, t_div))
        if h == NH - 1:
            t_copy = flush_tr(len(pend_tr))
    return t_copy



def build_A(debug=False):
    nc = bass.Bass("TRN2", target_bir_lowering=False)
    dr = {}
    if debug:
        dr["dbg_ln1"] = nc.dram_tensor("dbg_ln1", [128, NDT * CH], BF16, kind="ExternalOutput").ap()
        dr["dbg_A"] = nc.dram_tensor("dbg_A", [128, 2 * NDT * 64], BF16, kind="ExternalOutput").ap()
        dr["dbg_B"] = nc.dram_tensor("dbg_B", [128, D], BF16, kind="ExternalOutput").ap()
        dr["dbg_ax"] = nc.dram_tensor("dbg_ax", [128, CH], BF16, kind="ExternalOutput").ap()
        dr["dbg_rstdB"] = nc.dram_tensor("dbg_rstdB", [128, CH], BF16, kind="ExternalOutput").ap()
        dr["dbg_mrB"] = nc.dram_tensor("dbg_mrB", [128, CH], BF16, kind="ExternalOutput").ap()
    dr["xT"] = nc.dram_tensor("xT", [D, 2 * CH], F32, kind="ExternalInput").ap()
    dr["kernel6"] = nc.dram_tensor("kernel6", [NET, 128, NDT, 128], F32, kind="ExternalInput").ap()
    dr["wk6"] = nc.dram_tensor("wk6", [128, NDT, 128], F32, kind="ExternalInput").ap()
    dr["wv6"] = nc.dram_tensor("wv6", [128, NDT, 128], F32, kind="ExternalInput").ap()
    dr["btf"] = nc.dram_tensor("btf", [128, NET], F32, kind="ExternalInput").ap()
    dr["btq"] = nc.dram_tensor("btq", [128, NET], F32, kind="ExternalInput").ap()
    dr["wq6"] = nc.dram_tensor("wq6", [NET, 128, NDT, 128], F32, kind="ExternalInput").ap()
    dr["wqb"] = nc.dram_tensor("wqb", [NET, 128, NDT, 128], BF16, kind="ExternalOutput").ap()
    dr["kbc"] = nc.dram_tensor("kbc", [128, 1], F32, kind="ExternalInput").ap()
    dr["vbr"] = nc.dram_tensor("vbr", [1, 128], F32, kind="ExternalInput").ap()
    dr["gate2"] = nc.dram_tensor("gate2", [1, 32], F32, kind="ExternalInput").ap()
    dr["ident4"] = nc.dram_tensor("ident4", [128, 32], F32, kind="ExternalInput").ap()
    dr["loraA7"] = nc.dram_tensor("loraA7", [16, 128, 1024], F32, kind="ExternalInput").ap()
    dr["loraAs7"] = nc.dram_tensor("loraAs7", [128, 1024], F32, kind="ExternalInput").ap()
    dr["loraB7"] = nc.dram_tensor("loraB7", [16, 32, D], F32, kind="ExternalInput").ap()
    dr["loraBs"] = nc.dram_tensor("loraBs", [32, D], F32, kind="ExternalInput").ap()
    dr["x1T"] = nc.dram_tensor("x1T", [D, 2 * CH], F32, kind="ExternalOutput").ap()
    dr["qT"] = nc.dram_tensor("qT", [128, 2, NET, CH], BF16, kind="ExternalOutput").ap()
    dr["kshard"] = nc.dram_tensor("kshard", [128, 2 * CH], BF16, kind="ExternalOutput").ap()
    dr["vshard"] = nc.dram_tensor("vshard", [2 * CH, 128], BF16, kind="ExternalOutput").ap()

    with contextlib.ExitStack() as es:
        sb = alloc_common(nc, es, "A")
        p = Prog(nc, es)
        t_const = emit_consts(p, sb, "A")
        t_gate = p.op("gpsimd", "dma_start", out=sb["gate2b"][:], in_=dr["gate2"],
                      inc="dmg", dma=True)
        t_i4 = p.op("gpsimd", "dma_start", out=sb["ident4"][:], in_=dr["ident4"],
                    inc="dmg", dma=True)
        t_gate = t_i4
        p.op("sync", "dma_start", out=sb["btf"][:], in_=dr["btf"], inc="dms", dma=True)
        p.op("sync", "dma_start", out=sb["btq"][:], in_=dr["btq"], inc="dms", dma=True)
        p.op("sync", "dma_start", out=sb["kbc"][:], in_=dr["kbc"], inc="dms", dma=True)
        p.op("sync", "dma_start", out=sb["vbr"][:], in_=dr["vbr"], inc="dms", dma=True)
        t_misc = p.cur("dms")
        t_btfh = p.op("vector", "tensor_scalar", sb["btfh"][:], sb["btf"][:], 0.5,
                      None, op0=ALU.mult, waits=[t_misc], inc="ve")
        t_vrow = p.op("vector", "tensor_copy", sb["rowbf"][0:1, 0, 0:128],
                      sb["vbr"][0:1, :], waits=[t_misc], inc="ve")
        mmv = p.op("tensor", "matmul", sb["ps_misc"][:, 100:228],
                   sb["onescol"][0:1, 0:1].broadcast_to([1, 128]),
                   sb["rowbf"][0:1, 0, 0:128],
                   start=True, stop=True, waits=[t_vrow, t_const], inc="pe")
        t_vbB = p.op("vector", "tensor_copy", sb["vbB"][:], sb["ps_misc"][:, 100:228],
                     waits=[mmv], inc="ve")
        t_lora = emit_lora_build(p, sb, dr, t_gate, t_const, t_vbB, [t_i4])

        stream = SlabStream(p, sb)
        xr = dr["xT"].rearrange("(dt p) t -> p dt t", p=128)
        x1r = dr["x1T"].rearrange("(dt p) t -> p dt t", p=128)
        t_store_prev = None
        t_prev_kv_pe = None
        for ci in range(2):
            ch = ci
            w = [] if ci == 0 else [t_store_prev]
            t_x = p.op("sync", "dma_start", out=sb["xs"][:],
                       in_=xr[:, :, CH * ch:CH * ch + CH], waits=w,
                       inc=f"dx{ch}", dma=True)
            lnw = [t_x]
            if t_prev_kv_pe is not None:
                lnw += [t_prev_kv_pe, p.cur("sc"), p.cur("ve")]
            t_ln1 = emit_ln(p, sb, sb["xs"][:], sb["lnxt"], lnw)
            if debug and ci == 0:
                p.op("sync", "dma_start", out=dr["dbg_rstdB"], in_=sb["rstdB"][:],
                     waits=[t_ln1], inc="dst", dma=True)
                p.op("sync", "dma_start", out=dr["dbg_mrB"], in_=sb["mrB"][:],
                     waits=[t_ln1], inc="dst", dma=True)
                p.op("sync", "dma_start", out=dr["dbg_ln1"],
                     in_=sb["lnxt"][:].rearrange("p a b -> p (a b)"),
                     waits=[t_ln1], inc="dst", dma=True)
                p.op("sync", "dma_start", out=dr["dbg_A"],
                     in_=sb["A_combT"][:].rearrange("p a b -> p (a b)"),
                     waits=t_lora, inc="dst", dma=True)
                p.op("sync", "dma_start", out=dr["dbg_B"], in_=sb["B_combT"][:],
                     waits=t_lora, inc="dst", dma=True)
            wr = make_residual_writer(p, sb, sb["xs"], sb["btfh"], HALF)
            lora = (sb["A_combT"][:, ci], sb["B_combT"][64 * ci:64 * ci + 64, :], ci)
            if debug and ci == 0:
                sb["_dbg_ax_out"] = dr["dbg_ax"]
            fw = [t_ln1, t_btfh] + (t_lora if ci == 0 else [])
            lwf = [] if ci == 0 else [p.cur("wb0"), p.cur("wb1")]
            emit_proj(p, sb, stream, dr["kernel6"], sb["lnxt"], wr, lora=lora,
                      first_waits=fw, lw=lwf)
            t_ffn_pe = p.cur("pe")
            t_ffn_ve = p.cur("ve")
            t_store_prev = p.op("sync", "dma_start",
                                out=x1r[:, :, CH * ch:CH * ch + CH],
                                in_=sb["xs"][:], waits=[t_ffn_ve], inc="dst", dma=True)
            t_ln2 = emit_ln(p, sb, sb["xs"][:], sb["lnxt"], [t_ffn_pe])
            qstage = sb["tmp"][:].bitcast(BF16)
            st_toks = [None, None]

            def qwriter(et, ps, tok_pe, ch=ch, qstage=qstage, st_toks=st_toks):
                w = [tok_pe]
                if st_toks[et % 2] is not None:
                    w.append(st_toks[et % 2])
                t_sc = p.op("scalar", "activation",
                            out=qstage[:, 512 * (et % 2):512 * (et % 2) + 512],
                            in_=ps, func=AF.Identity, bias=sb["btq"][:, et:et + 1],
                            scale=ONE, waits=w, inc="sc")
                st_toks[et % 2] = p.op("sync", "dma_start",
                                       out=dr["qT"][:, ch, et, :],
                                       in_=qstage[:, 512 * (et % 2):512 * (et % 2) + 512],
                                       waits=[t_sc], inc=f"dq{et % 2}", dma=True)
                return st_toks[et % 2]
            if ci == 0:
                emit_proj(p, sb, stream, dr["wq6"], sb["lnxt"], qwriter,
                          first_waits=[t_ln2], wb=dr["wqb"])
            else:
                emit_proj(p, sb, stream, dr["wqb"], sb["lnxt"], qwriter,
                          first_waits=[t_ln2], eng="sync",
                          lw=[p.cur("wb0"), p.cur("wb1")])
            wbt = stream.take_wb()
            p.op("gpsimd", "dma_start",
                 out=sb["slab"][:, 0].rearrange("p dt c -> p (dt c)"),
                 in_=dr["wk6"].rearrange("p dt c -> p (dt c)"),
                 waits=[stream.done[-1]] + wbt, inc="dkv", dma=True)
            p.op("gpsimd", "dma_start",
                 out=sb["slab"][:, 1].rearrange("p dt c -> p (dt c)"),
                 in_=dr["wv6"].rearrange("p dt c -> p (dt c)"),
                 waits=[stream.done[-1]] + wbt, inc="dkv", dma=True)
            t_wkv = p.cur("dkv")
            emit_kv(p, sb, sb["lnxt"], ch, [t_ln2, t_wkv, t_vbB, t_misc, p.cur("dq0"), p.cur("dq1")])
            t_prev_kv_pe = p.cur("pe")
            stream.gate_next(t_prev_kv_pe, 2)
        t_sk = p.op("sync", "dma_start", out=dr["kshard"],
                    in_=sb["ksh"][:].rearrange("p ch t -> p (ch t)"),
                    waits=[p.cur("sc")], inc="dst", dma=True)
        t_sv = p.op("sync", "dma_start",
                    out=dr["vshard"].rearrange("(ch qt p) e -> p ch qt e", p=128, qt=4),
                    in_=sb["vsh"][:], waits=[p.cur("ve")], inc="dst", dma=True)
        p.op("sync", "nop", waits=[p.cur("dst")])
        p.run()
    return nc


def build_B():
    nc = bass.Bass("TRN2", target_bir_lowering=False)
    dr = {}
    dr["x1T"] = nc.dram_tensor("x1T", [D, 2 * CH], F32, kind="ExternalInput").ap()
    dr["qT"] = nc.dram_tensor("qT", [128, 2, NET, CH], BF16, kind="ExternalInput").ap()
    dr["wo6"] = nc.dram_tensor("wo6", [NET, 128, NDT, 128], F32, kind="ExternalInput").ap()
    dr["kernel6"] = nc.dram_tensor("kernel6", [NET, 128, NDT, 128], F32, kind="ExternalInput").ap()
    dr["btf"] = nc.dram_tensor("btf", [128, NET], F32, kind="ExternalInput").ap()
    dr["bto"] = nc.dram_tensor("bto", [128, NET], F32, kind="ExternalInput").ap()
    dr["gate2"] = nc.dram_tensor("gate2", [1, 32], F32, kind="ExternalInput").ap()
    dr["ident4"] = nc.dram_tensor("ident4", [128, 32], F32, kind="ExternalInput").ap()
    dr["loraA7"] = nc.dram_tensor("loraA7", [16, 128, 1024], F32, kind="ExternalInput").ap()
    dr["loraAs7"] = nc.dram_tensor("loraAs7", [128, 1024], F32, kind="ExternalInput").ap()
    dr["loraB7"] = nc.dram_tensor("loraB7", [16, 32, D], F32, kind="ExternalInput").ap()
    dr["loraBs"] = nc.dram_tensor("loraBs", [32, D], F32, kind="ExternalInput").ap()
    dr["meta"] = nc.dram_tensor("meta", [1, 8], mybir.dt.int32, kind="ExternalInput").ap()
    dr["kA"] = nc.dram_tensor("kA", [128, 32 * 128], BF16, kind="ExternalInput").ap()
    dr["vA"] = nc.dram_tensor("vA", [32 * 128, 129], BF16, kind="ExternalInput").ap()
    dr["kB"] = nc.dram_tensor("kB", [128, 64 * 128], BF16, kind="ExternalInput").ap()
    dr["vB"] = nc.dram_tensor("vB", [64 * 128, 129], BF16, kind="ExternalInput").ap()
    dr["outT"] = nc.dram_tensor("outT", [D, 2 * CH], F32, kind="ExternalOutput").ap()

    with contextlib.ExitStack() as es:
        sb = alloc_common(nc, es, "B")
        sb["meta"] = es.enter_context(nc.sbuf_tensor("sb_meta", [1, 8], mybir.dt.int32))
        p = Prog(nc, es)
        t_const = emit_consts(p, sb, "B")
        t_gate = p.op("gpsimd", "dma_start", out=sb["gate2b"][:], in_=dr["gate2"],
                      inc="dmg", dma=True)
        t_meta = p.op("sync", "dma_start", out=sb["meta"][:], in_=dr["meta"],
                      inc="dms", dma=True)
        t_i4 = p.op("gpsimd", "dma_start", out=sb["ident4"][:], in_=dr["ident4"],
                    inc="dmg", dma=True)
        t_gate = t_i4
        for e in ("tensor", "scalar", "vector"):
            p.wait(e, t_meta)
            p.load_reg(e, "_attn_reg", sb["meta"][0:1, 0:1])
        p.op("sync", "dma_start", out=sb["btf"][:], in_=dr["btf"], inc="dms", dma=True)
        p.op("sync", "dma_start", out=sb["bto"][:], in_=dr["bto"], inc="dms", dma=True)
        t_misc = p.cur("dms")
        t_btfh = p.op("vector", "tensor_scalar", sb["btfh"][:], sb["btf"][:], 0.5,
                      None, op0=ALU.mult, waits=[t_misc], inc="ve")
        t_lora = emit_lora_build(p, sb, dr, t_gate, t_const, None, [t_i4])

        stream = SlabStream(p, sb)
        x1r = dr["x1T"].rearrange("(dt p) t -> p dt t", p=128)
        outr = dr["outT"].rearrange("(dt p) t -> p dt t", p=128)
        kv_dram = [(dr["kA"], dr["vA"], 32), (dr["kB"], dr["vB"], 64)]
        t_store_prev = None
        t_attn_prev = None
        for ci in range(2):
            ch = ci
            w = [t_misc] if ci == 0 else [t_store_prev]
            t_x = p.op("sync", "dma_start", out=sb["xs"][:],
                       in_=x1r[:, :, CH * ch:CH * ch + CH], waits=w,
                       inc=f"dx{ch}", dma=True)
            w = [t_lora[2]] if ci == 0 else [t_attn_prev]
            t_q_sc = p.op("scalar", "dma_start",
                          out=sb["QT"][:].rearrange("p h t -> p (h t)"),
                          in_=dr["qT"][:, ci].rearrange("p e t -> p (e t)"),
                          waits=w + [t_x], inc=f"dq{ci}", dma=True)
            kdr, vdr, npos = kv_dram[ci]
            KT = sb["big"][:, 0:npos * 128]
            V1 = sb["big"][:, 8192:8192 + npos * 129].rearrange("p (k c) -> p k c", c=129)
            wkv = list(t_lora) + [p.cur("ve")] if ci == 0 else [t_attn_prev]
            p.op("scalar", "dma_start", out=KT, in_=kdr, waits=wkv,
                 inc=f"dkt{ci}", dma=True)
            p.op("scalar", "dma_start", out=V1,
                 in_=vdr.rearrange("(k p) c -> p k c", p=128),
                 inc=f"dkt{ci}", dma=True)
            t_kv = p.cur(f"dkt{ci}")
            engines = ("tensor", "scalar", "vector")
            t_entry = [t_q_sc, t_kv, p.cur("sc"), p.cur("ve")]
            snap = p.snapshot()
            deltas = []
            attn_sems = ("pe", "sc", "ve")
            for core in range(NCORES):
                p.restore_counts(snap)
                K = 4 * ((core if ci == 0 else 15 - core) + 1)
                p.branch_begin(engines, "_attn_reg", core)
                emit_attention(p, sb, K, KT, V1, sb["QT"], sb["lnxt"], t_entry)
                deltas.append({n: p._sems[n][1] - snap[0].get(n, 0)
                               for n in attn_sems})
                p.branch_end(engines)
            mx = {n: max(d[n] for d in deltas) for n in attn_sems}
            for core in range(NCORES):
                need = {n: mx[n] - deltas[core][n] for n in attn_sems}
                if any(need.values()):
                    p.branch_begin(engines, "_attn_reg", core)
                    p.sem_pad("tensor", "pe", need["pe"])
                    p.sem_pad("scalar", "sc", need["sc"])
                    p.sem_pad("vector", "ve", need["ve"])
                    p.branch_end(engines)
            p.restore_counts(snap)
            for n in attn_sems:
                p._sems[n][1] = snap[0].get(n, 0) + mx[n]
            t_attn_prev = (p._sems["ve"], p._sems["ve"][1])
            ow = make_residual_writer(p, sb, sb["xs"], sb["bto"], ONE)
            emit_proj(p, sb, stream, dr["wo6"], sb["lnxt"], ow,
                      first_waits=[t_attn_prev])
            t_o_pe = p.cur("pe")
            t_ln3 = emit_ln(p, sb, sb["xs"][:], sb["lnxt"], [t_o_pe, p.cur("sc")])
            fw2 = make_residual_writer(p, sb, sb["xs"], sb["btfh"], HALF,
                                       store_to=outr[:, :, CH * ch:CH * ch + CH])
            lora = (sb["A_combT"][:, ci], sb["B_combT"][64 * ci:64 * ci + 64, :], ci)
            emit_proj(p, sb, stream, dr["kernel6"], sb["lnxt"], fw2, lora=lora,
                      first_waits=[t_ln3, t_btfh])
            t_store_prev = p.cur("dst")
        p.op("sync", "nop", waits=[p.cur("dst")])
        p.run()
    return nc



def kernel(L0_attn_x, L0_attn_wq_w, L0_attn_wq_b, L0_attn_wk_w, L0_attn_wk_b,
           L0_attn_wv_w, L0_attn_wv_b, L0_attn_wo_w, L0_attn_wo_b, L0_ffn_gate,
           L0_ffn_kernel, L0_ffn_bias, L0_ffn_lora_A, L0_ffn_lora_B,
           L0_ffn_lora_A_shared, L0_ffn_lora_B_shared):
    f32 = np.float32
    x = np.asarray(L0_attn_x, f32)[0]
    xT = np.ascontiguousarray(x.T)
    kernel6 = _slab6(np.asarray(L0_ffn_kernel, f32), NET)
    wq6 = _slab6(np.asarray(L0_attn_wq_w, f32).T, NET)
    wo6 = _slab6(np.asarray(L0_attn_wo_w, f32).T, NET)
    wk6 = _slab6(np.asarray(L0_attn_wk_w, f32).T, 1)[0]
    wv6 = _slab6(np.asarray(L0_attn_wv_w, f32).T, 1)[0]
    btf = _bias_tile(L0_ffn_bias)
    btq = _bias_tile(L0_attn_wq_b)
    bto = _bias_tile(L0_attn_wo_b)
    kbc = np.ascontiguousarray(np.asarray(L0_attn_wk_b, f32).reshape(128, 1))
    vbr = np.ascontiguousarray(np.asarray(L0_attn_wv_b, f32).reshape(1, 128))
    gate = np.asarray(L0_ffn_gate, f32)
    lora_A = np.asarray(L0_ffn_lora_A, f32)
    lora_B = np.asarray(L0_ffn_lora_B, f32)
    lA_s = np.asarray(L0_ffn_lora_A_shared, f32)
    lB_s = np.asarray(L0_ffn_lora_B_shared, f32)
    loraA7 = np.ascontiguousarray(
        lora_A.transpose(0, 2, 1).reshape(16, 32, 128, 32).transpose(0, 2, 1, 3)
    ).reshape(16, 128, 1024)
    loraAs7 = np.ascontiguousarray(
        lA_s.T.reshape(32, 128, 32).transpose(1, 0, 2)).reshape(128, 1024)
    loraB7 = np.ascontiguousarray(lora_B.transpose(0, 2, 1))
    loraBs = np.ascontiguousarray(lB_s.T)

    chunks = [(i, 15 - i) for i in range(NCORES)]
    ident4 = np.ascontiguousarray(np.tile(np.eye(32, dtype=f32), (4, 1)))

    ncA = build_A()
    in_maps = []
    for cA, cB in chunks:
        xTc = np.ascontiguousarray(
            np.concatenate([xT[:, CH * cA:CH * cA + CH],
                            xT[:, CH * cB:CH * cB + CH]], axis=1))
        gate2 = np.ascontiguousarray(
            np.stack([gate[cA // 2], gate[cB // 2]]).reshape(1, 32))
        in_maps.append(dict(xT=xTc, kernel6=kernel6, wk6=wk6, wv6=wv6, btf=btf,
                            btq=btq, wq6=wq6, kbc=kbc, vbr=vbr, gate2=gate2,
                            ident4=ident4, loraA7=loraA7, loraAs7=loraAs7,
                            loraB7=loraB7, loraBs=loraBs))
    resA = run_bass_kernel_spmd(ncA, in_maps, core_ids=list(range(NCORES)))

    kfull = np.zeros((128, 16, CH), dtype=BF)
    vfull = np.zeros((SEQ, 128), dtype=BF)
    x1s = []
    for core, (cA, cB) in enumerate(chunks):
        r = resA.results[core]
        ks = np.asarray(r["kshard"]).reshape(128, 2, CH)
        vs = np.asarray(r["vshard"])
        kfull[:, cA, :] = ks[:, 0]
        kfull[:, cB, :] = ks[:, 1]
        vfull[CH * cA:CH * cA + CH] = vs[0:CH]
        vfull[CH * cB:CH * cB + CH] = vs[CH:2 * CH]
        x1s.append(np.asarray(r["x1T"]))
    kflat = np.ascontiguousarray(kfull.reshape(128, SEQ))

    def pack_slot(c, npos):
        K = 4 * (c + 1)
        kt = np.zeros((128, npos * 128), dtype=BF)
        v1 = np.zeros((npos * 128, 129), dtype=BF)
        for pp in range(K):
            g = K - 1 - pp
            kt[:, 128 * pp:128 * pp + 128] = kflat[:, 128 * g:128 * g + 128]
            v1[128 * pp:128 * pp + 128, 0:128] = vfull[128 * g:128 * g + 128]
            v1[128 * pp:128 * pp + 128, 128] = np.asarray(1.0, dtype=BF)
        return np.ascontiguousarray(kt), np.ascontiguousarray(v1)

    ncB = build_B()
    in_mapsB = []
    for core, (cA, cB) in enumerate(chunks):
        kAp, vAp = pack_slot(cA, 32)
        kBp, vBp = pack_slot(cB, 64)
        gate2 = np.ascontiguousarray(
            np.stack([gate[cA // 2], gate[cB // 2]]).reshape(1, 32))
        in_mapsB.append(dict(x1T=x1s[core], qT=np.asarray(resA.results[core]["qT"]),
                             wo6=wo6, kernel6=kernel6,
                             meta=np.full((1, 8), core, dtype=np.int32),
                             btf=btf, bto=bto, gate2=gate2, ident4=ident4,
                             loraA7=loraA7, loraAs7=loraAs7, loraB7=loraB7,
                             loraBs=loraBs, kA=kAp, vA=vAp, kB=kBp, vB=vBp))
    resB = run_bass_kernel_spmd(ncB, in_mapsB, core_ids=list(range(NCORES)))

    out = np.empty((SEQ, D), dtype=np.float32)
    for core, (cA, cB) in enumerate(chunks):
        oT = np.asarray(resB.results[core]["outT"])
        out[CH * cA:CH * cA + CH] = oT[:, 0:CH].T
        out[CH * cB:CH * cB + CH] = oT[:, CH:2 * CH].T
    return out.reshape(1, SEQ, D)
